# version 22
# baseline (speedup 1.0000x reference)
import math
import numpy as np
import ml_dtypes
import concourse.bass as bass
import concourse.mybir as mybir
from concourse.bass_utils import run_bass_kernel_spmd

F32 = mybir.dt.float32
BF16 = mybir.dt.bfloat16
I32 = mybir.dt.int32
AF = mybir.ActivationFunctionType
ALU = mybir.AluOpType
AX = mybir.AxisListType

B, S, D, H, HD, INTER = 2, 4096, 2048, 16, 128, 8192
NCORE, ROWS, SPAN, WIN = 8, 1024, 3072, 2048
NEG = -1.0e5
LN_EPS = 1e-5
DC, IC, QKVC, NSP = D // 128, INTER // 128, 3 * D // 128, SPAN // 128
INV_SQRT_HD = 1.0 / math.sqrt(128.0)
F2048 = 1.0 / (64.0 * math.sqrt(2048.0))
F8192 = 1.0 / (64.0 * math.sqrt(8192.0))
BF = ml_dtypes.bfloat16
WP_SCALAR_TR = True


def _hadamard(n):
    Hm = np.array([[1.0]], dtype=np.float32)
    while Hm.shape[0] < n:
        Hm = np.block([[Hm, Hm], [Hm, -Hm]])
    return Hm


class Ctr:
    def __init__(self, sem):
        self.sem = sem
        self.v = 0

    def inc(self, ins, n=1):
        ins.then_inc(self.sem, n)
        self.v += n
        return self.v



def build_A():
    nc = bass.Bass("TRN2", target_bir_lowering=False)
    x0span = nc.dram_tensor("x0span", [SPAN, D], F32, kind="ExternalInput").ap()
    kvalid = nc.dram_tensor("kvalid", [128, NSP], F32, kind="ExternalInput").ap()
    Wq = nc.dram_tensor("Wq", [D, D], F32, kind="ExternalInput").ap()
    Wk = nc.dram_tensor("Wk", [D, D], F32, kind="ExternalInput").ap()
    Wv = nc.dram_tensor("Wv", [D, D], F32, kind="ExternalInput").ap()
    Wo = nc.dram_tensor("Wo", [D, D], F32, kind="ExternalInput").ap()
    SUug = nc.dram_tensor("SUug", [128, DC], F32, kind="ExternalInput").ap()
    SUdp = nc.dram_tensor("SUdp", [128, IC], F32, kind="ExternalInput").ap()
    Wsug = nc.dram_tensor("Wsug", [128, 2 * IC], F32, kind="ExternalInput").ap()
    Wsdp = nc.dram_tensor("Wsdp", [128, DC], F32, kind="ExternalInput").ap()
    Wup = nc.dram_tensor("Wup", [INTER, D], F32, kind="ExternalInput").ap()
    Wgate = nc.dram_tensor("Wgate", [INTER, D], F32, kind="ExternalInput").ap()
    Wdown = nc.dram_tensor("Wdown", [D, INTER], F32, kind="ExternalInput").ap()
    Wqkv = nc.dram_tensor("Wqkv", [D, 3 * D], F32, kind="ExternalInput").ap()
    gslg = nc.dram_tensor("gslg", [INTER // 8, D], F32, kind="ExternalInput").ap()
    gslv = nc.dram_tensor("gslv", [INTER // 8, D], F32, kind="ExternalInput").ap()
    gslo = nc.dram_tensor("gslo", [D // 8, INTER], F32, kind="ExternalInput").ap()

    xB_out = nc.dram_tensor("xB_out", [DC, 128, ROWS], F32, kind="ExternalOutput").ap()
    qkv_out = nc.dram_tensor("qkv_out", [QKVC, 128, ROWS], F32, kind="ExternalOutput").ap()
    partials = nc.dram_tensor("partials", [1, 8], F32, kind="ExternalOutput").ap()

    xA_dram = nc.dram_tensor("xA_dram", [DC, 128, ROWS], F32).ap()
    wupbf_dram = nc.dram_tensor("wupbf_dram", [INTER, D], BF16).ap()
    wgbf_dram = nc.dram_tensor("wgbf_dram", [INTER, D], BF16).ap()
    wdbf_dram = nc.dram_tensor("wdbf_dram", [D, INTER], BF16).ap()
    x0ob_dram = nc.dram_tensor("x0ob_dram", [DC, 128, ROWS], BF16).ap()
    xBb_dram = nc.dram_tensor("xBb_dram", [DC, 128, ROWS], BF16).ap()
    xsu_dram = nc.dram_tensor("xsu_dram", [DC, 128, ROWS], BF16).ap()

    c_iden = nc.inline_tensor(np.eye(128, dtype=np.float32), name="c_iden").ap()
    c_idenb = nc.inline_tensor(np.eye(128, dtype=np.float32).astype(BF), name="c_idenb").ap()
    c_h128 = nc.inline_tensor(_hadamard(128).astype(BF), name="c_h128").ap()
    c_ones = nc.inline_tensor(np.ones((128, 1), dtype=BF), name="c_ones").ap()
    c_ones1r = nc.inline_tensor(np.ones((1, 128), dtype=BF), name="c_ones1r").ap()
    c_ones1rw = nc.inline_tensor(np.ones((1, 512), dtype=BF), name="c_ones1rw").ap()
    p_ = np.arange(128)[:, None]
    q_ = np.arange(SPAN)[None, :]
    t_ = p_ - q_ + 2560
    c_T2 = nc.inline_tensor(np.where((t_ >= 1) & (t_ <= WIN), 0.0, NEG).astype(BF), name="c_T2").ap()

    import contextlib
    _stack = contextlib.ExitStack()
    with _stack:
        _sb = lambda *a: _stack.enter_context(nc.sbuf_tensor(*a))
        _ps = lambda *a: _stack.enter_context(nc.psum_tensor(*a))
        _sem = lambda n: _stack.enter_context(nc.semaphore(n))
        iden_sb = _sb("iden_sb", [128, 128], F32)
        idenb_sb = _sb("idenb_sb", [128, 128], BF16)
        h128_sb = _sb("h128_sb", [128, 128], BF16)
        ones_sb = _sb("ones_sb", [128, 1], BF16)
        ones1r_sb = _sb("ones1r_sb", [1, 128], BF16)
        ones1rw_sb = _sb("ones1rw_sb", [1, 512], BF16)
        su_ug_sb = _sb("su_ug_sb", [128, DC], F32)
        su_dp_sb = _sb("su_dp_sb", [128, IC], F32)
        wsug64_sb = _sb("wsug64_sb", [128, 2 * IC], F32)
        wsdp64_sb = _sb("wsdp64_sb", [128, DC], F32)
        eps_sb = _sb("eps_sb", [1, 1], F32)
        stat_sb = _sb("stat_sb", [1, 5 * 512], F32)
        statb_sb = _sb("statb_sb", [1, 2 * 512], BF16)
        lnA_sb = _sb("lnA_sb", [128, 512], BF16)
        lnB_sb = _sb("lnB_sb", [128, 512], BF16)
        amax_sb = _sb("amax_sb", [128, 4], F32)
        gsum_sb = _sb("gsum_sb", [128, 4], F32)
        red_sb = _sb("red_sb", [128, 1], F32)
        part_sb = _sb("part_sb", [1, 8], F32)
        ptr_sb = _sb("ptr_sb", [1, 128], F32)
        evtmp = _sb("evtmp", [128, 512], F32)
        evtmp2 = _sb("evtmp2", [128, 512], F32)
        xres_sb = _sb("xres_sb", [128, 512], BF16)
        sqb_sb = _sb("sqb_sb", [128, 512], BF16)
        xpb_sb = _sb("xpb_sb", [128, 512], BF16)
        ps_abcd = _ps("ps_abcd", [128, 2048], F32)
        ps_a = ps_abcd[:, 0:512]
        ps_b = ps_abcd[:, 512:1024]
        ps_c = ps_abcd[:, 1024:1536]
        ps_d = ps_abcd[:, 1536:2048]
        ps_e = _ps("ps_e", [128, 512], F32)
        ps_tb = _ps("ps_tb", [128, 128], BF16)
        ps_sum = _ps("ps_sum", [1, 512], F32)
        ps_st = _ps("ps_st", [1, 512], F32)
        _ld = _sem("m_ld")
        _gd = _sem("m_gd")
        _pe = _sem("m_pe")
        _act = _sem("m_act")
        _dve = _sem("m_dve")
        _vd = _sem("m_vd")
        _sl = [_sem(f"m_sl{i}") for i in range(10)]
        block = _stack.enter_context(nc.Block())
        ld, gd, pe, act, dve = Ctr(_ld), Ctr(_gd), Ctr(_pe), Ctr(_act), Ctr(_dve)
        vd = Ctr(_vd)
        slc = [Ctr(s) for s in _sl]

        def emit_ln_stats_and_bcast(get_chunk):
            pe0, act0, dve0 = pe.v, act.v, dve.v
            for c in range(DC):
                @block.vector
                def _(v, c=c):
                    if c > 0:
                        v.wait_ge(_pe, pe0 + 2 * c)
                    dve.inc(v.tensor_copy(xpb_sb[:], get_chunk(c)))

                @block.scalar
                def _(s, c=c):
                    if c == 0:
                        s.wait_ge(_dve, dve0)
                    else:
                        s.wait_ge(_pe, pe0 + 2 * c)
                    act.inc(s.activation(sqb_sb[:], get_chunk(c), AF.Square))

                @block.tensor
                def _(t, c=c):
                    t.wait_ge(_dve, dve0 + c + 1)
                    t.wait_ge(_act, act0 + c + 1)
                    pe.inc(t.matmul(ps_st[:], ones_sb[:], xpb_sb[:], start=(c == 0), stop=(c == DC - 1)))
                    pe.inc(t.matmul(ps_sum[:], ones_sb[:], sqb_sb[:], start=(c == 0), stop=(c == DC - 1)))

            HBl = 512

            @block.vector
            def _(v):
                v.wait_ge(_pe, pe.v)
                dve.inc(v.tensor_copy(stat_sb[0:1, 0:HBl], ps_st[:]))
                dve.inc(v.tensor_copy(stat_sb[0:1, HBl:2 * HBl], ps_sum[:]))
                v.wait_ge(_dve, dve.v)
                c1 = 1.0 / float(D)
                dve.inc(v.tensor_scalar(stat_sb[0:1, 4 * HBl:5 * HBl], stat_sb[0:1, 0:HBl], c1, None, ALU.mult))
                v.wait_ge(_dve, dve.v)
                dve.inc(v.tensor_tensor(stat_sb[0:1, 2 * HBl:3 * HBl], stat_sb[0:1, 4 * HBl:5 * HBl], stat_sb[0:1, 4 * HBl:5 * HBl], ALU.mult))
                v.wait_ge(_dve, dve.v)
                dve.inc(v.scalar_tensor_tensor(stat_sb[0:1, 2 * HBl:3 * HBl], stat_sb[0:1, HBl:2 * HBl], c1, stat_sb[0:1, 2 * HBl:3 * HBl], ALU.mult, ALU.subtract))

            @block.scalar
            def _(s):
                s.wait_ge(_dve, dve.v)
                act.inc(s.activation(stat_sb[0:1, 2 * HBl:3 * HBl], stat_sb[0:1, 2 * HBl:3 * HBl], AF.Sqrt, bias=eps_sb[0:1, 0:1]))

            @block.vector
            def _(v):
                v.wait_ge(_act, act.v)
                dve.inc(v.reciprocal(stat_sb[0:1, 2 * HBl:3 * HBl], stat_sb[0:1, 2 * HBl:3 * HBl]))
                v.wait_ge(_dve, dve.v)
                dve.inc(v.tensor_tensor(stat_sb[0:1, 3 * HBl:4 * HBl], stat_sb[0:1, 4 * HBl:5 * HBl], stat_sb[0:1, 2 * HBl:3 * HBl], ALU.mult))
                v.wait_ge(_dve, dve.v)
                v.tensor_copy(statb_sb[0:1, 0:HBl], stat_sb[0:1, 2 * HBl:3 * HBl])
                dve.inc(v.tensor_copy(statb_sb[0:1, HBl:2 * HBl], stat_sb[0:1, 3 * HBl:4 * HBl]))

            @block.tensor
            def _(t):
                t.wait_ge(_dve, dve.v)
                pe.inc(t.matmul(ps_a[:], ones1r_sb[:], statb_sb[0:1, 0:HBl], start=True, stop=True))
                pe.inc(t.matmul(ps_b[:], ones1r_sb[:], statb_sb[0:1, HBl:2 * HBl], start=True, stop=True))

            @block.vector
            def _(v):
                v.wait_ge(_pe, pe.v)
                dve.inc(v.tensor_copy(lnA_sb[:], ps_a[:]))
                dve.inc(v.tensor_copy(lnB_sb[:], ps_b[:]))

        @block.sync
        def _(sync):
            for dst, src in [(iden_sb[:], c_iden), (idenb_sb[:], c_idenb),
                             (h128_sb[:], c_h128), (ones_sb[:], c_ones),
                             (ones1r_sb[:], c_ones1r), (ones1rw_sb[:], c_ones1rw),
                             (su_ug_sb[:], SUug),
                             (su_dp_sb[:], SUdp), (wsug64_sb[:], Wsug),
                             (wsdp64_sb[:], Wsdp)]:
                ld.inc(sync.dma_start(out=dst, in_=src), 16)

        @block.vector
        def _(v):
            v.wait_ge(_ld, ld.v)
            v.memset(eps_sb[:], LN_EPS)
            v.memset(amax_sb[:], 0.0)
            v.memset(gsum_sb[:], 0.0)
            v.tensor_scalar(wsug64_sb[:], wsug64_sb[:], 64.0, None, ALU.mult)
            dve.inc(v.tensor_scalar(wsdp64_sb[:], wsdp64_sb[:], 64.0, None, ALU.mult))

        with nc.sbuf_tensor("ctx_all", [128, H * ROWS], BF16) as ctx_all:
            with nc.sbuf_tensor("x0fm", [128, DC * SPAN], BF16) as x0fm:
                x0_t = x0span.rearrange("(n p) d -> n p d", p=128)
                pe1, act1 = pe.v, act.v
                with nc.sbuf_tensor("xin", [128, 4 * D], F32) as xin:
                    for i in range(NSP):
                        slot = i % 4

                        @block.sync
                        def _(sync, i=i, slot=slot):
                            if i >= 4:
                                sync.wait_ge(_pe, pe1 + (i - 3) * DC)
                            ld.inc(sync.dma_start(out=xin[:, slot * D:(slot + 1) * D], in_=x0_t[i]), 16)

                        @block.tensor
                        def _(tensor, i=i, slot=slot):
                            tensor.wait_ge(_ld, ld.v)
                            for c in range(DC):
                                m = i * DC + c
                                if m >= 2:
                                    tensor.wait_ge(_act, act1 + m - 1)
                                dst = ps_a if m % 2 == 0 else ps_b
                                pe.inc(tensor.transpose(dst[:, 0:128], xin[:, slot * D + c * 128:slot * D + (c + 1) * 128], iden_sb[:]))

                        @block.scalar
                        def _(scalar, i=i):
                            for c in range(DC):
                                m = i * DC + c
                                scalar.wait_ge(_pe, pe1 + m + 1)
                                src = ps_a if m % 2 == 0 else ps_b
                                act.inc(scalar.copy(x0fm[:, c * SPAN + i * 128:c * SPAN + (i + 1) * 128], src[:, 0:128]))
                ph1_act = act.v

                @block.gpsimd
                def _(g):
                    g.wait_ge(_act, ph1_act)
                    for c in range(DC):
                        gd.inc(g.dma_start(out=x0ob_dram[c], in_=x0fm[:, c * SPAN + WIN:c * SPAN + SPAN]), 16)
                spill_gd = gd.v

                with (
                    nc.sbuf_tensor("wk_sb", [128, 2 * DC * 128], BF16) as wk_sb,
                    nc.sbuf_tensor("wv_sb", [128, 2 * DC * 128], BF16) as wv_sb,
                    nc.sbuf_tensor("wq_sb", [128, DC * 128], BF16) as wq_sb,
                    nc.sbuf_tensor("kh_sb", [128, SPAN], BF16) as kh_sb,
                    nc.sbuf_tensor("vh_sb", [128, SPAN], BF16) as vh_sb,
                    nc.sbuf_tensor("vt_sb", [128, SPAN], BF16) as vt_sb,
                    nc.sbuf_tensor("qh_sb", [128, ROWS], BF16) as qh_sb,
                    nc.sbuf_tensor("ptil", [128, 2048], BF16) as ptil,
                    nc.sbuf_tensor("rec_sb", [1, 512], F32) as rec_sb,
                    nc.sbuf_tensor("recb_sb", [1, 512], BF16) as recb_sb,
                    nc.sbuf_tensor("bcast_sb", [128, 512], BF16) as bcast_sb,
                    nc.sbuf_tensor("t2_sb", [128, SPAN], BF16) as t2_sb,
                    nc.sbuf_tensor("kvalid_sb", [128, NSP], F32) as kvalid_sb,
                ):
                    @block.sync
                    def _(sync):
                        ld.inc(sync.dma_start(out=t2_sb[:], in_=c_T2), 16)
                        ld.inc(sync.dma_start(out=kvalid_sb[:], in_=kvalid), 16)
                    Wk_p = Wk.rearrange("(c p) f -> p c f", p=128)
                    Wv_p = Wv.rearrange("(c p) f -> p c f", p=128)
                    Wq_p = Wq.rearrange("(c p) f -> p c f", p=128)
                    head_pe_end = []
                    kv_gd_end = []
                    wq_gd_end = []
                    qproj_pe = []

                    def emit_kv_load(h):
                        sc = slc[h % 2]

                        @block.gpsimd
                        def _(g, h=h, sc=sc):
                            if h <= 1:
                                g.wait_ge(_act, ph1_act)
                            if h >= 2:
                                g.wait_ge(_pe, head_pe_end[h - 2])
                            sl = (h % 2) * DC * 128
                            for wsb, wsrc in [(wk_sb, Wk_p), (wv_sb, Wv_p)]:
                                for c in range(DC):
                                    sc.inc(g.dma_start(out=wsb[:, sl + c * 128:sl + (c + 1) * 128],
                                                       in_=wsrc[:, c, h * 128:(h + 1) * 128]), 16)
                        kv_gd_end.append(sc.v)

                    def emit_wq_load(h):
                        @block.gpsimd
                        def _(g, h=h):
                            if h >= 1:
                                g.wait_ge(_pe, qproj_pe[h - 1])
                            for c in range(DC):
                                slc[2].inc(g.dma_start(out=wq_sb[:, c * 128:(c + 1) * 128],
                                                       in_=Wq_p[:, c, h * 128:(h + 1) * 128]), 16)
                        wq_gd_end.append(slc[2].v)

                    emit_kv_load(0)
                    emit_wq_load(0)
                    for h in range(H):
                        if h + 1 < H:
                            emit_kv_load(h + 1)

                        hsl = (h % 2) * DC * 128
                        for wsb, dst, nblk, base in [(wk_sb, kh_sb, 6, 0), (wv_sb, vh_sb, 6, 0), (wq_sb, qh_sb, 2, WIN)]:
                            pe_p0, act_p0 = pe.v, act.v
                            off = hsl if wsb is not wq_sb else 0

                            @block.tensor
                            def _(tensor, wsb=wsb, nblk=nblk, base=base, pe_p0=pe_p0, act_p0=act_p0, h=h, off=off):
                                if wsb is wq_sb:
                                    tensor.wait_ge(_sl[2], wq_gd_end[h])
                                else:
                                    tensor.wait_ge(_sl[h % 2], kv_gd_end[h])
                                for nb in range(nblk):
                                    if nb >= 2:
                                        tensor.wait_ge(_act, act_p0 + nb - 1)
                                    bank = ps_a if nb % 2 == 0 else ps_b
                                    for c in range(DC):
                                        pe.inc(tensor.matmul(bank[:], wsb[:, off + c * 128:off + (c + 1) * 128],
                                                             x0fm[:, c * SPAN + base + nb * 512:c * SPAN + base + nb * 512 + 512],
                                                             start=(c == 0), stop=(c == DC - 1)))
                            if wsb is wq_sb:
                                qproj_pe.append(pe.v)
                                if h + 1 < H:
                                    emit_wq_load(h + 1)

                            @block.scalar
                            def _(scalar, dst=dst, nblk=nblk, pe_p0=pe_p0):
                                for nb in range(nblk):
                                    scalar.wait_ge(_pe, pe_p0 + (nb + 1) * DC)
                                    bank = ps_a if nb % 2 == 0 else ps_b
                                    act.inc(scalar.copy(dst[:, nb * 512:nb * 512 + 512], bank[:]))

                        pe_t0, act_t0 = pe.v, act.v

                        @block.tensor
                        def _(tensor, act_t0=act_t0):
                            tensor.wait_ge(_act, act_t0)
                            for tc in range(NSP):
                                if tc >= 1:
                                    tensor.wait_ge(_act, act_t0 + tc)
                                pe.inc(tensor.transpose(ps_tb[:], vh_sb[:, tc * 128:(tc + 1) * 128], idenb_sb[:]))

                        @block.scalar
                        def _(scalar, pe_t0=pe_t0):
                            for tc in range(NSP):
                                scalar.wait_ge(_pe, pe_t0 + tc + 1)
                                act.inc(scalar.copy(vt_sb[:, tc * 128:(tc + 1) * 128], ps_tb[:]))

                        for qb in range(2):
                            chunks = list(range(4 * qb, 4 * qb + 20))
                            s_done, e_done, p_done, m_done = [], [], [], []
                            for g in range(5):
                                @block.tensor
                                def _(tensor, g=g, qb=qb, chunks=chunks):
                                    if g == 0:
                                        tensor.wait_ge(_act, act.v)
                                        tensor.wait_ge(_dve, dve.v)
                                    else:
                                        tensor.wait_ge(_act, e_done[g - 1])
                                    for c4 in range(4):
                                        c = chunks[4 * g + c4]
                                        Doff = 128 * c - 512 * qb
                                        boundary = not (512 <= Doff <= 1921)
                                        bank = ps_abcd[:, c4 * 512:(c4 + 1) * 512]
                                        pe.inc(tensor.matmul(bank, kh_sb[:, c * 128:(c + 1) * 128],
                                                             qh_sb[:, qb * 512:qb * 512 + 512], start=True, stop=not boundary))
                                        if boundary:
                                            o = 2560 - Doff
                                            pe.inc(tensor.matmul(bank, idenb_sb[:], t2_sb[:, o:o + 512], start=False, stop=True))
                                    s_done.append(pe.v)

                                @block.vector
                                def _(v, g=g, chunks=chunks):
                                    v.wait_ge(_pe, s_done[g])
                                    for c4 in range(4):
                                        c = chunks[4 * g + c4]
                                        bank = ps_abcd[:, c4 * 512:(c4 + 1) * 512]
                                        dve.inc(v.tensor_scalar(bank, bank, kvalid_sb[:, c:c + 1], None, ALU.add))
                                    m_done.append(dve.v)

                                @block.scalar
                                def _(scalar, g=g):
                                    scalar.wait_ge(_dve, m_done[g])
                                    if g >= 1:
                                        scalar.wait_ge(_pe, p_done[g - 1])
                                    act.inc(scalar.activation(ptil[:, 0:2048],
                                                              ps_abcd[:, 0:2048], AF.Exp, scale=INV_SQRT_HD))
                                    e_done.append(act.v)

                                @block.tensor
                                def _(tensor, g=g, chunks=chunks):
                                    tensor.wait_ge(_act, e_done[g])
                                    if g == 0:
                                        tensor.wait_ge(_dve, dve.v)
                                    for c4 in range(4):
                                        c = chunks[4 * g + c4]
                                        sl = ptil[:, c4 * 512:(c4 + 1) * 512]
                                        pe.inc(tensor.matmul(ps_sum[:], ones_sb[:], sl,
                                                             start=(g == 0 and c4 == 0), stop=(g == 4 and c4 == 3)))
                                        pe.inc(tensor.matmul(ps_e[:], vt_sb[:, c * 128:(c + 1) * 128], sl,
                                                             start=(g == 0 and c4 == 0), stop=(g == 4 and c4 == 3)))
                                    p_done.append(pe.v)

                            @block.vector
                            def _(v):
                                v.wait_ge(_pe, pe.v)
                                dve.inc(v.reciprocal(rec_sb[:], ps_sum[:]))
                                v.wait_ge(_dve, dve.v)
                                dve.inc(v.tensor_copy(recb_sb[:], rec_sb[:]))

                            @block.tensor
                            def _(t):
                                t.wait_ge(_dve, dve.v)
                                pe.inc(t.matmul(ps_d[:], ones1r_sb[:], recb_sb[:], start=True, stop=True))

                            @block.vector
                            def _(v, h=h, qb=qb):
                                v.wait_ge(_pe, pe.v)
                                dve.inc(v.tensor_copy(bcast_sb[:], ps_d[:]))
                                v.wait_ge(_dve, dve.v)
                                dve.inc(v.tensor_tensor(ctx_all[:, h * ROWS + qb * 512:h * ROWS + qb * 512 + 512],
                                                        ps_e[:], bcast_sb[:], ALU.mult))
                        head_pe_end.append(pe.v)
                ph2_pe, ph2_act, ph2_dve, ph2_gd = pe.v, act.v, dve.v, gd.v

            with (
                nc.sbuf_tensor("wo_sb", [128, DC * 128], BF16) as wo_sb,
                nc.sbuf_tensor("xpreH", [128, DC * 512], F32) as xpreH,
            ):
                Wo_r = Wo.rearrange("(c p) f -> c p f", p=128)
                for hb in range(2):
                    evac_dve = []
                    for oc in range(DC):
                        @block.gpsimd
                        def _(g, oc=oc, hb=hb):
                            g.wait_ge(_gd, spill_gd)
                            g.wait_ge(_pe, pe.v)
                            g.wait_ge(_dve, dve.v)
                            for c in range(DC):
                                gd.inc(g.dma_start(out=wo_sb[:, c * 128:(c + 1) * 128], in_=Wo_r[c, :, oc * 128:(oc + 1) * 128]), 16)
                            gd.inc(g.dma_start(out=xres_sb[:], in_=x0ob_dram[oc][:, hb * 512:hb * 512 + 512]), 16)

                        @block.tensor
                        def _(tensor, oc=oc, hb=hb):
                            tensor.wait_ge(_gd, gd.v)
                            if oc == 0 and hb == 0:
                                tensor.wait_ge(_dve, ph2_dve)
                            if oc >= 2:
                                tensor.wait_ge(_dve, evac_dve[oc - 2])
                            bank = ps_a if oc % 2 == 0 else ps_b
                            for ic in range(DC):
                                pe.inc(tensor.matmul(bank[:], wo_sb[:, ic * 128:(ic + 1) * 128],
                                                     ctx_all[:, ic * ROWS + hb * 512:ic * ROWS + hb * 512 + 512],
                                                     start=(ic == 0), stop=(ic == DC - 1)))

                        @block.vector
                        def _(v, oc=oc):
                            v.wait_ge(_pe, pe.v)
                            bank = ps_a if oc % 2 == 0 else ps_b
                            dve.inc(v.tensor_tensor(xpreH[:, oc * 512:oc * 512 + 512], bank[:], xres_sb[:], ALU.add))
                            evac_dve.append(dve.v)

                    emit_ln_stats_and_bcast(lambda c: xpreH[:, c * 512:c * 512 + 512])

                    for oc in range(DC):
                        @block.vector
                        def _(v, oc=oc):
                            v.wait_ge(_dve, dve.v)
                            v.wait_ge(_ld, ld.v)
                            dve.inc(v.tensor_tensor(evtmp[:], xpreH[:, oc * 512:oc * 512 + 512], lnA_sb[:], ALU.mult))
                            v.wait_ge(_dve, dve.v)
                            dve.inc(v.tensor_tensor(evtmp[:], evtmp[:], lnB_sb[:], ALU.subtract))
                            v.wait_ge(_dve, dve.v)
                            dve.inc(v.tensor_scalar(xpb_sb[:], evtmp[:], su_ug_sb[:, oc:oc + 1], None, ALU.mult))

                        @block.sync
                        def _(sy, oc=oc, hb=hb):
                            sy.wait_ge(_dve, dve.v)
                            ld.inc(sy.dma_start(out=xA_dram[oc][:, hb * 512:hb * 512 + 512], in_=evtmp[:]), 16)
                            ld.inc(sy.dma_start(out=xsu_dram[oc][:, hb * 512:hb * 512 + 512], in_=xpb_sb[:]), 16)
            ph3_ld = ld.v

        def emit_butterflies(Xbuf, stripe_sb, Cn, Tn, TS, npass):
            for st in range(Tn // TS):
                @block.vector
                def _(v, st=st):
                    for j in range(npass):
                        g_ = Cn >> (j + 1)
                        s_ = 1 << j
                        xv = Xbuf[:].rearrange("p (g b s t) -> p g b s t", g=g_, b=2, s=s_)
                        sv = stripe_sb[:, 0:Cn * TS].rearrange("p (g b s t) -> p g b s t", g=g_, b=2, s=s_)
                        if j % 2 == 0:
                            lo = xv[:, :, 0:1, :, st * TS:st * TS + TS]
                            hi = xv[:, :, 1:2, :, st * TS:st * TS + TS]
                            dlo = sv[:, :, 0:1]
                            dhi = sv[:, :, 1:2]
                        else:
                            lo = sv[:, :, 0:1]
                            hi = sv[:, :, 1:2]
                            dlo = xv[:, :, 0:1, :, st * TS:st * TS + TS]
                            dhi = xv[:, :, 1:2, :, st * TS:st * TS + TS]
                        v.tensor_tensor(dlo, lo, hi, ALU.add)
                        dve.inc(v.tensor_tensor(dhi, lo, hi, ALU.subtract))
                        v.wait_ge(_dve, dve.v)

        with (
            nc.sbuf_tensor("ybuf", [128, DC * ROWS], BF16) as ybuf,
            nc.sbuf_tensor("h2a", [128, IC * 512], BF16) as h2a,
            nc.sbuf_tensor("stripe", [128, IC * 64], BF16) as stripe,
            nc.sbuf_tensor("wstg_sb", [128, 2 * D], BF16) as wstg_sb,
            nc.sbuf_tensor("wTu_sb", [128, DC * 512], BF16) as wTu_sb,
            nc.sbuf_tensor("wTg_sb", [128, DC * 512], BF16) as wTg_sb,
            nc.sbuf_tensor("gtmp", [128, 512], BF16) as gtmp,
            nc.sbuf_tensor("utmp", [128, 512], BF16) as utmp,
            nc.sbuf_tensor("xsuin", [128, 2 * 512], BF16) as xsuin,
            nc.sbuf_tensor("xBpre", [128, DC * 512], F32) as xBpre,
        ):
            pe40, act40 = pe.v, act.v
            dve40 = dve.v
            for nb in range(2 * DC):
                c_chunk, half = divmod(nb, 2)

                @block.gpsimd
                def _(g, nb=nb, c_chunk=c_chunk, half=half):
                    g.wait_ge(_ld, ph3_ld)
                    if nb >= 2:
                        g.wait_ge(_pe, pe40 + nb - 1)
                    gd.inc(g.dma_start(out=xsuin[:, (nb % 2) * 512:(nb % 2) * 512 + 512],
                                       in_=xsu_dram[c_chunk][:, half * 512:half * 512 + 512]), 16)

                @block.tensor
                def _(t, nb=nb):
                    t.wait_ge(_gd, gd.v)
                    if nb >= 2:
                        t.wait_ge(_act, act40 + nb - 1)
                    bank = ps_a if nb % 2 == 0 else ps_b
                    pe.inc(t.matmul(bank[:], h128_sb[:], xsuin[:, (nb % 2) * 512:(nb % 2) * 512 + 512], start=True, stop=True))

                @block.scalar
                def _(s, nb=nb, c_chunk=c_chunk, half=half):
                    s.wait_ge(_pe, pe40 + nb + 1)
                    bank = ps_a if nb % 2 == 0 else ps_b
                    act.inc(s.activation(ybuf[:, c_chunk * ROWS + half * 512:c_chunk * ROWS + half * 512 + 512],
                                         bank[:], AF.Copy, scale=F2048))

            @block.vector
            def _(v):
                v.wait_ge(_act, act.v)
                dve.inc(v.nop())
            emit_butterflies(ybuf, stripe, DC, ROWS, 64, 4)
            h1_dve = dve.v

            Wup_r = Wup.rearrange("(m p) d -> m p d", p=128)
            Wg_r = Wgate.rearrange("(m p) d -> m p d", p=128)
            Wd_r = Wdown.rearrange("(m p) d -> m p d", p=128)
            wupbf_r = wupbf_dram.rearrange("(m p) d -> m p d", p=128)
            wgbf_r = wgbf_dram.rearrange("(m p) d -> m p d", p=128)

            cast_jobs = []
            for m in range(IC):
                cast_jobs.append((Wup_r[m], wupbf_r[m]))
            for m in range(IC):
                cast_jobs.append((Wg_r[m], wgbf_r[m]))
            for m in range(DC):
                for s4 in range(4):
                    cast_jobs.append((Wd_r[m, :, s4 * D:(s4 + 1) * D],
                                      wdbf_dram.rearrange("o (s d) -> o s d", s=4)[m * 128:(m + 1) * 128, s4]))
            cj_gd = []
            for j, (srcap, dstap) in enumerate(cast_jobs):
                sl = j % 2

                @block.gpsimd
                def _(g, j=j, sl=sl, srcap=srcap, dstap=dstap, dve40=dve40):
                    if j <= 1:
                        g.wait_ge(_dve, dve40)
                    else:
                        g.wait_ge(_sl[7], 16 * (j - 1))
                    slc[5 + sl].inc(g.dma_start(out=wstg_sb[:, sl * D:(sl + 1) * D], in_=srcap), 16)
                    cj_gd.append(slc[5 + sl].v)
                    g.wait_ge(_sl[5 + sl], cj_gd[j])
                    slc[7].inc(g.dma_start(out=dstap, in_=wstg_sb[:, sl * D:(sl + 1) * D]), 16)

            for hb in range(2):
                for hb4 in range(IC // 4):
                    @block.sync
                    def _(sy, hb4=hb4, hb=hb):
                        sy.wait_ge(_pe, pe.v)
                        if hb == 0:
                            sy.wait_ge(_sl[7], 16 * (68 + 4 * hb4))
                        for k in range(DC):
                            ld.inc(sy.dma_start(out=wTu_sb[:, k * 512:(k + 1) * 512],
                                                in_=wupbf_dram[hb4 * 512:(hb4 + 1) * 512, k * 128:(k + 1) * 128], transpose=True), 16)
                        for k in range(DC):
                            ld.inc(sy.dma_start(out=wTg_sb[:, k * 512:(k + 1) * 512],
                                                in_=wgbf_dram[hb4 * 512:(hb4 + 1) * 512, k * 128:(k + 1) * 128], transpose=True), 16)

                    for mi in range(4):
                        m = hb4 * 4 + mi

                        @block.tensor
                        def _(t, m=m, mi=mi, hb=hb, h1_dve=h1_dve):
                            t.wait_ge(_ld, ld.v)
                            if m == 0 and hb == 0:
                                t.wait_ge(_dve, h1_dve)
                            else:
                                t.wait_ge(_dve, dve.v)
                            for k in range(DC):
                                pe.inc(t.matmul(ps_a[:], wTu_sb[:, k * 512 + mi * 128:k * 512 + (mi + 1) * 128],
                                                ybuf[:, k * ROWS + hb * 512:k * ROWS + hb * 512 + 512],
                                                start=(k == 0), stop=(k == DC - 1)))
                            for k in range(DC):
                                pe.inc(t.matmul(ps_b[:], wTg_sb[:, k * 512 + mi * 128:k * 512 + (mi + 1) * 128],
                                                ybuf[:, k * ROWS + hb * 512:k * ROWS + hb * 512 + 512],
                                                start=(k == 0), stop=(k == DC - 1)))

                        @block.scalar
                        def _(s, m=m):
                            s.wait_ge(_pe, pe.v)
                            act.inc(s.activation(gtmp[:], ps_b[:], AF.Silu, scale=wsug64_sb[:, IC + m:IC + m + 1]))

                        @block.vector
                        def _(v, m=m):
                            v.wait_ge(_act, act.v)
                            dve.inc(v.tensor_scalar(utmp[:], ps_a[:], wsug64_sb[:, m:m + 1], None, ALU.mult))
                            v.wait_ge(_dve, dve.v)
                            dve.inc(v.scalar_tensor_tensor(h2a[:, m * 512:m * 512 + 512], utmp[:],
                                                           su_dp_sb[:, m:m + 1], gtmp[:], ALU.mult, ALU.mult))

                peA0, actA0 = pe.v, act.v
                for m in range(IC):
                    @block.tensor
                    def _(t, m=m):
                        if m == 0:
                            t.wait_ge(_dve, dve.v)
                        if m >= 2:
                            t.wait_ge(_act, actA0 + m - 1)
                        bank = ps_c if m % 2 == 0 else ps_d
                        pe.inc(t.matmul(bank[:], h128_sb[:], h2a[:, m * 512:m * 512 + 512], start=True, stop=True))

                    @block.scalar
                    def _(s, m=m):
                        s.wait_ge(_pe, peA0 + m + 1)
                        bank = ps_c if m % 2 == 0 else ps_d
                        act.inc(s.activation(h2a[:, m * 512:m * 512 + 512], bank[:], AF.Copy, scale=F8192))

                @block.vector
                def _(v):
                    v.wait_ge(_act, act.v)
                    dve.inc(v.nop())
                emit_butterflies(h2a, stripe, IC, 512, 64, 6)

                evac3 = []
                bf_dve = dve.v
                for ob in range(DC // 4):
                    dbank = [ps_a, ps_b, ps_c, ps_d]
                    for fg in range(4):
                        @block.sync
                        def _(sy, ob=ob, fg=fg, hb=hb):
                            sy.wait_ge(_pe, pe.v)
                            if hb == 0 and ob == 0 and fg == 0:
                                sy.wait_ge(_sl[7], 16 * 192)
                            for k in range(DC):
                                fk = fg * DC + k
                                ld.inc(sy.dma_start(out=wTu_sb[:, k * 512:(k + 1) * 512],
                                                    in_=wdbf_dram[ob * 512:(ob + 1) * 512, fk * 128:(fk + 1) * 128], transpose=True), 16)

                        for oi in range(4):
                            oc = ob * 4 + oi

                            @block.tensor
                            def _(t, ob=ob, fg=fg, oi=oi, oc=oc, bf_dve=bf_dve):
                                t.wait_ge(_ld, ld.v)
                                if ob == 0 and fg == 0 and oi == 0:
                                    t.wait_ge(_dve, bf_dve)
                                if fg == 0 and oi == 0 and ob > 0:
                                    t.wait_ge(_dve, dve.v)
                                bank = dbank[oi]
                                for k in range(DC):
                                    fk = fg * DC + k
                                    pe.inc(t.matmul(bank[:], wTu_sb[:, k * 512 + oi * 128:k * 512 + (oi + 1) * 128],
                                                    h2a[:, fk * 512:fk * 512 + 512], start=(fg == 0 and k == 0), stop=(fg == 3 and k == DC - 1)))

                    for oi in range(4):
                        oc = ob * 4 + oi

                        @block.gpsimd
                        def _(g, oc=oc, hb=hb):
                            g.wait_ge(_dve, dve.v)
                            gd.inc(g.dma_start(out=evtmp2[:], in_=xA_dram[oc][:, hb * 512:hb * 512 + 512]), 16)

                        @block.vector
                        def _(v, oc=oc, oi=oi):
                            v.wait_ge(_pe, pe.v)
                            v.wait_ge(_gd, gd.v)
                            bank = dbank[oi]
                            dve.inc(v.tensor_scalar(evtmp[:], bank[:], wsdp64_sb[:, oc:oc + 1], None, ALU.mult))
                            v.wait_ge(_dve, dve.v)
                            dve.inc(v.tensor_tensor(xBpre[:, oc * 512:oc * 512 + 512], evtmp[:], evtmp2[:], ALU.add))
                            evac3.append(dve.v)

                emit_ln_stats_and_bcast(lambda c: xBpre[:, c * 512:c * 512 + 512])

                for oc in range(DC):
                    @block.vector
                    def _(v, oc=oc):
                        v.wait_ge(_dve, dve.v)
                        v.wait_ge(_ld, ld.v)
                        v.tensor_tensor(evtmp[:], xBpre[:, oc * 512:oc * 512 + 512], lnA_sb[:], ALU.mult)
                        dve.inc(v.tensor_tensor(evtmp[:], evtmp[:], lnB_sb[:], ALU.subtract))
                        dve.inc(v.tensor_copy(xpb_sb[:], evtmp[:]))

                    @block.sync
                    def _(sy, oc=oc, hb=hb):
                        sy.wait_ge(_dve, dve.v)
                        ld.inc(sy.dma_start(out=xB_out[oc][:, hb * 512:hb * 512 + 512], in_=evtmp[:]), 16)
                        ld.inc(sy.dma_start(out=xBb_dram[oc][:, hb * 512:hb * 512 + 512], in_=xpb_sb[:]), 16)
        ph4_ld = ld.v

        with (
            nc.sbuf_tensor("xBb_sb", [128, DC * ROWS], BF16) as xBb_sb,
            nc.sbuf_tensor("wqkv_sb", [128, 2 * DC * 128], BF16) as wqkv_sb,
        ):
            @block.gpsimd
            def _(g):
                g.wait_ge(_ld, ph4_ld)
                g.wait_ge(_dve, dve.v)
                for c in range(DC):
                    gd.inc(g.dma_start(out=xBb_sb[:, c * ROWS:(c + 1) * ROWS], in_=xBb_dram[c]), 16)

            Wqkv_p = Wqkv.rearrange("(c p) f -> p c f", p=128)
            qk_gd, qk_mm = [], []

            def emit_qkv_load(oc):
                sc = slc[0 + oc % 2]

                @block.gpsimd
                def _(g, oc=oc, sc=sc, pev=pe.v, dvev=dve.v):
                    if oc <= 1:
                        g.wait_ge(_pe, pev)
                        g.wait_ge(_dve, dvev)
                    if oc >= 2:
                        g.wait_ge(_pe, qk_mm[oc - 2])
                    sl = (oc % 2) * DC * 128
                    for c in range(DC):
                        sc.inc(g.dma_start(out=wqkv_sb[:, sl + c * 128:sl + (c + 1) * 128],
                                           in_=Wqkv_p[:, c, oc * 128:(oc + 1) * 128]), 16)
                qk_gd.append(sc.v)

            emit_qkv_load(0)
            for oc in range(QKVC):
                if oc + 1 < QKVC:
                    emit_qkv_load(oc + 1)
                for nb in range(2):
                    @block.tensor
                    def _(t, oc=oc, nb=nb):
                        t.wait_ge(_sl[oc % 2], qk_gd[oc])
                        t.wait_ge(_dve, dve.v)
                        sl = (oc % 2) * DC * 128
                        bank = ps_a if nb % 2 == 0 else ps_b
                        for c in range(DC):
                            pe.inc(t.matmul(bank[:], wqkv_sb[:, sl + c * 128:sl + (c + 1) * 128],
                                            xBb_sb[:, c * ROWS + nb * 512:c * ROWS + nb * 512 + 512],
                                            start=(c == 0), stop=(c == DC - 1)))
                        if nb == 1:
                            qk_mm.append(pe.v)

                    @block.vector
                    def _(v, oc=oc, nb=nb):
                        v.wait_ge(_pe, pe.v)
                        v.wait_ge(_ld, ld.v)
                        bank = ps_a if nb % 2 == 0 else ps_b
                        dve.inc(v.tensor_copy(evtmp[:], bank[:]))
                        dve.inc(v.tensor_reduce(red_sb[:], bank[:], axis=AX.X, op=ALU.max, apply_absolute_value=True))
                        v.wait_ge(_dve, dve.v)
                        j = oc // DC
                        dve.inc(v.tensor_tensor(amax_sb[:, j:j + 1], amax_sb[:, j:j + 1], red_sb[:], ALU.max))

                    @block.sync
                    def _(sy, oc=oc, nb=nb):
                        sy.wait_ge(_dve, dve.v)
                        ld.inc(sy.dma_start(out=qkv_out[oc][:, nb * 512:nb * 512 + 512], in_=evtmp[:]), 16)

        with nc.sbuf_tensor("gw_sb", [128, INTER], F32) as gw_sb:
            for j, (src, nch, rowlen) in enumerate([(gslg, 8, D), (gslv, 8, D), (gslo, 2, INTER)]):
                src_r = src.rearrange("(n p) d -> n p d", p=128)
                for i in range(nch):
                    @block.sync
                    def _(sy, i=i, src_r=src_r, rowlen=rowlen):
                        sy.wait_ge(_dve, dve.v)
                        ld.inc(sy.dma_start(out=gw_sb[:, 0:rowlen], in_=src_r[i]), 16)

                    @block.vector
                    def _(v, j=j, rowlen=rowlen):
                        v.wait_ge(_ld, ld.v)
                        dve.inc(v.tensor_reduce(red_sb[:], gw_sb[:, 0:rowlen], axis=AX.X, op=ALU.add, apply_absolute_value=True))
                        v.wait_ge(_dve, dve.v)
                        dve.inc(v.tensor_tensor(gsum_sb[:, j:j + 1], gsum_sb[:, j:j + 1], red_sb[:], ALU.add))

        for j in range(6):
            @block.tensor
            def _(t, j=j):
                t.wait_ge(_dve, dve.v)
                srccol = amax_sb[:, j:j + 1] if j < 3 else gsum_sb[:, j - 3:j - 2]
                pe.inc(t.transpose(ps_d[0:1, 0:128], srccol, iden_sb[:]))

            @block.vector
            def _(v, j=j):
                v.wait_ge(_pe, pe.v)
                dve.inc(v.tensor_copy(ptr_sb[:], ps_d[0:1, 0:128]))
                v.wait_ge(_dve, dve.v)
                dve.inc(v.tensor_reduce(part_sb[0:1, j:j + 1], ptr_sb[:], axis=AX.X,
                                        op=(ALU.max if j < 3 else ALU.add),
                                        apply_absolute_value=False))

        @block.vector
        def _(v):
            v.wait_ge(_dve, dve.v)
            dve.inc(v.memset(part_sb[0:1, 6:8], 0.0))

        @block.sync
        def _(sy):
            sy.wait_ge(_dve, dve.v)
            ld.inc(sy.dma_start(out=partials, in_=part_sb[:]), 16)

        @block.gpsimd
        def _(g):
            g.wait_ge(_ld, ld.v)
            g.wait_ge(_gd, gd.v)
            g.wait_ge(_act, act.v)
            g.wait_ge(_pe, pe.v)
            g.wait_ge(_dve, dve.v)

    return nc



def prep_A_inputs(inputs, core):
    grp, j = divmod(core, 4)
    x0 = np.asarray(inputs["L0_attn_x"], dtype=np.float32)[grp]
    r0 = j * ROWS
    lo = r0 - WIN
    span = np.zeros((SPAN, D), dtype=np.float32)
    s0 = max(0, lo)
    span[s0 - lo:, :] = x0[s0:r0 + ROWS, :]
    kv = np.zeros((128, NSP), dtype=np.float32)
    npad = -lo if lo < 0 else 0
    if npad > 0:
        rowidx = np.arange(SPAN).reshape(NSP, 128).T
        kv = np.where(rowidx < npad, np.float32(NEG), np.float32(0.0))
    f32 = lambda a: np.ascontiguousarray(np.asarray(a, dtype=np.float32))
    return {
        "x0span": span,
        "kvalid": f32(kv),
        "Wq": f32(inputs["L0_attn_Wq"]), "Wk": f32(inputs["L0_attn_Wk"]),
        "Wv": f32(inputs["L0_attn_Wv"]), "Wo": f32(inputs["L0_attn_Wo"]),
        "SUug": f32(np.asarray(inputs["L0_ffn_SU_ug"]).reshape(DC, 128).T),
        "SUdp": f32(np.asarray(inputs["L0_ffn_SU_dp"]).reshape(IC, 128).T),
        "Wsug": f32(np.asarray(inputs["L0_ffn_Wscale_ug"]).reshape(2 * IC, 128).T),
        "Wsdp": f32(np.asarray(inputs["L0_ffn_Wscale_dp"]).reshape(DC, 128).T),
        "Wup": f32(inputs["L0_ffn_W_up"]), "Wgate": f32(inputs["L0_ffn_W_gate"]),
        "Wdown": f32(inputs["L0_ffn_W_down"]), "Wqkv": f32(inputs["L1_attn_W_qkv"]),
        "gslg": f32(np.asarray(inputs["L1_ffn_gate_w"])[core * INTER // 8:(core + 1) * INTER // 8]),
        "gslv": f32(np.asarray(inputs["L1_ffn_val_w"])[core * INTER // 8:(core + 1) * INTER // 8]),
        "gslo": f32(np.asarray(inputs["L1_ffn_out_w"])[core * D // 8:(core + 1) * D // 8]),
    }



LN2C = math.log(2.0)
GMEAN = 1.0 / float(INTER * D)


def build_B(dbg=False):
    nc = bass.Bass("TRN2", target_bir_lowering=False)
    qfm = nc.dram_tensor("qfm", [D, ROWS], F32, kind="ExternalInput").ap()
    kfm = nc.dram_tensor("kfm", [D, S], F32, kind="ExternalInput").ap()
    vtm = nc.dram_tensor("vtm", [S, D], F32, kind="ExternalInput").ap()
    xBi = nc.dram_tensor("xBi", [D, ROWS], F32, kind="ExternalInput").ap()
    prep = nc.dram_tensor("prep", [128, 64], F32, kind="ExternalInput").ap()
    W_out = nc.dram_tensor("W_out", [D, D], F32, kind="ExternalInput").ap()
    b_out = nc.dram_tensor("b_out", [128, DC], F32, kind="ExternalInput").ap()
    gate_w = nc.dram_tensor("gate_w", [INTER, D], F32, kind="ExternalInput").ap()
    val_w = nc.dram_tensor("val_w", [INTER, D], F32, kind="ExternalInput").ap()
    out_w = nc.dram_tensor("out_w", [D, INTER], F32, kind="ExternalInput").ap()
    gate_b = nc.dram_tensor("gate_b", [128, IC], F32, kind="ExternalInput").ap()
    val_b = nc.dram_tensor("val_b", [128, IC], F32, kind="ExternalInput").ap()
    out_b = nc.dram_tensor("out_b", [128, DC], F32, kind="ExternalInput").ap()

    y_out = nc.dram_tensor("y_out", [DC, 128, ROWS], F32, kind="ExternalOutput").ap()

    xC_dram = nc.dram_tensor("xC_dram", [DC, 128, ROWS], F32).ap()
    k8_dram = nc.dram_tensor("k8_dram", [H, 128, S], BF16).ap()
    if dbg:
        dbg_ctx = nc.dram_tensor("dbg_ctx", [128, H * ROWS], BF16, kind="ExternalOutput").ap()
        dbg_k8 = nc.dram_tensor("dbg_k8", [H, 128, S], BF16, kind="ExternalOutput").ap()
        dbg_q8 = nc.dram_tensor("dbg_q8", [H, 128, ROWS], BF16, kind="ExternalOutput").ap()
        dbg_scal = nc.dram_tensor("dbg_scal", [128, 16], F32, kind="ExternalOutput").ap()
        dbg_xc = nc.dram_tensor("dbg_xc", [DC, 128, ROWS], F32, kind="ExternalOutput").ap()
    q8_dram = nc.dram_tensor("q8_dram", [H, 128, ROWS], BF16).ap()
    v8_dram = nc.dram_tensor("v8_dram", [H, 128, S], BF16).ap()
    wgt_dram = nc.dram_tensor("wgt_dram", [INTER, D], BF16).ap()
    wvt_dram = nc.dram_tensor("wvt_dram", [INTER, D], BF16).ap()
    wot_dram = nc.dram_tensor("wot_dram", [D, INTER], BF16).ap()
    gx_dram = nc.dram_tensor("gx_dram", [1, ROWS], F32).ap()
    gh_dram = nc.dram_tensor("gh_dram", [1, ROWS], F32).ap()

    c_iden = nc.inline_tensor(np.eye(128, dtype=np.float32), name="c_iden").ap()
    c_ones = nc.inline_tensor(np.ones((128, 1), dtype=BF), name="c_ones").ap()
    c_ones1r = nc.inline_tensor(np.ones((1, 128), dtype=BF), name="c_ones1r").ap()

    import contextlib
    _stack = contextlib.ExitStack()
    with _stack:
        _sb = lambda *a: _stack.enter_context(nc.sbuf_tensor(*a))
        _ps = lambda *a: _stack.enter_context(nc.psum_tensor(*a))
        _sem = lambda n: _stack.enter_context(nc.semaphore(n))
        iden_sb = _sb("iden_sb", [128, 128], F32)
        ones_sb = _sb("ones_sb", [128, 1], BF16)
        ones1r_sb = _sb("ones1r_sb", [1, 128], BF16)
        prep_sb = _sb("prep_sb", [128, 64], F32)
        eps_sb = _sb("eps_sb", [1, 1], F32)
        scal_sb = _sb("scal_sb", [128, 16], F32)
        stat_sb = _sb("stat_sb", [1, 5 * 512], F32)
        statb_sb = _sb("statb_sb", [1, 2 * 512], BF16)
        lnA_sb = _sb("lnA_sb", [128, 512], BF16)
        lnB_sb = _sb("lnB_sb", [128, 512], BF16)
        sqb_sb = _sb("sqb_sb", [128, 512], BF16)
        xpb_sb = _sb("xpb_sb", [128, 512], BF16)
        evtmp = _sb("evtmp", [128, 512], F32)
        evtmp2 = _sb("evtmp2", [128, 512], F32)
        rowt_sb = _sb("rowt_sb", [1, ROWS], F32)
        rowt2_sb = _sb("rowt2_sb", [1, ROWS], F32)
        rowb_sb = _sb("rowb_sb", [1, ROWS], BF16)
        scxb_sb = _sb("scxb_sb", [128, ROWS], BF16)
        deqg_sb = _sb("deqg_sb", [128, ROWS], BF16)
        deqv_sb = _sb("deqv_sb", [128, ROWS], BF16)
        schb_sb = _sb("schb_sb", [128, 512], BF16)
        deqh_sb = _sb("deqh_sb", [128, 512], BF16)
        mxt_sb = _sb("mxt_sb", [128, ROWS], F32)
        colred_sb = _sb("colred_sb", [128, 8], F32)
        ps_abcd = _ps("ps_abcd", [128, 2048], F32)
        ps_a = ps_abcd[:, 0:512]
        ps_b = ps_abcd[:, 512:1024]
        ps_c = ps_abcd[:, 1024:1536]
        ps_d = ps_abcd[:, 1536:2048]
        ps_sum = _ps("ps_sum", [1, 512], F32)
        ps_st = _ps("ps_st", [1, 512], F32)
        ps_e = _ps("ps_e", [128, 512], F32)
        ps_f = _ps("ps_f", [128, 512], F32)
        _ld = _sem("m_ld")
        _gd = _sem("m_gd")
        _pe = _sem("m_pe")
        _act = _sem("m_act")
        _dve = _sem("m_dve")
        block = _stack.enter_context(nc.Block())
        ld, gd, pe, act, dve = Ctr(_ld), Ctr(_gd), Ctr(_pe), Ctr(_act), Ctr(_dve)

        def emit_ln_stats_and_bcast(get_chunk):
            pe0, act0, dve0 = pe.v, act.v, dve.v
            for c in range(DC):
                @block.vector
                def _(v, c=c):
                    if c > 0:
                        v.wait_ge(_pe, pe0 + 2 * c)
                    dve.inc(v.tensor_copy(xpb_sb[:], get_chunk(c)))

                @block.scalar
                def _(s, c=c):
                    if c == 0:
                        s.wait_ge(_dve, dve0)
                    else:
                        s.wait_ge(_pe, pe0 + 2 * c)
                    act.inc(s.activation(sqb_sb[:], get_chunk(c), AF.Square))

                @block.tensor
                def _(t, c=c):
                    t.wait_ge(_dve, dve0 + c + 1)
                    t.wait_ge(_act, act0 + c + 1)
                    pe.inc(t.matmul(ps_st[:], ones_sb[:], xpb_sb[:], start=(c == 0), stop=(c == DC - 1)))
                    pe.inc(t.matmul(ps_sum[:], ones_sb[:], sqb_sb[:], start=(c == 0), stop=(c == DC - 1)))

            HBl = 512

            @block.vector
            def _(v):
                v.wait_ge(_pe, pe.v)
                dve.inc(v.tensor_copy(stat_sb[0:1, 0:HBl], ps_st[:]))
                dve.inc(v.tensor_copy(stat_sb[0:1, HBl:2 * HBl], ps_sum[:]))
                v.wait_ge(_dve, dve.v)
                c1 = 1.0 / float(D)
                dve.inc(v.tensor_scalar(stat_sb[0:1, 4 * HBl:5 * HBl], stat_sb[0:1, 0:HBl], c1, None, ALU.mult))
                v.wait_ge(_dve, dve.v)
                dve.inc(v.tensor_tensor(stat_sb[0:1, 2 * HBl:3 * HBl], stat_sb[0:1, 4 * HBl:5 * HBl], stat_sb[0:1, 4 * HBl:5 * HBl], ALU.mult))
                v.wait_ge(_dve, dve.v)
                dve.inc(v.scalar_tensor_tensor(stat_sb[0:1, 2 * HBl:3 * HBl], stat_sb[0:1, HBl:2 * HBl], c1, stat_sb[0:1, 2 * HBl:3 * HBl], ALU.mult, ALU.subtract))

            @block.scalar
            def _(s):
                s.wait_ge(_dve, dve.v)
                act.inc(s.activation(stat_sb[0:1, 2 * HBl:3 * HBl], stat_sb[0:1, 2 * HBl:3 * HBl], AF.Sqrt, bias=eps_sb[0:1, 0:1]))

            @block.vector
            def _(v):
                v.wait_ge(_act, act.v)
                dve.inc(v.reciprocal(stat_sb[0:1, 2 * HBl:3 * HBl], stat_sb[0:1, 2 * HBl:3 * HBl]))
                v.wait_ge(_dve, dve.v)
                dve.inc(v.tensor_tensor(stat_sb[0:1, 3 * HBl:4 * HBl], stat_sb[0:1, 4 * HBl:5 * HBl], stat_sb[0:1, 2 * HBl:3 * HBl], ALU.mult))
                v.wait_ge(_dve, dve.v)
                v.tensor_copy(statb_sb[0:1, 0:HBl], stat_sb[0:1, 2 * HBl:3 * HBl])
                dve.inc(v.tensor_copy(statb_sb[0:1, HBl:2 * HBl], stat_sb[0:1, 3 * HBl:4 * HBl]))

            @block.tensor
            def _(t):
                t.wait_ge(_dve, dve.v)
                pe.inc(t.matmul(ps_a[:], ones1r_sb[:], statb_sb[0:1, 0:HBl], start=True, stop=True))
                pe.inc(t.matmul(ps_b[:], ones1r_sb[:], statb_sb[0:1, HBl:2 * HBl], start=True, stop=True))

            @block.vector
            def _(v):
                v.wait_ge(_pe, pe.v)
                dve.inc(v.tensor_copy(lnA_sb[:], ps_a[:]))
                dve.inc(v.tensor_copy(lnB_sb[:], ps_b[:]))

        @block.sync
        def _(sync):
            for dst, src_ in [(iden_sb[:], c_iden), (ones_sb[:], c_ones),
                              (ones1r_sb[:], c_ones1r), (prep_sb[:], prep)]:
                ld.inc(sync.dma_start(out=dst, in_=src_), 16)

        @block.vector
        def _(v):
            v.wait_ge(_ld, ld.v)
            v.memset(eps_sb[:], LN_EPS)
            dve.inc(v.memset(scal_sb[:, 11:12], -3.0 * LN2C))
            pv = prep_sb[:].rearrange("p (c s) -> p s c", c=8, s=8)
            for j in range(3):
                dve.inc(v.tensor_reduce(scal_sb[:, j:j + 1], pv[:, j:j + 1, :], axis=AX.X, op=ALU.max))
            for j in range(3):
                dve.inc(v.tensor_reduce(scal_sb[:, 12 + j:13 + j], pv[:, 3 + j:4 + j, :], axis=AX.X, op=ALU.add))
            v.wait_ge(_dve, dve.v)
            dve.inc(v.tensor_scalar(scal_sb[:, 4:5], scal_sb[:, 2:3], 1.0 / 448.0, None, ALU.mult))
            dve.inc(v.tensor_scalar(scal_sb[:, 3:4], scal_sb[:, 0:1], 1.0 / 448.0, None, ALU.mult))
            v.wait_ge(_dve, dve.v)
            dve.inc(v.scalar_tensor_tensor(scal_sb[:, 3:4], scal_sb[:, 1:2], 1.0 / 448.0, scal_sb[:, 3:4], ALU.mult, ALU.mult))
            v.wait_ge(_dve, dve.v)
            dve.inc(v.tensor_scalar(scal_sb[:, 3:4], scal_sb[:, 3:4], INV_SQRT_HD, None, ALU.mult))
            v.wait_ge(_dve, dve.v)
            for j in range(3):
                dve.inc(v.reciprocal(scal_sb[:, j:j + 1], scal_sb[:, j:j + 1]))
                v.wait_ge(_dve, dve.v)
            v.wait_ge(_dve, dve.v)
            for j in range(3):
                dve.inc(v.tensor_scalar(scal_sb[:, j:j + 1], scal_sb[:, j:j + 1], 448.0, None, ALU.mult))
            v.wait_ge(_dve, dve.v)
            for j in range(3):
                dve.inc(v.tensor_scalar(scal_sb[:, 12 + j:13 + j], scal_sb[:, 12 + j:13 + j], GMEAN, None, ALU.mult))
            v.wait_ge(_dve, dve.v)
            for j in range(3):
                dve.inc(v.tensor_scalar(scal_sb[:, 12 + j:13 + j], scal_sb[:, 12 + j:13 + j], 1e-5, None, ALU.max))
            v.wait_ge(_dve, dve.v)
            for j in range(3):
                dve.inc(v.tensor_scalar(scal_sb[:, 5 + 2 * j:6 + 2 * j], scal_sb[:, 12 + j:13 + j], 0.5, None, ALU.mult))
            v.wait_ge(_dve, dve.v)
            for j in range(3):
                dve.inc(v.tensor_scalar(scal_sb[:, 6 + 2 * j:7 + 2 * j], scal_sb[:, 5 + 2 * j:6 + 2 * j], -1.0, None, ALU.mult))
            v.wait_ge(_dve, dve.v)
            dve.inc(v.reciprocal(scal_sb[:, 15:16], scal_sb[:, 12:13]))
            v.wait_ge(_dve, dve.v)
            dve.inc(v.tensor_tensor(scal_sb[:, 15:16], scal_sb[:, 15:16], scal_sb[:, 13:14], ALU.mult))
        b0_dve = dve.v

        kfm_r = kfm.rearrange("(c p) s -> c p s", p=128)
        qfm_r = qfm.rearrange("(c p) s -> c p s", p=128)
        vtm_r = vtm.rearrange("(c p) h -> p c h", p=128)
        with nc.sbuf_tensor("ctx_all", [128, H * ROWS], BF16) as ctx_all:
            _st1 = contextlib.ExitStack()
            with _st1:
                W1 = 1024
                qin2 = _st1.enter_context(nc.sbuf_tensor("qin2", [128, 4 * W1], F32))
                q8st = _st1.enter_context(nc.sbuf_tensor("q8st", [128, 4 * W1], BF16))
                qa_s = _st1.enter_context(nc.sbuf_tensor("qa_s", [128, 4 * W1], F32))
                qu_s = _st1.enter_context(nc.sbuf_tensor("qu_s", [128, 4 * W1], F32))
                qw_s = _st1.enter_context(nc.sbuf_tensor("qw_s", [128, 4 * W1], F32))
                qr_s = _st1.enter_context(nc.sbuf_tensor("qr_s", [128, 4 * W1], F32))
                qti_s = _st1.enter_context(nc.sbuf_tensor("qti_s", [128, 4 * W1], I32))
                qsg_s = _st1.enter_context(nc.sbuf_tensor("qsg_s", [128, 4 * W1], BF16))
                qjobs = []
                for h in range(H):
                    for u in range(4):
                        qjobs.append((kfm_r[h][:, u * W1:(u + 1) * W1], 1,
                                      k8_dram[h][:, u * W1:(u + 1) * W1], False))
                    for u in range(4):
                        qjobs.append((vtm_r[:, u * 8:(u + 1) * 8, h * 128:(h + 1) * 128], 2,
                                      v8_dram[h][:, u * W1:(u + 1) * W1], True))
                    qjobs.append((qfm_r[h], 0, q8_dram[h], False))
                njob = len(qjobs)
                q_dve0, q_pe0, q_act0 = dve.v, pe.v, act.v
                src_done = [0] * 4
                d4_done = [q_dve0] * 4
                st_done = [0] * 4
                for p in range(njob // 2):
                    prs = []
                    for j in (2 * p, 2 * p + 1):
                        sl = j % 4
                        srcap, invc, dstd, isv = qjobs[j]
                        dst = q8st[:, sl * W1:(sl + 1) * W1]
                        prs.append((sl, srcap, invc, dstd, isv, dst))

                    @block.sync
                    def _(sy, p=p, prs=prs):
                        for sl, srcap, invc, dstd, isv, dst in prs:
                            if p <= 1:
                                sy.wait_ge(_dve, q_dve0)
                                sy.wait_ge(_pe, q_pe0)
                                sy.wait_ge(_act, q_act0)
                            else:
                                sy.wait_ge(_act, src_done[sl])
                            if isv:
                                ld.inc(sy.dma_start(out=qin2[:, sl * W1:(sl + 1) * W1]
                                                    .rearrange("p (c h2) -> p c h2", c=8),
                                                    in_=srcap), 16)
                            else:
                                ld.inc(sy.dma_start(out=qin2[:, sl * W1:(sl + 1) * W1], in_=srcap), 16)
                    pair_ld = ld.v

                    @block.scalar
                    def _(s, prs=prs, pair_ld=pair_ld):
                        s.wait_ge(_ld, pair_ld)
                        for sl, srcap, invc, dstd, isv, dst in prs:
                            s.wait_ge(_dve, d4_done[sl])
                        for sl, srcap, invc, dstd, isv, dst in prs:
                            act.inc(s.activation(qsg_s[:, sl * W1:(sl + 1) * W1],
                                                 qin2[:, sl * W1:(sl + 1) * W1], AF.Sign))
                        for sl, srcap, invc, dstd, isv, dst in prs:
                            act.inc(s.activation(qu_s[:, sl * W1:(sl + 1) * W1],
                                                 qin2[:, sl * W1:(sl + 1) * W1], AF.Abs))
                            src_done[sl] = act.v
                    a1 = act.v

                    @block.vector
                    def _(v, prs=prs, a1=a1):
                        v.wait_ge(_act, a1)
                        for sl, srcap, invc, dstd, isv, dst in prs:
                            dve.inc(v.tensor_scalar(qa_s[:, sl * W1:(sl + 1) * W1],
                                                    qu_s[:, sl * W1:(sl + 1) * W1],
                                                    scal_sb[:, invc:invc + 1], 2.0 ** -20,
                                                    ALU.mult, ALU.max))
                    d1 = dve.v

                    @block.scalar
                    def _(s, prs=prs, d1=d1):
                        s.wait_ge(_dve, d1)
                        for sl, srcap, invc, dstd, isv, dst in prs:
                            act.inc(s.activation(qu_s[:, sl * W1:(sl + 1) * W1],
                                                 qa_s[:, sl * W1:(sl + 1) * W1], AF.Ln))
                    a2 = act.v

                    @block.vector
                    def _(v, prs=prs, a2=a2):
                        v.wait_ge(_act, a2)
                        for sl, srcap, invc, dstd, isv, dst in prs:
                            dve.inc(v.tensor_scalar(qti_s[:, sl * W1:(sl + 1) * W1],
                                                    qu_s[:, sl * W1:(sl + 1) * W1],
                                                    1.0 / LN2C, -0.49995, ALU.mult, ALU.add))
                        v.wait_ge(_dve, dve.v)
                        for sl, srcap, invc, dstd, isv, dst in prs:
                            dve.inc(v.tensor_scalar(qu_s[:, sl * W1:(sl + 1) * W1],
                                                    qti_s[:, sl * W1:(sl + 1) * W1],
                                                    -3.0, -9.0, ALU.add, ALU.max))
                    d2 = dve.v

                    @block.scalar
                    def _(s, prs=prs, d2=d2):
                        s.wait_ge(_dve, d2)
                        for sl, srcap, invc, dstd, isv, dst in prs:
                            act.inc(s.activation(qw_s[:, sl * W1:(sl + 1) * W1],
                                                 qu_s[:, sl * W1:(sl + 1) * W1], AF.Exp, scale=-LN2C))
                    a3 = act.v

                    @block.vector
                    def _(v, prs=prs, a3=a3):
                        v.wait_ge(_act, a3)
                        for sl, srcap, invc, dstd, isv, dst in prs:
                            dve.inc(v.scalar_tensor_tensor(qr_s[:, sl * W1:(sl + 1) * W1],
                                                           qa_s[:, sl * W1:(sl + 1) * W1], 1.0,
                                                           qw_s[:, sl * W1:(sl + 1) * W1],
                                                           ALU.mult, ALU.mult))
                        v.wait_ge(_dve, dve.v)
                        for sl, srcap, invc, dstd, isv, dst in prs:
                            dve.inc(v.tensor_copy(qti_s[:, sl * W1:(sl + 1) * W1],
                                                  qr_s[:, sl * W1:(sl + 1) * W1]))
                    d3 = dve.v

                    @block.scalar
                    def _(s, prs=prs, d3=d3):
                        s.wait_ge(_dve, d3)
                        for sl, srcap, invc, dstd, isv, dst in prs:
                            act.inc(s.activation(qw_s[:, sl * W1:(sl + 1) * W1],
                                                 qu_s[:, sl * W1:(sl + 1) * W1], AF.Exp,
                                                 scale=LN2C))
                    a4 = act.v

                    @block.vector
                    def _(v, prs=prs, a4=a4):
                        v.wait_ge(_act, a4)
                        for sl, srcap, invc, dstd, isv, dst in prs:
                            if dstd is not None:
                                v.wait_ge(_gd, st_done[sl])
                        for sl, srcap, invc, dstd, isv, dst in prs:
                            dve.inc(v.tensor_tensor(qr_s[:, sl * W1:(sl + 1) * W1],
                                                    qti_s[:, sl * W1:(sl + 1) * W1],
                                                    qw_s[:, sl * W1:(sl + 1) * W1], ALU.mult))
                        v.wait_ge(_dve, dve.v)
                        for sl, srcap, invc, dstd, isv, dst in prs:
                            dve.inc(v.tensor_tensor(dst, qr_s[:, sl * W1:(sl + 1) * W1],
                                                    qsg_s[:, sl * W1:(sl + 1) * W1], ALU.mult))
                            d4_done[sl] = dve.v

                    @block.gpsimd
                    def _(g, prs=prs, dvev=dve.v):
                        g.wait_ge(_dve, dvev)
                        for sl, srcap, invc, dstd, isv, dst in prs:
                            if dstd is not None:
                                gd.inc(g.dma_start(out=dstd, in_=q8st[:, sl * W1:(sl + 1) * W1]), 16)
                                st_done[sl] = gd.v
                stq_ld, stq_gd = ld.v, gd.v
                stq_dve, stq_act = dve.v, act.v

            _stF = contextlib.ExitStack()
            with _stF:
                k8sl = _stF.enter_context(nc.sbuf_tensor("k8sl", [128, 4 * S], BF16))
                v8sl = _stF.enter_context(nc.sbuf_tensor("v8sl", [128, 4 * S], BF16))
                ptil = _stF.enter_context(nc.sbuf_tensor("ptil", [128, 2 * 2048], BF16))
                rec_sb = _stF.enter_context(nc.sbuf_tensor("rec_sb", [1, 512], F32))
                recb_sb = _stF.enter_context(nc.sbuf_tensor("recb_sb", [1, 512], BF16))
                bcast_sb = _stF.enter_context(nc.sbuf_tensor("bcast_sb", [128, 512], BF16))
                q8_all = _stF.enter_context(nc.sbuf_tensor("q8_all", [128, H * ROWS], BF16))

                @block.sync
                def _(sy):
                    sy.wait_ge(_ld, stq_ld)
                    sy.wait_ge(_gd, stq_gd)
                    sy.wait_ge(_dve, stq_dve)
                    sy.wait_ge(_act, stq_act)
                    for h in range(H):
                        ld.inc(sy.dma_start(out=q8_all[:, h * ROWS:(h + 1) * ROWS], in_=q8_dram[h]), 16)

                nch = S // 128
                ngr = nch // 4
                fl_pe, fl_ld = [], []
                for h in range(H):
                    hsl = h % 4

                    @block.sync
                    def _(sy, h=h, hsl=hsl):
                        if h >= 4:
                            sy.wait_ge(_pe, fl_pe[h - 4])
                        ld.inc(sy.dma_start(out=k8sl[:, hsl * S:(hsl + 1) * S], in_=k8_dram[h]), 16)
                        ld.inc(sy.dma_start(out=v8sl[:, hsl * S:(hsl + 1) * S], in_=v8_dram[h]), 16)
                    fl_ld.append(ld.v)

                    v3 = v8sl[:, hsl * S:(hsl + 1) * S].rearrange("p (c h2) -> p c h2", c=nch)
                    for qb in range(2):
                        s_done, e_done, p_done = [], [], []
                        for g in range(ngr):
                            @block.tensor
                            def _(tensor, g=g, qb=qb, h=h, hsl=hsl):
                                if g == 0:
                                    tensor.wait_ge(_dve, dve.v)
                                    tensor.wait_ge(_ld, fl_ld[h])
                                else:
                                    tensor.wait_ge(_act, e_done[g - 1])
                                for c in range(4):
                                    idx = 4 * g + c
                                    pe.inc(tensor.matmul(ps_abcd[:, c * 512:(c + 1) * 512],
                                                         k8sl[:, hsl * S + idx * 128:hsl * S + (idx + 1) * 128],
                                                         q8_all[:, h * ROWS + qb * 512:h * ROWS + qb * 512 + 512],
                                                         start=True, stop=True))
                                s_done.append(pe.v)

                            @block.scalar
                            def _(scalar, g=g):
                                scalar.wait_ge(_pe, s_done[g])
                                if g >= 2:
                                    scalar.wait_ge(_pe, p_done[g - 2])
                                act.inc(scalar.activation(ptil[:, (g % 2) * 2048:(g % 2) * 2048 + 2048],
                                                          ps_abcd[:, 0:2048],
                                                          AF.Exp, scale=scal_sb[:, 3:4]))
                                e_done.append(act.v)

                            @block.tensor
                            def _(tensor, g=g):
                                tensor.wait_ge(_act, e_done[g])
                                if g == 0:
                                    tensor.wait_ge(_dve, dve.v)
                                for c in range(4):
                                    idx = 4 * g + c
                                    sl2 = ptil[:, (g % 2) * 2048 + c * 512:(g % 2) * 2048 + (c + 1) * 512]
                                    pe.inc(tensor.matmul(ps_sum[:], ones_sb[:], sl2,
                                                         start=(g == 0 and c == 0), stop=(g == ngr - 1 and c == 3)))
                                    pe.inc(tensor.matmul(ps_e[:], v3[:, idx], sl2,
                                                         start=(g == 0 and c == 0), stop=(g == ngr - 1 and c == 3)))
                                p_done.append(pe.v)

                        @block.vector
                        def _(v):
                            v.wait_ge(_pe, pe.v)
                            dve.inc(v.reciprocal(rec_sb[:], ps_sum[:]))
                            v.wait_ge(_dve, dve.v)
                            dve.inc(v.tensor_scalar(rec_sb[:], rec_sb[:], scal_sb[0:1, 4:5], None, ALU.mult))
                            v.wait_ge(_dve, dve.v)
                            dve.inc(v.tensor_copy(recb_sb[:], rec_sb[:]))

                        @block.tensor
                        def _(t):
                            t.wait_ge(_dve, dve.v)
                            pe.inc(t.matmul(ps_f[:], ones1r_sb[:], recb_sb[:], start=True, stop=True))

                        @block.vector
                        def _(v, h=h, qb=qb):
                            v.wait_ge(_pe, pe.v)
                            dve.inc(v.tensor_copy(bcast_sb[:], ps_f[:]))
                            v.wait_ge(_dve, dve.v)
                            dve.inc(v.tensor_tensor(ctx_all[:, h * ROWS + qb * 512:h * ROWS + qb * 512 + 512],
                                                    ps_e[:], bcast_sb[:], ALU.mult))
                    fl_pe.append(pe.v)
            b2_dve, b2_pe = dve.v, pe.v
            if dbg:
                @block.sync
                def _(sy):
                    sy.wait_ge(_dve, dve.v)
                    sy.wait_ge(_gd, gd.v)
                    ld.inc(sy.dma_start(out=dbg_ctx, in_=ctx_all[:]), 16)
                    ld.inc(sy.dma_start(out=dbg_scal, in_=scal_sb[:]), 16)
                    for h in range(H):
                        ld.inc(sy.dma_start(out=dbg_k8[h], in_=k8_dram[h]), 16)
                        ld.inc(sy.dma_start(out=dbg_q8[h], in_=q8_dram[h]), 16)

            with (
                nc.sbuf_tensor("wo_sb", [128, DC * 128], BF16) as wo_sb,
                nc.sbuf_tensor("bo_sb", [128, DC], F32) as bo_sb,
                nc.sbuf_tensor("xpreH", [128, DC * 512], F32) as xpreH,
            ):
                @block.sync
                def _(sy):
                    sy.wait_ge(_dve, b2_dve)
                    ld.inc(sy.dma_start(out=bo_sb[:], in_=b_out), 16)

                @block.vector
                def _(v):
                    v.wait_ge(_ld, ld.v)
                    dve.inc(v.memset(mxt_sb[:], 0.0))

                Wo_r = W_out.rearrange("(c p) f -> c p f", p=128)
                xBi_r = xBi.rearrange("(c p) s -> c p s", p=128)
                for hb in range(2):
                    evac_dve = []
                    for oc in range(DC):
                        @block.gpsimd
                        def _(g, oc=oc, hb=hb):
                            g.wait_ge(_pe, pe.v)
                            g.wait_ge(_dve, dve.v)
                            for c in range(DC):
                                gd.inc(g.dma_start(out=wo_sb[:, c * 128:(c + 1) * 128], in_=Wo_r[c, :, oc * 128:(oc + 1) * 128]), 16)
                            gd.inc(g.dma_start(out=evtmp2[:], in_=xBi_r[oc][:, hb * 512:hb * 512 + 512]), 16)

                        @block.tensor
                        def _(tensor, oc=oc, hb=hb):
                            tensor.wait_ge(_gd, gd.v)
                            if oc == 0 and hb == 0:
                                tensor.wait_ge(_dve, b2_dve)
                            if oc >= 2:
                                tensor.wait_ge(_dve, evac_dve[oc - 2])
                            bank = ps_a if oc % 2 == 0 else ps_b
                            for ic in range(DC):
                                pe.inc(tensor.matmul(bank[:], wo_sb[:, ic * 128:(ic + 1) * 128],
                                                     ctx_all[:, ic * ROWS + hb * 512:ic * ROWS + hb * 512 + 512],
                                                     start=(ic == 0), stop=(ic == DC - 1)))

                        @block.vector
                        def _(v, oc=oc, hb=hb):
                            v.wait_ge(_pe, pe.v)
                            v.wait_ge(_gd, gd.v)
                            bank = ps_a if oc % 2 == 0 else ps_b
                            dve.inc(v.tensor_scalar(evtmp[:], bank[:], bo_sb[:, oc:oc + 1], None, ALU.add))
                            v.wait_ge(_dve, dve.v)
                            dve.inc(v.tensor_tensor(xpreH[:, oc * 512:oc * 512 + 512], evtmp[:], evtmp2[:], ALU.add))
                            evac_dve.append(dve.v)

                    emit_ln_stats_and_bcast(lambda c: xpreH[:, c * 512:c * 512 + 512])

                    for oc in range(DC):
                        @block.vector
                        def _(v, oc=oc, hb=hb):
                            v.wait_ge(_dve, dve.v)
                            v.wait_ge(_ld, ld.v)
                            dve.inc(v.tensor_tensor(evtmp[:], xpreH[:, oc * 512:oc * 512 + 512], lnA_sb[:], ALU.mult))
                            v.wait_ge(_dve, dve.v)
                            dve.inc(v.tensor_tensor(evtmp[:], evtmp[:], lnB_sb[:], ALU.subtract))
                        @block.scalar
                        def _(s, oc=oc):
                            s.wait_ge(_dve, dve.v)
                            act.inc(s.activation(evtmp2[:], evtmp[:], AF.Abs))

                        @block.vector
                        def _(v, oc=oc, hb=hb):
                            v.wait_ge(_act, act.v)
                            dve.inc(v.tensor_tensor(mxt_sb[:, hb * 512:hb * 512 + 512],
                                                    mxt_sb[:, hb * 512:hb * 512 + 512], evtmp2[:], ALU.max))

                        @block.sync
                        def _(sy, oc=oc, hb=hb):
                            sy.wait_ge(_dve, dve.v)
                            ld.inc(sy.dma_start(out=xC_dram[oc][:, hb * 512:hb * 512 + 512], in_=evtmp[:]), 16)
            b3_ld, b3_dve = ld.v, dve.v

        _st4 = contextlib.ExitStack()
        with _st4:
            xi_sb = _st4.enter_context(nc.sbuf_tensor("xi_sb", [128, DC * 512], BF16))
            h2_sb = _st4.enter_context(nc.sbuf_tensor("h2_sb", [128, IC * 512], BF16))
            wTg_sb = _st4.enter_context(nc.sbuf_tensor("wTg_sb", [128, DC * 512], BF16))
            wTv_sb = _st4.enter_context(nc.sbuf_tensor("wTv_sb", [128, DC * 512], BF16))
            gb_sb = _st4.enter_context(nc.sbuf_tensor("gb_sb", [128, IC], F32))
            vb_sb = _st4.enter_context(nc.sbuf_tensor("vb_sb", [128, IC], F32))
            ob_sb = _st4.enter_context(nc.sbuf_tensor("ob_sb", [128, DC], F32))
            gtmp = _st4.enter_context(nc.sbuf_tensor("gtmp", [128, 512], F32))
            vtmp = _st4.enter_context(nc.sbuf_tensor("vtmp", [128, 512], F32))
            gbf = _st4.enter_context(nc.sbuf_tensor("gbf", [128, 512], BF16))
            mxh_sb = _st4.enter_context(nc.sbuf_tensor("mxh_sb", [128, 512], F32))
            qtii = _st4.enter_context(nc.sbuf_tensor("qtii", [128, 512], I32))
            xcin = _st4.enter_context(nc.sbuf_tensor("xcin", [128, 512], F32))
            xpre2 = _st4.enter_context(nc.sbuf_tensor("xpre2", [128, DC * 512], F32))
            @block.sync
            def _(sy):
                ld.inc(sy.dma_start(out=gb_sb[:], in_=gate_b), 16)
                ld.inc(sy.dma_start(out=vb_sb[:], in_=val_b), 16)
                ld.inc(sy.dma_start(out=ob_sb[:], in_=out_b), 16)

            for tb in range(ROWS // 128):
                @block.tensor
                def _(t, tb=tb):
                    t.wait_ge(_dve, b3_dve if tb == 0 else dve.v)
                    pe.inc(t.transpose(ps_d[:, 0:128], mxt_sb[:, tb * 128:(tb + 1) * 128], iden_sb[:]))

                @block.vector
                def _(v, tb=tb):
                    v.wait_ge(_pe, pe.v)
                    dve.inc(v.tensor_reduce(colred_sb[:, tb:tb + 1], ps_d[:, 0:128], axis=AX.X, op=ALU.max))

            gx_v = gx_dram.rearrange("a (c p) -> a c p", p=128)

            @block.sync
            def _(sy):
                sy.wait_ge(_dve, dve.v)
                for tb in range(ROWS // 128):
                    ld.inc(sy.dma_start(out=gx_v[0, tb], in_=colred_sb[:, tb:tb + 1]), 16)
                sy.wait_ge(_ld, ld.v)
                ld.inc(sy.dma_start(out=rowt_sb[:], in_=gx_dram), 16)

            def emit_rowmath_and_bcast(dst_pair, mulcol, postmul127):
                @block.vector
                def _(v):
                    v.wait_ge(_ld, ld.v)
                    v.wait_ge(_pe, pe.v)
                    if mulcol is None:
                        dve.inc(v.tensor_scalar(rowt2_sb[:], rowt_sb[:], 1e-5, None, ALU.max))
                        v.wait_ge(_dve, dve.v)
                        dve.inc(v.tensor_copy(rowt_sb[:], rowt2_sb[:]))
                        v.wait_ge(_dve, dve.v)
                        dve.inc(v.reciprocal(rowt2_sb[:], rowt2_sb[:]))
                        v.wait_ge(_dve, dve.v)
                        dve.inc(v.tensor_scalar(rowt2_sb[:], rowt2_sb[:], 127.0, None, ALU.mult))
                    else:
                        dve.inc(v.tensor_scalar(rowt2_sb[:], rowt_sb[:], 1.0 / 127.0, None, ALU.mult))
                        v.wait_ge(_dve, dve.v)
                        dve.inc(v.tensor_scalar(rowt2_sb[:], rowt2_sb[:], scal_sb[0:1, mulcol:mulcol + 1], None, ALU.mult))
                    v.wait_ge(_dve, dve.v)
                    dve.inc(v.tensor_copy(rowb_sb[:], rowt2_sb[:]))

                @block.tensor
                def _(t):
                    t.wait_ge(_dve, dve.v)
                    pe.inc(t.matmul(ps_a[:], ones1r_sb[:], rowb_sb[0:1, 0:512], start=True, stop=True))
                    pe.inc(t.matmul(ps_b[:], ones1r_sb[:], rowb_sb[0:1, 512:1024], start=True, stop=True))

                @block.vector
                def _(v):
                    v.wait_ge(_pe, pe.v)
                    v.tensor_copy(dst_pair[:, 0:512], ps_a[:])
                    dve.inc(v.tensor_copy(dst_pair[:, 512:1024], ps_b[:]))

            emit_rowmath_and_bcast(scxb_sb, None, None)
            emit_rowmath_and_bcast(deqg_sb, 12, None)
            emit_rowmath_and_bcast(deqv_sb, 13, None)

            Wg_r = gate_w.rearrange("(m p) d -> m p d", p=128)
            Wv2_r = val_w.rearrange("(m p) d -> m p d", p=128)
            Wo2_r = out_w.rearrange("(m p) d -> m p d", p=128)
            gh_v = gh_dram.rearrange("a (c p) -> a c p", p=128)
            wgt_r = wgt_dram.rearrange("(m p) d -> m p d", p=128)
            wvt_r = wvt_dram.rearrange("(m p) d -> m p d", p=128)
            wot_r = wot_dram.rearrange("(m p) d -> m p d", p=128)

            wnat_al = xpre2[:, 0:D]
            s1_al = h2_sb[:, 0:D]
            s2_al = h2_sb[:, D:2 * D]
            tern_jobs = []
            for m in range(IC):
                tern_jobs.append((Wg_r[m], 6, 5, wgt_r[m]))
            for m in range(IC):
                tern_jobs.append((Wv2_r[m], 8, 7, wvt_r[m]))
            for m in range(DC):
                for s4 in range(4):
                    tern_jobs.append((Wo2_r[m, :, s4 * D:(s4 + 1) * D], 10, 9,
                                      wot_r[m * 4 + s4] if False else wot_dram.rearrange("o (s d) -> o s d", s=4)[m * 128:(m + 1) * 128, s4]))
            tj_s2, tj_ad, tj_st = [], [], []
            for j, (srcap, bneg, bpos, dstap) in enumerate(tern_jobs):
                sl = j % 2
                wslot = xpre2[:, sl * D:(sl + 1) * D]
                stg = wTg_sb[:, 0:D] if sl == 0 else wTv_sb[:, 0:D]

                @block.gpsimd
                def _(g, j=j, srcap=srcap, wslot=wslot, dvev=dve.v):
                    if j <= 1:
                        g.wait_ge(_dve, dvev)
                    else:
                        g.wait_ge(_dve, tj_s2[j - 2])
                    gd.inc(g.dma_start(out=wslot, in_=srcap), 16)

                @block.vector
                def _(v, j=j, bneg=bneg, bpos=bpos, wslot=wslot, stg=stg):
                    v.wait_ge(_gd, gd.v)
                    if j >= 2:
                        v.wait_ge(_ld, tj_st[j - 2])
                    dve.inc(v.tensor_scalar(s1_al, wslot, scal_sb[:, bpos:bpos + 1], None, ALU.is_gt))
                    dve.inc(v.tensor_scalar(s2_al, wslot, scal_sb[:, bneg:bneg + 1], None, ALU.is_lt))
                    tj_s2.append(dve.v)
                    v.wait_ge(_dve, dve.v)
                    dve.inc(v.tensor_tensor(stg, s1_al, s2_al, ALU.subtract))
                    tj_ad.append(dve.v)

                @block.sync
                def _(sy, j=j, dstap=dstap, stg=stg):
                    sy.wait_ge(_dve, tj_ad[j])
                    ld.inc(sy.dma_start(out=dstap, in_=stg), 16)
                    tj_st.append(ld.v)
            tern_ld = ld.v

            for hb in range(2):
                for oc in range(DC):
                    @block.gpsimd
                    def _(g, oc=oc, hb=hb):
                        g.wait_ge(_ld, b3_ld)
                        g.wait_ge(_dve, dve.v)
                        gd.inc(g.dma_start(out=xcin[:], in_=xC_dram[oc][:, hb * 512:hb * 512 + 512]), 16)

                    @block.vector
                    def _(v, oc=oc, hb=hb):
                        v.wait_ge(_gd, gd.v)
                        dve.inc(v.tensor_tensor(gtmp[:], xcin[:], scxb_sb[:, hb * 512:hb * 512 + 512], ALU.mult))
                        v.wait_ge(_dve, dve.v)
                        dve.inc(v.tensor_copy(qtii[:], gtmp[:]))
                        v.wait_ge(_dve, dve.v)
                        dve.inc(v.tensor_scalar(xi_sb[:, oc * 512:oc * 512 + 512], qtii[:], -128.0, 127.0, ALU.max, ALU.min))

                @block.vector
                def _(v):
                    v.wait_ge(_dve, dve.v)
                    dve.inc(v.memset(mxh_sb[:], 0.0))
                xi_dve = dve.v

                for hb4 in range(IC // 4):
                    @block.sync
                    def _(sy, hb4=hb4, first=(hb4 == 0)):
                        sy.wait_ge(_pe, pe.v)
                        if first:
                            sy.wait_ge(_ld, tern_ld)
                        for k in range(DC):
                            ld.inc(sy.dma_start(out=wTg_sb[:, k * 512:(k + 1) * 512],
                                                in_=wgt_dram[hb4 * 512:(hb4 + 1) * 512, k * 128:(k + 1) * 128], transpose=True), 16)
                        for k in range(DC):
                            ld.inc(sy.dma_start(out=wTv_sb[:, k * 512:(k + 1) * 512],
                                                in_=wvt_dram[hb4 * 512:(hb4 + 1) * 512, k * 128:(k + 1) * 128], transpose=True), 16)

                    for mi in range(4):
                        m = hb4 * 4 + mi

                        @block.tensor
                        def _(t, m=m, mi=mi):
                            t.wait_ge(_ld, ld.v)
                            if m == 0:
                                t.wait_ge(_dve, xi_dve)
                            else:
                                t.wait_ge(_dve, dve.v)
                            for k in range(DC):
                                pe.inc(t.matmul(ps_a[:], wTg_sb[:, k * 512 + mi * 128:k * 512 + (mi + 1) * 128],
                                                xi_sb[:, k * 512:k * 512 + 512], start=(k == 0), stop=(k == DC - 1)))
                            for k in range(DC):
                                pe.inc(t.matmul(ps_b[:], wTv_sb[:, k * 512 + mi * 128:k * 512 + (mi + 1) * 128],
                                                xi_sb[:, k * 512:k * 512 + 512], start=(k == 0), stop=(k == DC - 1)))

                        @block.vector
                        def _(v, m=m, hb=hb):
                            v.wait_ge(_pe, pe.v)
                            dve.inc(v.tensor_tensor(gtmp[:], ps_a[:], deqg_sb[:, hb * 512:hb * 512 + 512], ALU.mult))

                        @block.scalar
                        def _(s, m=m):
                            s.wait_ge(_dve, dve.v)
                            act.inc(s.activation(gbf[:], gtmp[:], AF.Silu, bias=gb_sb[:, m:m + 1]))

                        @block.vector
                        def _(v, m=m, hb=hb):
                            dve.inc(v.tensor_tensor(vtmp[:], ps_b[:], deqv_sb[:, hb * 512:hb * 512 + 512], ALU.mult))
                            v.wait_ge(_dve, dve.v)
                            dve.inc(v.tensor_scalar(vtmp[:], vtmp[:], vb_sb[:, m:m + 1], None, ALU.add))
                            v.wait_ge(_act, act.v)
                            v.wait_ge(_dve, dve.v)
                            dve.inc(v.tensor_tensor(h2_sb[:, m * 512:m * 512 + 512], gbf[:], vtmp[:], ALU.mult))
                            v.wait_ge(_dve, dve.v)

                        @block.scalar
                        def _(s, m=m):
                            s.wait_ge(_dve, dve.v)
                            act.inc(s.activation(vtmp[:], h2_sb[:, m * 512:m * 512 + 512], AF.Abs))

                        @block.vector
                        def _(v, m=m):
                            v.wait_ge(_act, act.v)
                            dve.inc(v.tensor_tensor(mxh_sb[:], mxh_sb[:], vtmp[:], ALU.max))
                            v.wait_ge(_dve, dve.v)

                for tb in range(4):
                    @block.tensor
                    def _(t, tb=tb):
                        t.wait_ge(_dve, dve.v)
                        pe.inc(t.transpose(ps_d[:, 0:128], mxh_sb[:, tb * 128:(tb + 1) * 128], iden_sb[:]))

                    @block.vector
                    def _(v, tb=tb):
                        v.wait_ge(_pe, pe.v)
                        dve.inc(v.tensor_reduce(colred_sb[:, tb:tb + 1], ps_d[:, 0:128], axis=AX.X, op=ALU.max))

                @block.sync
                def _(sy, hb=hb):
                    sy.wait_ge(_dve, dve.v)
                    for tb in range(4):
                        ld.inc(sy.dma_start(out=gh_v[0, 4 * hb + tb], in_=colred_sb[:, tb:tb + 1]), 16)
                    sy.wait_ge(_ld, ld.v)
                    ld.inc(sy.dma_start(out=rowt_sb[0:1, 0:512], in_=gh_dram[:, hb * 512:hb * 512 + 512]), 16)

                @block.vector
                def _(v):
                    v.wait_ge(_ld, ld.v)
                    v.wait_ge(_pe, pe.v)
                    dve.inc(v.tensor_scalar(rowt2_sb[0:1, 0:512], rowt_sb[0:1, 0:512], 1e-5, None, ALU.max))
                    v.wait_ge(_dve, dve.v)
                    dve.inc(v.tensor_copy(rowt_sb[0:1, 0:512], rowt2_sb[0:1, 0:512]))
                    v.wait_ge(_dve, dve.v)
                    dve.inc(v.reciprocal(rowt2_sb[0:1, 0:512], rowt2_sb[0:1, 0:512]))
                    v.wait_ge(_dve, dve.v)
                    dve.inc(v.tensor_scalar(rowt2_sb[0:1, 0:512], rowt2_sb[0:1, 0:512], 127.0, None, ALU.mult))
                    v.wait_ge(_dve, dve.v)
                    dve.inc(v.tensor_copy(rowb_sb[0:1, 0:512], rowt2_sb[0:1, 0:512]))

                @block.tensor
                def _(t):
                    t.wait_ge(_dve, dve.v)
                    pe.inc(t.matmul(ps_a[:], ones1r_sb[:], rowb_sb[0:1, 0:512], start=True, stop=True))

                @block.vector
                def _(v):
                    v.wait_ge(_pe, pe.v)
                    dve.inc(v.tensor_copy(schb_sb[:], ps_a[:]))
                    v.wait_ge(_dve, dve.v)
                    dve.inc(v.tensor_scalar(rowt2_sb[0:1, 0:512], rowt_sb[0:1, 0:512], 1.0 / 127.0, None, ALU.mult))
                    v.wait_ge(_dve, dve.v)
                    dve.inc(v.tensor_scalar(rowt2_sb[0:1, 0:512], rowt2_sb[0:1, 0:512], scal_sb[0:1, 14:15], None, ALU.mult))
                    v.wait_ge(_dve, dve.v)
                    dve.inc(v.tensor_copy(rowb_sb[0:1, 0:512], rowt2_sb[0:1, 0:512]))

                @block.tensor
                def _(t):
                    t.wait_ge(_dve, dve.v)
                    pe.inc(t.matmul(ps_a[:], ones1r_sb[:], rowb_sb[0:1, 0:512], start=True, stop=True))

                @block.vector
                def _(v):
                    v.wait_ge(_pe, pe.v)
                    dve.inc(v.tensor_copy(deqh_sb[:], ps_a[:]))

                for m in range(IC):
                    @block.vector
                    def _(v, m=m):
                        v.wait_ge(_dve, dve.v)
                        dve.inc(v.tensor_tensor(gtmp[:], h2_sb[:, m * 512:m * 512 + 512], schb_sb[:], ALU.mult))
                        v.wait_ge(_dve, dve.v)
                        dve.inc(v.tensor_copy(qtii[:], gtmp[:]))
                        v.wait_ge(_dve, dve.v)
                        dve.inc(v.tensor_scalar(h2_sb[:, m * 512:m * 512 + 512], qtii[:], -128.0, 127.0, ALU.max, ALU.min))

                h2q_dve = dve.v
                for ob in range(4):
                    obank = [ps_a, ps_b, ps_c, ps_d]
                    for fg in range(4):
                        @block.sync
                        def _(sy, ob=ob, fg=fg):
                            sy.wait_ge(_pe, pe.v)
                            for k in range(DC):
                                fk = fg * DC + k
                                ld.inc(sy.dma_start(out=wTg_sb[:, k * 512:(k + 1) * 512],
                                                    in_=wot_dram[ob * 512:(ob + 1) * 512, fk * 128:(fk + 1) * 128], transpose=True), 16)

                        for oi in range(4):
                            oc = ob * 4 + oi

                            @block.tensor
                            def _(t, ob=ob, fg=fg, oi=oi, oc=oc):
                                t.wait_ge(_ld, ld.v)
                                if ob == 0 and fg == 0 and oi == 0:
                                    t.wait_ge(_dve, h2q_dve)
                                if fg == 0 and oi == 0 and ob > 0:
                                    t.wait_ge(_dve, dve.v)
                                bank = obank[oi]
                                for k in range(DC):
                                    fk = fg * DC + k
                                    pe.inc(t.matmul(bank[:], wTg_sb[:, k * 512 + oi * 128:k * 512 + (oi + 1) * 128],
                                                    h2_sb[:, fk * 512:fk * 512 + 512], start=(fg == 0 and k == 0), stop=(fg == 3 and k == DC - 1)))

                    for oi in range(4):
                        oc = ob * 4 + oi

                        @block.gpsimd
                        def _(g, oc=oc, hb=hb):
                            g.wait_ge(_dve, dve.v)
                            gd.inc(g.dma_start(out=xcin[:], in_=xC_dram[oc][:, hb * 512:hb * 512 + 512]), 16)

                        @block.vector
                        def _(v, oc=oc, oi=oi):
                            v.wait_ge(_pe, pe.v)
                            v.wait_ge(_gd, gd.v)
                            bank = obank[oi]
                            dve.inc(v.tensor_tensor(gtmp[:], bank[:], deqh_sb[:], ALU.mult))
                            v.wait_ge(_dve, dve.v)
                            dve.inc(v.tensor_scalar(gtmp[:], gtmp[:], ob_sb[:, oc:oc + 1], None, ALU.add))
                            v.wait_ge(_dve, dve.v)
                            dve.inc(v.tensor_tensor(xpre2[:, oc * 512:oc * 512 + 512], gtmp[:], xcin[:], ALU.add))

                emit_ln_stats_and_bcast(lambda c: xpre2[:, c * 512:c * 512 + 512])

                for oc in range(DC):
                    @block.vector
                    def _(v, oc=oc):
                        v.wait_ge(_dve, dve.v)
                        v.wait_ge(_ld, ld.v)
                        dve.inc(v.tensor_tensor(evtmp[:], xpre2[:, oc * 512:oc * 512 + 512], lnA_sb[:], ALU.mult))
                        v.wait_ge(_dve, dve.v)
                        dve.inc(v.tensor_tensor(evtmp[:], evtmp[:], lnB_sb[:], ALU.subtract))

                    @block.sync
                    def _(sy, oc=oc, hb=hb):
                        sy.wait_ge(_dve, dve.v)
                        ld.inc(sy.dma_start(out=y_out[oc][:, hb * 512:hb * 512 + 512], in_=evtmp[:]), 16)

        if dbg:
            @block.sync
            def _(sy):
                sy.wait_ge(_dve, dve.v)
                sy.wait_ge(_pe, pe.v)
                for c in range(DC):
                    ld.inc(sy.dma_start(out=dbg_xc[c], in_=xC_dram[c]), 16)

        @block.gpsimd
        def _(g):
            g.wait_ge(_ld, ld.v)
            g.wait_ge(_gd, gd.v)
            g.wait_ge(_act, act.v)
            g.wait_ge(_pe, pe.v)
            g.wait_ge(_dve, dve.v)

    return nc



def prep_B_inputs(inputs, resA, core):
    grp = core // 4
    gcores = range(grp * 4, grp * 4 + 4)
    qkv = {c: np.asarray(resA[c]["qkv_out"]).reshape(3 * D, ROWS) for c in gcores}
    kfull = np.concatenate([qkv[c][D:2 * D, :] for c in gcores], axis=1)
    vfull = np.concatenate([qkv[c][2 * D:3 * D, :] for c in gcores], axis=1).T
    parts = np.stack([np.asarray(resA[c]["partials"])[0] for c in range(NCORE)])
    prep_rep = np.tile(parts.reshape(-1), (128, 1)).astype(np.float32)
    f32 = lambda a: np.ascontiguousarray(np.asarray(a, dtype=np.float32))
    chunk_vec = lambda v, n: f32(np.asarray(v).reshape(n, 128).T)
    return {
        "qfm": f32(qkv[core][0:D, :]),
        "kfm": f32(kfull),
        "vtm": f32(vfull),
        "xBi": f32(np.asarray(resA[core]["xB_out"]).reshape(D, ROWS)),
        "prep": f32(prep_rep),
        "W_out": f32(inputs["L1_attn_W_out"]),
        "b_out": chunk_vec(inputs["L1_attn_b_out"], DC),
        "gate_w": f32(inputs["L1_ffn_gate_w"]),
        "val_w": f32(inputs["L1_ffn_val_w"]),
        "out_w": f32(inputs["L1_ffn_out_w"]),
        "gate_b": chunk_vec(inputs["L1_ffn_gate_b"], IC),
        "val_b": chunk_vec(inputs["L1_ffn_val_b"], IC),
        "out_b": chunk_vec(inputs["L1_ffn_out_b"], DC),
    }


def kernel(**inputs):
    ncA = build_A()
    in_maps_A = [prep_A_inputs(inputs, c) for c in range(NCORE)]
    resA = run_bass_kernel_spmd(ncA, in_maps_A, core_ids=list(range(NCORE))).results

    ncB = build_B()
    in_maps_B = [prep_B_inputs(inputs, resA, c) for c in range(NCORE)]
    resB = run_bass_kernel_spmd(ncB, in_maps_B, core_ids=list(range(NCORE))).results

    y = np.zeros((B, S, D), dtype=np.float32)
    for core in range(NCORE):
        grp, j = divmod(core, 4)
        yfm = np.asarray(resB[core]["y_out"]).reshape(D, ROWS)
        y[grp, j * ROWS:(j + 1) * ROWS, :] = yfm.T
    return y


# version 24 (final)
# speedup vs baseline: 1.0426x; 1.0426x over previous
import math
import numpy as np
import ml_dtypes
import concourse.bass as bass
import concourse.mybir as mybir
from concourse.bass_utils import run_bass_kernel_spmd

F32 = mybir.dt.float32
BF16 = mybir.dt.bfloat16
I32 = mybir.dt.int32
AF = mybir.ActivationFunctionType
ALU = mybir.AluOpType
AX = mybir.AxisListType

B, S, D, H, HD, INTER = 2, 4096, 2048, 16, 128, 8192
NCORE, ROWS, SPAN, WIN = 8, 1024, 3072, 2048
NEG = -1.0e5
LN_EPS = 1e-5
DC, IC, QKVC, NSP = D // 128, INTER // 128, 3 * D // 128, SPAN // 128
INV_SQRT_HD = 1.0 / math.sqrt(128.0)
F2048 = 1.0 / (64.0 * math.sqrt(2048.0))
F8192 = 1.0 / (64.0 * math.sqrt(8192.0))
BF = ml_dtypes.bfloat16
WP_SCALAR_TR = True


def _hadamard(n):
    Hm = np.array([[1.0]], dtype=np.float32)
    while Hm.shape[0] < n:
        Hm = np.block([[Hm, Hm], [Hm, -Hm]])
    return Hm


class Ctr:
    def __init__(self, sem):
        self.sem = sem
        self.v = 0

    def inc(self, ins, n=1):
        ins.then_inc(self.sem, n)
        self.v += n
        return self.v



def build_A():
    nc = bass.Bass("TRN2", target_bir_lowering=False)
    x0span = nc.dram_tensor("x0span", [SPAN, D], F32, kind="ExternalInput").ap()
    kvalid = nc.dram_tensor("kvalid", [128, NSP], F32, kind="ExternalInput").ap()
    Wq = nc.dram_tensor("Wq", [D, D], F32, kind="ExternalInput").ap()
    Wk = nc.dram_tensor("Wk", [D, D], F32, kind="ExternalInput").ap()
    Wv = nc.dram_tensor("Wv", [D, D], F32, kind="ExternalInput").ap()
    Wo = nc.dram_tensor("Wo", [D, D], F32, kind="ExternalInput").ap()
    SUug = nc.dram_tensor("SUug", [128, DC], F32, kind="ExternalInput").ap()
    SUdp = nc.dram_tensor("SUdp", [128, IC], F32, kind="ExternalInput").ap()
    Wsug = nc.dram_tensor("Wsug", [128, 2 * IC], F32, kind="ExternalInput").ap()
    Wsdp = nc.dram_tensor("Wsdp", [128, DC], F32, kind="ExternalInput").ap()
    Wup = nc.dram_tensor("Wup", [INTER, D], F32, kind="ExternalInput").ap()
    Wgate = nc.dram_tensor("Wgate", [INTER, D], F32, kind="ExternalInput").ap()
    Wdown = nc.dram_tensor("Wdown", [D, INTER], F32, kind="ExternalInput").ap()
    Wqkv = nc.dram_tensor("Wqkv", [D, 3 * D], F32, kind="ExternalInput").ap()
    gslg = nc.dram_tensor("gslg", [INTER // 8, D], F32, kind="ExternalInput").ap()
    gslv = nc.dram_tensor("gslv", [INTER // 8, D], F32, kind="ExternalInput").ap()
    gslo = nc.dram_tensor("gslo", [D // 8, INTER], F32, kind="ExternalInput").ap()

    xB_out = nc.dram_tensor("xB_out", [DC, 128, ROWS], F32, kind="ExternalOutput").ap()
    qkv_out = nc.dram_tensor("qkv_out", [QKVC, 128, ROWS], F32, kind="ExternalOutput").ap()
    partials = nc.dram_tensor("partials", [1, 8], F32, kind="ExternalOutput").ap()

    xA_dram = nc.dram_tensor("xA_dram", [DC, 128, ROWS], F32).ap()
    wupbf_dram = nc.dram_tensor("wupbf_dram", [INTER, D], BF16).ap()
    wgbf_dram = nc.dram_tensor("wgbf_dram", [INTER, D], BF16).ap()
    wdbf_dram = nc.dram_tensor("wdbf_dram", [D, INTER], BF16).ap()
    x0ob_dram = nc.dram_tensor("x0ob_dram", [DC, 128, ROWS], BF16).ap()
    xBb_dram = nc.dram_tensor("xBb_dram", [DC, 128, ROWS], BF16).ap()
    xsu_dram = nc.dram_tensor("xsu_dram", [DC, 128, ROWS], BF16).ap()

    c_iden = nc.inline_tensor(np.eye(128, dtype=np.float32), name="c_iden").ap()
    c_idenb = nc.inline_tensor(np.eye(128, dtype=np.float32).astype(BF), name="c_idenb").ap()
    c_h128 = nc.inline_tensor(_hadamard(128).astype(BF), name="c_h128").ap()
    c_ones = nc.inline_tensor(np.ones((128, 1), dtype=BF), name="c_ones").ap()
    c_ones1r = nc.inline_tensor(np.ones((1, 128), dtype=BF), name="c_ones1r").ap()
    c_ones1rw = nc.inline_tensor(np.ones((1, 512), dtype=BF), name="c_ones1rw").ap()
    p_ = np.arange(128)[:, None]
    q_ = np.arange(SPAN)[None, :]
    t_ = p_ - q_ + 2560
    c_T2 = nc.inline_tensor(np.where((t_ >= 1) & (t_ <= WIN), 0.0, NEG).astype(BF), name="c_T2").ap()

    import contextlib
    _stack = contextlib.ExitStack()
    with _stack:
        _sb = lambda *a: _stack.enter_context(nc.sbuf_tensor(*a))
        _ps = lambda *a: _stack.enter_context(nc.psum_tensor(*a))
        _sem = lambda n: _stack.enter_context(nc.semaphore(n))
        iden_sb = _sb("iden_sb", [128, 128], F32)
        idenb_sb = _sb("idenb_sb", [128, 128], BF16)
        h128_sb = _sb("h128_sb", [128, 128], BF16)
        ones_sb = _sb("ones_sb", [128, 1], BF16)
        ones1r_sb = _sb("ones1r_sb", [1, 128], BF16)
        ones1rw_sb = _sb("ones1rw_sb", [1, 512], BF16)
        su_ug_sb = _sb("su_ug_sb", [128, DC], F32)
        su_dp_sb = _sb("su_dp_sb", [128, IC], F32)
        wsug64_sb = _sb("wsug64_sb", [128, 2 * IC], F32)
        wsdp64_sb = _sb("wsdp64_sb", [128, DC], F32)
        eps_sb = _sb("eps_sb", [1, 1], F32)
        stat_sb = _sb("stat_sb", [1, 5 * 512], F32)
        statb_sb = _sb("statb_sb", [1, 2 * 512], BF16)
        lnA_sb = _sb("lnA_sb", [128, 512], BF16)
        lnB_sb = _sb("lnB_sb", [128, 512], BF16)
        amax_sb = _sb("amax_sb", [128, 4], F32)
        gsum_sb = _sb("gsum_sb", [128, 4], F32)
        red_sb = _sb("red_sb", [128, 1], F32)
        part_sb = _sb("part_sb", [1, 8], F32)
        ptr_sb = _sb("ptr_sb", [1, 128], F32)
        evtmp = _sb("evtmp", [128, 512], F32)
        evtmp2 = _sb("evtmp2", [128, 512], F32)
        xres_sb = _sb("xres_sb", [128, 512], BF16)
        sqb_sb = _sb("sqb_sb", [128, 512], BF16)
        xpb_sb = _sb("xpb_sb", [128, 512], BF16)
        ps_abcd = _ps("ps_abcd", [128, 2048], F32)
        ps_a = ps_abcd[:, 0:512]
        ps_b = ps_abcd[:, 512:1024]
        ps_c = ps_abcd[:, 1024:1536]
        ps_d = ps_abcd[:, 1536:2048]
        ps_e = _ps("ps_e", [128, 512], F32)
        ps_tb = _ps("ps_tb", [128, 128], BF16)
        ps_sum = _ps("ps_sum", [1, 512], F32)
        ps_st = _ps("ps_st", [1, 512], F32)
        _ld = _sem("m_ld")
        _gd = _sem("m_gd")
        _pe = _sem("m_pe")
        _act = _sem("m_act")
        _dve = _sem("m_dve")
        _vd = _sem("m_vd")
        _sl = [_sem(f"m_sl{i}") for i in range(10)]
        block = _stack.enter_context(nc.Block())
        ld, gd, pe, act, dve = Ctr(_ld), Ctr(_gd), Ctr(_pe), Ctr(_act), Ctr(_dve)
        vd = Ctr(_vd)
        slc = [Ctr(s) for s in _sl]

        def emit_ln_stats_and_bcast(get_chunk):
            pe0, act0, dve0 = pe.v, act.v, dve.v
            for c in range(DC):
                @block.vector
                def _(v, c=c):
                    if c > 0:
                        v.wait_ge(_pe, pe0 + 2 * c)
                    dve.inc(v.tensor_copy(xpb_sb[:], get_chunk(c)))

                @block.scalar
                def _(s, c=c):
                    if c == 0:
                        s.wait_ge(_dve, dve0)
                    else:
                        s.wait_ge(_pe, pe0 + 2 * c)
                    act.inc(s.activation(sqb_sb[:], get_chunk(c), AF.Square))

                @block.tensor
                def _(t, c=c):
                    t.wait_ge(_dve, dve0 + c + 1)
                    t.wait_ge(_act, act0 + c + 1)
                    pe.inc(t.matmul(ps_st[:], ones_sb[:], xpb_sb[:], start=(c == 0), stop=(c == DC - 1)))
                    pe.inc(t.matmul(ps_sum[:], ones_sb[:], sqb_sb[:], start=(c == 0), stop=(c == DC - 1)))

            HBl = 512

            @block.vector
            def _(v):
                v.wait_ge(_pe, pe.v)
                dve.inc(v.tensor_copy(stat_sb[0:1, 0:HBl], ps_st[:]))
                dve.inc(v.tensor_copy(stat_sb[0:1, HBl:2 * HBl], ps_sum[:]))
                v.wait_ge(_dve, dve.v)
                c1 = 1.0 / float(D)
                dve.inc(v.tensor_scalar(stat_sb[0:1, 4 * HBl:5 * HBl], stat_sb[0:1, 0:HBl], c1, None, ALU.mult))
                v.wait_ge(_dve, dve.v)
                dve.inc(v.tensor_tensor(stat_sb[0:1, 2 * HBl:3 * HBl], stat_sb[0:1, 4 * HBl:5 * HBl], stat_sb[0:1, 4 * HBl:5 * HBl], ALU.mult))
                v.wait_ge(_dve, dve.v)
                dve.inc(v.scalar_tensor_tensor(stat_sb[0:1, 2 * HBl:3 * HBl], stat_sb[0:1, HBl:2 * HBl], c1, stat_sb[0:1, 2 * HBl:3 * HBl], ALU.mult, ALU.subtract))

            @block.scalar
            def _(s):
                s.wait_ge(_dve, dve.v)
                act.inc(s.activation(stat_sb[0:1, 2 * HBl:3 * HBl], stat_sb[0:1, 2 * HBl:3 * HBl], AF.Sqrt, bias=eps_sb[0:1, 0:1]))

            @block.vector
            def _(v):
                v.wait_ge(_act, act.v)
                dve.inc(v.reciprocal(stat_sb[0:1, 2 * HBl:3 * HBl], stat_sb[0:1, 2 * HBl:3 * HBl]))
                v.wait_ge(_dve, dve.v)
                dve.inc(v.tensor_tensor(stat_sb[0:1, 3 * HBl:4 * HBl], stat_sb[0:1, 4 * HBl:5 * HBl], stat_sb[0:1, 2 * HBl:3 * HBl], ALU.mult))
                v.wait_ge(_dve, dve.v)
                v.tensor_copy(statb_sb[0:1, 0:HBl], stat_sb[0:1, 2 * HBl:3 * HBl])
                dve.inc(v.tensor_copy(statb_sb[0:1, HBl:2 * HBl], stat_sb[0:1, 3 * HBl:4 * HBl]))

            @block.tensor
            def _(t):
                t.wait_ge(_dve, dve.v)
                pe.inc(t.matmul(ps_a[:], ones1r_sb[:], statb_sb[0:1, 0:HBl], start=True, stop=True))
                pe.inc(t.matmul(ps_b[:], ones1r_sb[:], statb_sb[0:1, HBl:2 * HBl], start=True, stop=True))

            @block.vector
            def _(v):
                v.wait_ge(_pe, pe.v)
                dve.inc(v.tensor_copy(lnA_sb[:], ps_a[:]))
                dve.inc(v.tensor_copy(lnB_sb[:], ps_b[:]))

        @block.sync
        def _(sync):
            for dst, src in [(iden_sb[:], c_iden), (idenb_sb[:], c_idenb),
                             (h128_sb[:], c_h128), (ones_sb[:], c_ones),
                             (ones1r_sb[:], c_ones1r), (ones1rw_sb[:], c_ones1rw),
                             (su_ug_sb[:], SUug),
                             (su_dp_sb[:], SUdp), (wsug64_sb[:], Wsug),
                             (wsdp64_sb[:], Wsdp)]:
                ld.inc(sync.dma_start(out=dst, in_=src), 16)

        @block.vector
        def _(v):
            v.wait_ge(_ld, ld.v)
            v.memset(eps_sb[:], LN_EPS)
            v.memset(amax_sb[:], 0.0)
            v.memset(gsum_sb[:], 0.0)
            v.tensor_scalar(wsug64_sb[:], wsug64_sb[:], 64.0, None, ALU.mult)
            dve.inc(v.tensor_scalar(wsdp64_sb[:], wsdp64_sb[:], 64.0, None, ALU.mult))

        with nc.sbuf_tensor("ctx_all", [128, H * ROWS], BF16) as ctx_all:
            with nc.sbuf_tensor("x0fm", [128, DC * SPAN], BF16) as x0fm:
                x0_t = x0span.rearrange("(n p) d -> n p d", p=128)
                pe1, act1 = pe.v, act.v
                with nc.sbuf_tensor("xin", [128, 4 * D], F32) as xin:
                    for i in range(NSP):
                        slot = i % 4

                        @block.sync
                        def _(sync, i=i, slot=slot):
                            if i >= 4:
                                sync.wait_ge(_pe, pe1 + (i - 3) * DC)
                            ld.inc(sync.dma_start(out=xin[:, slot * D:(slot + 1) * D], in_=x0_t[i]), 16)

                        @block.tensor
                        def _(tensor, i=i, slot=slot):
                            tensor.wait_ge(_ld, ld.v)
                            for c in range(DC):
                                m = i * DC + c
                                if m >= 2:
                                    tensor.wait_ge(_act, act1 + m - 1)
                                dst = ps_a if m % 2 == 0 else ps_b
                                pe.inc(tensor.transpose(dst[:, 0:128], xin[:, slot * D + c * 128:slot * D + (c + 1) * 128], iden_sb[:]))

                        @block.scalar
                        def _(scalar, i=i):
                            for c in range(DC):
                                m = i * DC + c
                                scalar.wait_ge(_pe, pe1 + m + 1)
                                src = ps_a if m % 2 == 0 else ps_b
                                act.inc(scalar.copy(x0fm[:, c * SPAN + i * 128:c * SPAN + (i + 1) * 128], src[:, 0:128]))
                ph1_act = act.v

                @block.gpsimd
                def _(g):
                    g.wait_ge(_act, ph1_act)
                    for c in range(DC):
                        gd.inc(g.dma_start(out=x0ob_dram[c], in_=x0fm[:, c * SPAN + WIN:c * SPAN + SPAN]), 16)
                spill_gd = gd.v

                with (
                    nc.sbuf_tensor("wk_sb", [128, 2 * DC * 128], BF16) as wk_sb,
                    nc.sbuf_tensor("wv_sb", [128, 2 * DC * 128], BF16) as wv_sb,
                    nc.sbuf_tensor("wq_sb", [128, DC * 128], BF16) as wq_sb,
                    nc.sbuf_tensor("kh_sb", [128, SPAN], BF16) as kh_sb,
                    nc.sbuf_tensor("vh_sb", [128, SPAN], BF16) as vh_sb,
                    nc.sbuf_tensor("vt_sb", [128, SPAN], BF16) as vt_sb,
                    nc.sbuf_tensor("qh_sb", [128, ROWS], BF16) as qh_sb,
                    nc.sbuf_tensor("ptil", [128, 2048], BF16) as ptil,
                    nc.sbuf_tensor("rec_sb", [1, 512], F32) as rec_sb,
                    nc.sbuf_tensor("recb_sb", [1, 512], BF16) as recb_sb,
                    nc.sbuf_tensor("bcast_sb", [128, 512], BF16) as bcast_sb,
                    nc.sbuf_tensor("t2_sb", [128, SPAN], BF16) as t2_sb,
                    nc.sbuf_tensor("kvalid_sb", [128, NSP], F32) as kvalid_sb,
                ):
                    @block.sync
                    def _(sync):
                        ld.inc(sync.dma_start(out=t2_sb[:], in_=c_T2), 16)
                        ld.inc(sync.dma_start(out=kvalid_sb[:], in_=kvalid), 16)
                    Wk_p = Wk.rearrange("(c p) f -> p c f", p=128)
                    Wv_p = Wv.rearrange("(c p) f -> p c f", p=128)
                    Wq_p = Wq.rearrange("(c p) f -> p c f", p=128)
                    head_pe_end = []
                    kv_gd_end = []
                    wq_gd_end = []
                    qproj_pe = []

                    def emit_kv_load(h):
                        sc = slc[h % 2]

                        @block.gpsimd
                        def _(g, h=h, sc=sc):
                            if h <= 1:
                                g.wait_ge(_act, ph1_act)
                            if h >= 2:
                                g.wait_ge(_pe, head_pe_end[h - 2])
                            sl = (h % 2) * DC * 128
                            for wsb, wsrc in [(wk_sb, Wk_p), (wv_sb, Wv_p)]:
                                for c in range(DC):
                                    sc.inc(g.dma_start(out=wsb[:, sl + c * 128:sl + (c + 1) * 128],
                                                       in_=wsrc[:, c, h * 128:(h + 1) * 128]), 16)
                        kv_gd_end.append(sc.v)

                    def emit_wq_load(h):
                        @block.gpsimd
                        def _(g, h=h):
                            if h >= 1:
                                g.wait_ge(_pe, qproj_pe[h - 1])
                            for c in range(DC):
                                slc[2].inc(g.dma_start(out=wq_sb[:, c * 128:(c + 1) * 128],
                                                       in_=Wq_p[:, c, h * 128:(h + 1) * 128]), 16)
                        wq_gd_end.append(slc[2].v)

                    emit_kv_load(0)
                    emit_wq_load(0)
                    for h in range(H):
                        if h + 1 < H:
                            emit_kv_load(h + 1)

                        hsl = (h % 2) * DC * 128
                        for wsb, dst, nblk, base in [(wk_sb, kh_sb, 6, 0), (wv_sb, vh_sb, 6, 0), (wq_sb, qh_sb, 2, WIN)]:
                            pe_p0, act_p0 = pe.v, act.v
                            off = hsl if wsb is not wq_sb else 0

                            @block.tensor
                            def _(tensor, wsb=wsb, nblk=nblk, base=base, pe_p0=pe_p0, act_p0=act_p0, h=h, off=off):
                                if wsb is wq_sb:
                                    tensor.wait_ge(_sl[2], wq_gd_end[h])
                                else:
                                    tensor.wait_ge(_sl[h % 2], kv_gd_end[h])
                                for nb in range(nblk):
                                    if nb >= 2:
                                        tensor.wait_ge(_act, act_p0 + nb - 1)
                                    bank = ps_a if nb % 2 == 0 else ps_b
                                    for c in range(DC):
                                        pe.inc(tensor.matmul(bank[:], wsb[:, off + c * 128:off + (c + 1) * 128],
                                                             x0fm[:, c * SPAN + base + nb * 512:c * SPAN + base + nb * 512 + 512],
                                                             start=(c == 0), stop=(c == DC - 1)))
                            if wsb is wq_sb:
                                qproj_pe.append(pe.v)
                                if h + 1 < H:
                                    emit_wq_load(h + 1)

                            @block.scalar
                            def _(scalar, dst=dst, nblk=nblk, pe_p0=pe_p0):
                                for nb in range(nblk):
                                    scalar.wait_ge(_pe, pe_p0 + (nb + 1) * DC)
                                    bank = ps_a if nb % 2 == 0 else ps_b
                                    act.inc(scalar.copy(dst[:, nb * 512:nb * 512 + 512], bank[:]))

                        pe_t0, act_t0 = pe.v, act.v

                        @block.tensor
                        def _(tensor, act_t0=act_t0):
                            tensor.wait_ge(_act, act_t0)
                            for tc in range(NSP):
                                if tc >= 1:
                                    tensor.wait_ge(_act, act_t0 + tc)
                                pe.inc(tensor.transpose(ps_tb[:], vh_sb[:, tc * 128:(tc + 1) * 128], idenb_sb[:]))

                        @block.scalar
                        def _(scalar, pe_t0=pe_t0):
                            for tc in range(NSP):
                                scalar.wait_ge(_pe, pe_t0 + tc + 1)
                                act.inc(scalar.copy(vt_sb[:, tc * 128:(tc + 1) * 128], ps_tb[:]))

                        for qb in range(2):
                            chunks = list(range(4 * qb, 4 * qb + 20))
                            s_done, e_done, p_done, m_done = [], [], [], []
                            for g in range(5):
                                @block.tensor
                                def _(tensor, g=g, qb=qb, chunks=chunks):
                                    if g == 0:
                                        tensor.wait_ge(_act, act.v)
                                        tensor.wait_ge(_dve, dve.v)
                                    else:
                                        tensor.wait_ge(_act, e_done[g - 1])
                                    for c4 in range(4):
                                        c = chunks[4 * g + c4]
                                        Doff = 128 * c - 512 * qb
                                        boundary = not (512 <= Doff <= 1921)
                                        bank = ps_abcd[:, c4 * 512:(c4 + 1) * 512]
                                        pe.inc(tensor.matmul(bank, kh_sb[:, c * 128:(c + 1) * 128],
                                                             qh_sb[:, qb * 512:qb * 512 + 512], start=True, stop=not boundary))
                                        if boundary:
                                            o = 2560 - Doff
                                            pe.inc(tensor.matmul(bank, idenb_sb[:], t2_sb[:, o:o + 512], start=False, stop=True))
                                    s_done.append(pe.v)

                                @block.vector
                                def _(v, g=g, chunks=chunks):
                                    v.wait_ge(_pe, s_done[g])
                                    for c4 in range(4):
                                        c = chunks[4 * g + c4]
                                        bank = ps_abcd[:, c4 * 512:(c4 + 1) * 512]
                                        dve.inc(v.tensor_scalar(bank, bank, kvalid_sb[:, c:c + 1], None, ALU.add))
                                    m_done.append(dve.v)

                                @block.scalar
                                def _(scalar, g=g):
                                    scalar.wait_ge(_dve, m_done[g])
                                    if g >= 1:
                                        scalar.wait_ge(_pe, p_done[g - 1])
                                    act.inc(scalar.activation(ptil[:, 0:2048],
                                                              ps_abcd[:, 0:2048], AF.Exp, scale=INV_SQRT_HD))
                                    e_done.append(act.v)

                                @block.tensor
                                def _(tensor, g=g, chunks=chunks):
                                    tensor.wait_ge(_act, e_done[g])
                                    if g == 0:
                                        tensor.wait_ge(_dve, dve.v)
                                    for c4 in range(4):
                                        c = chunks[4 * g + c4]
                                        sl = ptil[:, c4 * 512:(c4 + 1) * 512]
                                        pe.inc(tensor.matmul(ps_sum[:], ones_sb[:], sl,
                                                             start=(g == 0 and c4 == 0), stop=(g == 4 and c4 == 3)))
                                        pe.inc(tensor.matmul(ps_e[:], vt_sb[:, c * 128:(c + 1) * 128], sl,
                                                             start=(g == 0 and c4 == 0), stop=(g == 4 and c4 == 3)))
                                    p_done.append(pe.v)

                            @block.vector
                            def _(v):
                                v.wait_ge(_pe, pe.v)
                                dve.inc(v.reciprocal(rec_sb[:], ps_sum[:]))
                                v.wait_ge(_dve, dve.v)
                                dve.inc(v.tensor_copy(recb_sb[:], rec_sb[:]))

                            @block.tensor
                            def _(t):
                                t.wait_ge(_dve, dve.v)
                                pe.inc(t.matmul(ps_d[:], ones1r_sb[:], recb_sb[:], start=True, stop=True))

                            @block.vector
                            def _(v, h=h, qb=qb):
                                v.wait_ge(_pe, pe.v)
                                dve.inc(v.tensor_copy(bcast_sb[:], ps_d[:]))
                                v.wait_ge(_dve, dve.v)
                                dve.inc(v.tensor_tensor(ctx_all[:, h * ROWS + qb * 512:h * ROWS + qb * 512 + 512],
                                                        ps_e[:], bcast_sb[:], ALU.mult))
                        head_pe_end.append(pe.v)
                ph2_pe, ph2_act, ph2_dve, ph2_gd = pe.v, act.v, dve.v, gd.v

            with (
                nc.sbuf_tensor("wo_sb", [128, DC * 128], BF16) as wo_sb,
                nc.sbuf_tensor("xpreH", [128, DC * 512], F32) as xpreH,
            ):
                Wo_r = Wo.rearrange("(c p) f -> c p f", p=128)
                for hb in range(2):
                    evac_dve = []
                    for oc in range(DC):
                        @block.gpsimd
                        def _(g, oc=oc, hb=hb):
                            g.wait_ge(_gd, spill_gd)
                            g.wait_ge(_pe, pe.v)
                            g.wait_ge(_dve, dve.v)
                            for c in range(DC):
                                gd.inc(g.dma_start(out=wo_sb[:, c * 128:(c + 1) * 128], in_=Wo_r[c, :, oc * 128:(oc + 1) * 128]), 16)
                            gd.inc(g.dma_start(out=xres_sb[:], in_=x0ob_dram[oc][:, hb * 512:hb * 512 + 512]), 16)

                        @block.tensor
                        def _(tensor, oc=oc, hb=hb):
                            tensor.wait_ge(_gd, gd.v)
                            if oc == 0 and hb == 0:
                                tensor.wait_ge(_dve, ph2_dve)
                            if oc >= 2:
                                tensor.wait_ge(_dve, evac_dve[oc - 2])
                            bank = ps_a if oc % 2 == 0 else ps_b
                            for ic in range(DC):
                                pe.inc(tensor.matmul(bank[:], wo_sb[:, ic * 128:(ic + 1) * 128],
                                                     ctx_all[:, ic * ROWS + hb * 512:ic * ROWS + hb * 512 + 512],
                                                     start=(ic == 0), stop=(ic == DC - 1)))

                        @block.vector
                        def _(v, oc=oc):
                            v.wait_ge(_pe, pe.v)
                            bank = ps_a if oc % 2 == 0 else ps_b
                            dve.inc(v.tensor_tensor(xpreH[:, oc * 512:oc * 512 + 512], bank[:], xres_sb[:], ALU.add))
                            evac_dve.append(dve.v)

                    emit_ln_stats_and_bcast(lambda c: xpreH[:, c * 512:c * 512 + 512])

                    for oc in range(DC):
                        @block.vector
                        def _(v, oc=oc):
                            v.wait_ge(_dve, dve.v)
                            v.wait_ge(_ld, ld.v)
                            dve.inc(v.tensor_tensor(evtmp[:], xpreH[:, oc * 512:oc * 512 + 512], lnA_sb[:], ALU.mult))
                            v.wait_ge(_dve, dve.v)
                            dve.inc(v.tensor_tensor(evtmp[:], evtmp[:], lnB_sb[:], ALU.subtract))
                            v.wait_ge(_dve, dve.v)
                            dve.inc(v.tensor_scalar(xpb_sb[:], evtmp[:], su_ug_sb[:, oc:oc + 1], None, ALU.mult))

                        @block.sync
                        def _(sy, oc=oc, hb=hb):
                            sy.wait_ge(_dve, dve.v)
                            ld.inc(sy.dma_start(out=xA_dram[oc][:, hb * 512:hb * 512 + 512], in_=evtmp[:]), 16)
                            ld.inc(sy.dma_start(out=xsu_dram[oc][:, hb * 512:hb * 512 + 512], in_=xpb_sb[:]), 16)
            ph3_ld = ld.v

        def emit_butterflies(Xbuf, stripe_sb, Cn, Tn, TS, npass):
            for st in range(Tn // TS):
                @block.vector
                def _(v, st=st):
                    for j in range(npass):
                        g_ = Cn >> (j + 1)
                        s_ = 1 << j
                        xv = Xbuf[:].rearrange("p (g b s t) -> p g b s t", g=g_, b=2, s=s_)
                        sv = stripe_sb[:, 0:Cn * TS].rearrange("p (g b s t) -> p g b s t", g=g_, b=2, s=s_)
                        if j % 2 == 0:
                            lo = xv[:, :, 0:1, :, st * TS:st * TS + TS]
                            hi = xv[:, :, 1:2, :, st * TS:st * TS + TS]
                            dlo = sv[:, :, 0:1]
                            dhi = sv[:, :, 1:2]
                        else:
                            lo = sv[:, :, 0:1]
                            hi = sv[:, :, 1:2]
                            dlo = xv[:, :, 0:1, :, st * TS:st * TS + TS]
                            dhi = xv[:, :, 1:2, :, st * TS:st * TS + TS]
                        v.tensor_tensor(dlo, lo, hi, ALU.add)
                        dve.inc(v.tensor_tensor(dhi, lo, hi, ALU.subtract))
                        v.wait_ge(_dve, dve.v)

        with (
            nc.sbuf_tensor("ybuf", [128, DC * ROWS], BF16) as ybuf,
            nc.sbuf_tensor("h2a", [128, IC * 512], BF16) as h2a,
            nc.sbuf_tensor("stripe", [128, IC * 64], BF16) as stripe,
            nc.sbuf_tensor("wstg_sb", [128, 2 * D], BF16) as wstg_sb,
            nc.sbuf_tensor("wTu_sb", [128, DC * 512], BF16) as wTu_sb,
            nc.sbuf_tensor("wTg_sb", [128, DC * 512], BF16) as wTg_sb,
            nc.sbuf_tensor("gtmp", [128, 512], BF16) as gtmp,
            nc.sbuf_tensor("utmp", [128, 512], BF16) as utmp,
            nc.sbuf_tensor("xsuin", [128, 2 * 512], BF16) as xsuin,
            nc.sbuf_tensor("xBpre", [128, DC * 512], F32) as xBpre,
        ):
            pe40, act40 = pe.v, act.v
            dve40 = dve.v
            for nb in range(2 * DC):
                c_chunk, half = divmod(nb, 2)

                @block.gpsimd
                def _(g, nb=nb, c_chunk=c_chunk, half=half):
                    g.wait_ge(_ld, ph3_ld)
                    if nb >= 2:
                        g.wait_ge(_pe, pe40 + nb - 1)
                    gd.inc(g.dma_start(out=xsuin[:, (nb % 2) * 512:(nb % 2) * 512 + 512],
                                       in_=xsu_dram[c_chunk][:, half * 512:half * 512 + 512]), 16)

                @block.tensor
                def _(t, nb=nb):
                    t.wait_ge(_gd, gd.v)
                    if nb >= 2:
                        t.wait_ge(_act, act40 + nb - 1)
                    bank = ps_a if nb % 2 == 0 else ps_b
                    pe.inc(t.matmul(bank[:], h128_sb[:], xsuin[:, (nb % 2) * 512:(nb % 2) * 512 + 512], start=True, stop=True))

                @block.scalar
                def _(s, nb=nb, c_chunk=c_chunk, half=half):
                    s.wait_ge(_pe, pe40 + nb + 1)
                    bank = ps_a if nb % 2 == 0 else ps_b
                    act.inc(s.activation(ybuf[:, c_chunk * ROWS + half * 512:c_chunk * ROWS + half * 512 + 512],
                                         bank[:], AF.Copy, scale=F2048))

            @block.vector
            def _(v):
                v.wait_ge(_act, act.v)
                dve.inc(v.nop())
            emit_butterflies(ybuf, stripe, DC, ROWS, 64, 4)
            h1_dve = dve.v

            Wup_r = Wup.rearrange("(m p) d -> m p d", p=128)
            Wg_r = Wgate.rearrange("(m p) d -> m p d", p=128)
            Wd_r = Wdown.rearrange("(m p) d -> m p d", p=128)
            wupbf_r = wupbf_dram.rearrange("(m p) d -> m p d", p=128)
            wgbf_r = wgbf_dram.rearrange("(m p) d -> m p d", p=128)

            cast_jobs = []
            for m in range(IC):
                cast_jobs.append((Wup_r[m], wupbf_r[m]))
            for m in range(IC):
                cast_jobs.append((Wg_r[m], wgbf_r[m]))
            for m in range(DC):
                for s4 in range(4):
                    cast_jobs.append((Wd_r[m, :, s4 * D:(s4 + 1) * D],
                                      wdbf_dram.rearrange("o (s d) -> o s d", s=4)[m * 128:(m + 1) * 128, s4]))
            cj_gd = []
            for j, (srcap, dstap) in enumerate(cast_jobs):
                sl = j % 2

                @block.gpsimd
                def _(g, j=j, sl=sl, srcap=srcap, dstap=dstap, dve40=dve40):
                    if j <= 1:
                        g.wait_ge(_dve, dve40)
                    else:
                        g.wait_ge(_sl[7], 16 * (j - 1))
                    slc[5 + sl].inc(g.dma_start(out=wstg_sb[:, sl * D:(sl + 1) * D], in_=srcap), 16)
                    cj_gd.append(slc[5 + sl].v)
                    g.wait_ge(_sl[5 + sl], cj_gd[j])
                    slc[7].inc(g.dma_start(out=dstap, in_=wstg_sb[:, sl * D:(sl + 1) * D]), 16)

            for hb in range(2):
                for hb4 in range(IC // 4):
                    @block.sync
                    def _(sy, hb4=hb4, hb=hb):
                        sy.wait_ge(_pe, pe.v)
                        if hb == 0:
                            sy.wait_ge(_sl[7], 16 * (68 + 4 * hb4))
                        for k in range(DC):
                            ld.inc(sy.dma_start(out=wTu_sb[:, k * 512:(k + 1) * 512],
                                                in_=wupbf_dram[hb4 * 512:(hb4 + 1) * 512, k * 128:(k + 1) * 128], transpose=True), 16)
                        for k in range(DC):
                            ld.inc(sy.dma_start(out=wTg_sb[:, k * 512:(k + 1) * 512],
                                                in_=wgbf_dram[hb4 * 512:(hb4 + 1) * 512, k * 128:(k + 1) * 128], transpose=True), 16)

                    for mi in range(4):
                        m = hb4 * 4 + mi

                        @block.tensor
                        def _(t, m=m, mi=mi, hb=hb, h1_dve=h1_dve):
                            t.wait_ge(_ld, ld.v)
                            if m == 0 and hb == 0:
                                t.wait_ge(_dve, h1_dve)
                            else:
                                t.wait_ge(_dve, dve.v)
                            for k in range(DC):
                                pe.inc(t.matmul(ps_a[:], wTu_sb[:, k * 512 + mi * 128:k * 512 + (mi + 1) * 128],
                                                ybuf[:, k * ROWS + hb * 512:k * ROWS + hb * 512 + 512],
                                                start=(k == 0), stop=(k == DC - 1)))
                            for k in range(DC):
                                pe.inc(t.matmul(ps_b[:], wTg_sb[:, k * 512 + mi * 128:k * 512 + (mi + 1) * 128],
                                                ybuf[:, k * ROWS + hb * 512:k * ROWS + hb * 512 + 512],
                                                start=(k == 0), stop=(k == DC - 1)))

                        @block.scalar
                        def _(s, m=m):
                            s.wait_ge(_pe, pe.v)
                            act.inc(s.activation(gtmp[:], ps_b[:], AF.Silu, scale=wsug64_sb[:, IC + m:IC + m + 1]))

                        @block.vector
                        def _(v, m=m):
                            v.wait_ge(_act, act.v)
                            dve.inc(v.tensor_scalar(utmp[:], ps_a[:], wsug64_sb[:, m:m + 1], None, ALU.mult))
                            v.wait_ge(_dve, dve.v)
                            dve.inc(v.scalar_tensor_tensor(h2a[:, m * 512:m * 512 + 512], utmp[:],
                                                           su_dp_sb[:, m:m + 1], gtmp[:], ALU.mult, ALU.mult))

                peA0, actA0 = pe.v, act.v
                for m in range(IC):
                    @block.tensor
                    def _(t, m=m):
                        if m == 0:
                            t.wait_ge(_dve, dve.v)
                        if m >= 2:
                            t.wait_ge(_act, actA0 + m - 1)
                        bank = ps_c if m % 2 == 0 else ps_d
                        pe.inc(t.matmul(bank[:], h128_sb[:], h2a[:, m * 512:m * 512 + 512], start=True, stop=True))

                    @block.scalar
                    def _(s, m=m):
                        s.wait_ge(_pe, peA0 + m + 1)
                        bank = ps_c if m % 2 == 0 else ps_d
                        act.inc(s.activation(h2a[:, m * 512:m * 512 + 512], bank[:], AF.Copy, scale=F8192))

                @block.vector
                def _(v):
                    v.wait_ge(_act, act.v)
                    dve.inc(v.nop())
                emit_butterflies(h2a, stripe, IC, 512, 64, 6)

                evac3 = []
                bf_dve = dve.v
                for ob in range(DC // 4):
                    dbank = [ps_a, ps_b, ps_c, ps_d]
                    for fg in range(4):
                        @block.sync
                        def _(sy, ob=ob, fg=fg, hb=hb):
                            sy.wait_ge(_pe, pe.v)
                            if hb == 0 and ob == 0 and fg == 0:
                                sy.wait_ge(_sl[7], 16 * 192)
                            for k in range(DC):
                                fk = fg * DC + k
                                ld.inc(sy.dma_start(out=wTu_sb[:, k * 512:(k + 1) * 512],
                                                    in_=wdbf_dram[ob * 512:(ob + 1) * 512, fk * 128:(fk + 1) * 128], transpose=True), 16)

                        for oi in range(4):
                            oc = ob * 4 + oi

                            @block.tensor
                            def _(t, ob=ob, fg=fg, oi=oi, oc=oc, bf_dve=bf_dve):
                                t.wait_ge(_ld, ld.v)
                                if ob == 0 and fg == 0 and oi == 0:
                                    t.wait_ge(_dve, bf_dve)
                                if fg == 0 and oi == 0 and ob > 0:
                                    t.wait_ge(_dve, dve.v)
                                bank = dbank[oi]
                                for k in range(DC):
                                    fk = fg * DC + k
                                    pe.inc(t.matmul(bank[:], wTu_sb[:, k * 512 + oi * 128:k * 512 + (oi + 1) * 128],
                                                    h2a[:, fk * 512:fk * 512 + 512], start=(fg == 0 and k == 0), stop=(fg == 3 and k == DC - 1)))

                    for oi in range(4):
                        oc = ob * 4 + oi

                        @block.gpsimd
                        def _(g, oc=oc, hb=hb):
                            g.wait_ge(_dve, dve.v)
                            gd.inc(g.dma_start(out=evtmp2[:], in_=xA_dram[oc][:, hb * 512:hb * 512 + 512]), 16)

                        @block.vector
                        def _(v, oc=oc, oi=oi):
                            v.wait_ge(_pe, pe.v)
                            v.wait_ge(_gd, gd.v)
                            bank = dbank[oi]
                            dve.inc(v.tensor_scalar(evtmp[:], bank[:], wsdp64_sb[:, oc:oc + 1], None, ALU.mult))
                            v.wait_ge(_dve, dve.v)
                            dve.inc(v.tensor_tensor(xBpre[:, oc * 512:oc * 512 + 512], evtmp[:], evtmp2[:], ALU.add))
                            evac3.append(dve.v)

                emit_ln_stats_and_bcast(lambda c: xBpre[:, c * 512:c * 512 + 512])

                for oc in range(DC):
                    @block.vector
                    def _(v, oc=oc):
                        v.wait_ge(_dve, dve.v)
                        v.wait_ge(_ld, ld.v)
                        v.tensor_tensor(evtmp[:], xBpre[:, oc * 512:oc * 512 + 512], lnA_sb[:], ALU.mult)
                        dve.inc(v.tensor_tensor(evtmp[:], evtmp[:], lnB_sb[:], ALU.subtract))
                        dve.inc(v.tensor_copy(xpb_sb[:], evtmp[:]))

                    @block.sync
                    def _(sy, oc=oc, hb=hb):
                        sy.wait_ge(_dve, dve.v)
                        ld.inc(sy.dma_start(out=xB_out[oc][:, hb * 512:hb * 512 + 512], in_=evtmp[:]), 16)
                        ld.inc(sy.dma_start(out=xBb_dram[oc][:, hb * 512:hb * 512 + 512], in_=xpb_sb[:]), 16)
        ph4_ld = ld.v

        with (
            nc.sbuf_tensor("xBb_sb", [128, DC * ROWS], BF16) as xBb_sb,
            nc.sbuf_tensor("wqkv_sb", [128, 2 * DC * 128], BF16) as wqkv_sb,
        ):
            @block.gpsimd
            def _(g):
                g.wait_ge(_ld, ph4_ld)
                g.wait_ge(_dve, dve.v)
                for c in range(DC):
                    gd.inc(g.dma_start(out=xBb_sb[:, c * ROWS:(c + 1) * ROWS], in_=xBb_dram[c]), 16)

            Wqkv_p = Wqkv.rearrange("(c p) f -> p c f", p=128)
            qk_gd, qk_mm = [], []

            def emit_qkv_load(oc):
                sc = slc[0 + oc % 2]

                @block.gpsimd
                def _(g, oc=oc, sc=sc, pev=pe.v, dvev=dve.v):
                    if oc <= 1:
                        g.wait_ge(_pe, pev)
                        g.wait_ge(_dve, dvev)
                    if oc >= 2:
                        g.wait_ge(_pe, qk_mm[oc - 2])
                    sl = (oc % 2) * DC * 128
                    for c in range(DC):
                        sc.inc(g.dma_start(out=wqkv_sb[:, sl + c * 128:sl + (c + 1) * 128],
                                           in_=Wqkv_p[:, c, oc * 128:(oc + 1) * 128]), 16)
                qk_gd.append(sc.v)

            emit_qkv_load(0)
            for oc in range(QKVC):
                if oc + 1 < QKVC:
                    emit_qkv_load(oc + 1)
                for nb in range(2):
                    @block.tensor
                    def _(t, oc=oc, nb=nb):
                        t.wait_ge(_sl[oc % 2], qk_gd[oc])
                        t.wait_ge(_dve, dve.v)
                        sl = (oc % 2) * DC * 128
                        bank = ps_a if nb % 2 == 0 else ps_b
                        for c in range(DC):
                            pe.inc(t.matmul(bank[:], wqkv_sb[:, sl + c * 128:sl + (c + 1) * 128],
                                            xBb_sb[:, c * ROWS + nb * 512:c * ROWS + nb * 512 + 512],
                                            start=(c == 0), stop=(c == DC - 1)))
                        if nb == 1:
                            qk_mm.append(pe.v)

                    @block.vector
                    def _(v, oc=oc, nb=nb):
                        v.wait_ge(_pe, pe.v)
                        v.wait_ge(_ld, ld.v)
                        bank = ps_a if nb % 2 == 0 else ps_b
                        dve.inc(v.tensor_copy(evtmp[:], bank[:]))
                        dve.inc(v.tensor_reduce(red_sb[:], bank[:], axis=AX.X, op=ALU.max, apply_absolute_value=True))
                        v.wait_ge(_dve, dve.v)
                        j = oc // DC
                        dve.inc(v.tensor_tensor(amax_sb[:, j:j + 1], amax_sb[:, j:j + 1], red_sb[:], ALU.max))

                    @block.sync
                    def _(sy, oc=oc, nb=nb):
                        sy.wait_ge(_dve, dve.v)
                        ld.inc(sy.dma_start(out=qkv_out[oc][:, nb * 512:nb * 512 + 512], in_=evtmp[:]), 16)

        with nc.sbuf_tensor("gw_sb", [128, INTER], F32) as gw_sb:
            for j, (src, nch, rowlen) in enumerate([(gslg, 8, D), (gslv, 8, D), (gslo, 2, INTER)]):
                src_r = src.rearrange("(n p) d -> n p d", p=128)
                for i in range(nch):
                    @block.sync
                    def _(sy, i=i, src_r=src_r, rowlen=rowlen):
                        sy.wait_ge(_dve, dve.v)
                        ld.inc(sy.dma_start(out=gw_sb[:, 0:rowlen], in_=src_r[i]), 16)

                    @block.vector
                    def _(v, j=j, rowlen=rowlen):
                        v.wait_ge(_ld, ld.v)
                        dve.inc(v.tensor_reduce(red_sb[:], gw_sb[:, 0:rowlen], axis=AX.X, op=ALU.add, apply_absolute_value=True))
                        v.wait_ge(_dve, dve.v)
                        dve.inc(v.tensor_tensor(gsum_sb[:, j:j + 1], gsum_sb[:, j:j + 1], red_sb[:], ALU.add))

        for j in range(6):
            @block.tensor
            def _(t, j=j):
                t.wait_ge(_dve, dve.v)
                srccol = amax_sb[:, j:j + 1] if j < 3 else gsum_sb[:, j - 3:j - 2]
                pe.inc(t.transpose(ps_d[0:1, 0:128], srccol, iden_sb[:]))

            @block.vector
            def _(v, j=j):
                v.wait_ge(_pe, pe.v)
                dve.inc(v.tensor_copy(ptr_sb[:], ps_d[0:1, 0:128]))
                v.wait_ge(_dve, dve.v)
                dve.inc(v.tensor_reduce(part_sb[0:1, j:j + 1], ptr_sb[:], axis=AX.X,
                                        op=(ALU.max if j < 3 else ALU.add),
                                        apply_absolute_value=False))

        @block.vector
        def _(v):
            v.wait_ge(_dve, dve.v)
            dve.inc(v.memset(part_sb[0:1, 6:8], 0.0))

        @block.sync
        def _(sy):
            sy.wait_ge(_dve, dve.v)
            ld.inc(sy.dma_start(out=partials, in_=part_sb[:]), 16)

        @block.gpsimd
        def _(g):
            g.wait_ge(_ld, ld.v)
            g.wait_ge(_gd, gd.v)
            g.wait_ge(_act, act.v)
            g.wait_ge(_pe, pe.v)
            g.wait_ge(_dve, dve.v)

    return nc



def prep_A_inputs(inputs, core):
    grp, j = divmod(core, 4)
    x0 = np.asarray(inputs["L0_attn_x"], dtype=np.float32)[grp]
    r0 = j * ROWS
    lo = r0 - WIN
    span = np.zeros((SPAN, D), dtype=np.float32)
    s0 = max(0, lo)
    span[s0 - lo:, :] = x0[s0:r0 + ROWS, :]
    kv = np.zeros((128, NSP), dtype=np.float32)
    npad = -lo if lo < 0 else 0
    if npad > 0:
        rowidx = np.arange(SPAN).reshape(NSP, 128).T
        kv = np.where(rowidx < npad, np.float32(NEG), np.float32(0.0))
    f32 = lambda a: np.ascontiguousarray(np.asarray(a, dtype=np.float32))
    return {
        "x0span": span,
        "kvalid": f32(kv),
        "Wq": f32(inputs["L0_attn_Wq"]), "Wk": f32(inputs["L0_attn_Wk"]),
        "Wv": f32(inputs["L0_attn_Wv"]), "Wo": f32(inputs["L0_attn_Wo"]),
        "SUug": f32(np.asarray(inputs["L0_ffn_SU_ug"]).reshape(DC, 128).T),
        "SUdp": f32(np.asarray(inputs["L0_ffn_SU_dp"]).reshape(IC, 128).T),
        "Wsug": f32(np.asarray(inputs["L0_ffn_Wscale_ug"]).reshape(2 * IC, 128).T),
        "Wsdp": f32(np.asarray(inputs["L0_ffn_Wscale_dp"]).reshape(DC, 128).T),
        "Wup": f32(inputs["L0_ffn_W_up"]), "Wgate": f32(inputs["L0_ffn_W_gate"]),
        "Wdown": f32(inputs["L0_ffn_W_down"]), "Wqkv": f32(inputs["L1_attn_W_qkv"]),
        "gslg": f32(np.asarray(inputs["L1_ffn_gate_w"])[core * INTER // 8:(core + 1) * INTER // 8]),
        "gslv": f32(np.asarray(inputs["L1_ffn_val_w"])[core * INTER // 8:(core + 1) * INTER // 8]),
        "gslo": f32(np.asarray(inputs["L1_ffn_out_w"])[core * D // 8:(core + 1) * D // 8]),
    }



LN2C = math.log(2.0)
GMEAN = 1.0 / float(INTER * D)


def build_B(dbg=False):
    nc = bass.Bass("TRN2", target_bir_lowering=False)
    qfm = nc.dram_tensor("qfm", [D, ROWS], F32, kind="ExternalInput").ap()
    kfm = nc.dram_tensor("kfm", [D, S], F32, kind="ExternalInput").ap()
    vtm = nc.dram_tensor("vtm", [S, D], F32, kind="ExternalInput").ap()
    xBi = nc.dram_tensor("xBi", [D, ROWS], F32, kind="ExternalInput").ap()
    prep = nc.dram_tensor("prep", [128, 64], F32, kind="ExternalInput").ap()
    W_out = nc.dram_tensor("W_out", [D, D], F32, kind="ExternalInput").ap()
    b_out = nc.dram_tensor("b_out", [128, DC], F32, kind="ExternalInput").ap()
    gate_w = nc.dram_tensor("gate_w", [INTER, D], F32, kind="ExternalInput").ap()
    val_w = nc.dram_tensor("val_w", [INTER, D], F32, kind="ExternalInput").ap()
    out_w = nc.dram_tensor("out_w", [D, INTER], F32, kind="ExternalInput").ap()
    gate_b = nc.dram_tensor("gate_b", [128, IC], F32, kind="ExternalInput").ap()
    val_b = nc.dram_tensor("val_b", [128, IC], F32, kind="ExternalInput").ap()
    out_b = nc.dram_tensor("out_b", [128, DC], F32, kind="ExternalInput").ap()

    y_out = nc.dram_tensor("y_out", [DC, 128, ROWS], F32, kind="ExternalOutput").ap()

    xC_dram = nc.dram_tensor("xC_dram", [DC, 128, ROWS], F32).ap()
    k8_dram = nc.dram_tensor("k8_dram", [H, 128, S], BF16).ap()
    if dbg:
        dbg_ctx = nc.dram_tensor("dbg_ctx", [128, H * ROWS], BF16, kind="ExternalOutput").ap()
        dbg_k8 = nc.dram_tensor("dbg_k8", [H, 128, S], BF16, kind="ExternalOutput").ap()
        dbg_q8 = nc.dram_tensor("dbg_q8", [H, 128, ROWS], BF16, kind="ExternalOutput").ap()
        dbg_scal = nc.dram_tensor("dbg_scal", [128, 16], F32, kind="ExternalOutput").ap()
        dbg_xc = nc.dram_tensor("dbg_xc", [DC, 128, ROWS], F32, kind="ExternalOutput").ap()
    q8_dram = nc.dram_tensor("q8_dram", [H, 128, ROWS], BF16).ap()
    v8_dram = nc.dram_tensor("v8_dram", [H, 128, S], BF16).ap()
    wgt_dram = nc.dram_tensor("wgt_dram", [INTER, D], BF16).ap()
    wvt_dram = nc.dram_tensor("wvt_dram", [INTER, D], BF16).ap()
    wot_dram = nc.dram_tensor("wot_dram", [D, INTER], BF16).ap()
    gx_dram = nc.dram_tensor("gx_dram", [1, ROWS], F32).ap()
    gh_dram = nc.dram_tensor("gh_dram", [1, ROWS], F32).ap()

    c_iden = nc.inline_tensor(np.eye(128, dtype=np.float32), name="c_iden").ap()
    c_ones = nc.inline_tensor(np.ones((128, 1), dtype=BF), name="c_ones").ap()
    c_ones1r = nc.inline_tensor(np.ones((1, 128), dtype=BF), name="c_ones1r").ap()

    import contextlib
    _stack = contextlib.ExitStack()
    with _stack:
        _sb = lambda *a: _stack.enter_context(nc.sbuf_tensor(*a))
        _ps = lambda *a: _stack.enter_context(nc.psum_tensor(*a))
        _sem = lambda n: _stack.enter_context(nc.semaphore(n))
        iden_sb = _sb("iden_sb", [128, 128], F32)
        ones_sb = _sb("ones_sb", [128, 1], BF16)
        ones1r_sb = _sb("ones1r_sb", [1, 128], BF16)
        prep_sb = _sb("prep_sb", [128, 64], F32)
        eps_sb = _sb("eps_sb", [1, 1], F32)
        scal_sb = _sb("scal_sb", [128, 16], F32)
        stat_sb = _sb("stat_sb", [1, 5 * 512], F32)
        statb_sb = _sb("statb_sb", [1, 2 * 512], BF16)
        lnA_sb = _sb("lnA_sb", [128, 512], BF16)
        lnB_sb = _sb("lnB_sb", [128, 512], BF16)
        sqb_sb = _sb("sqb_sb", [128, 512], BF16)
        xpb_sb = _sb("xpb_sb", [128, 512], BF16)
        evtmp = _sb("evtmp", [128, 512], F32)
        evtmp2 = _sb("evtmp2", [128, 512], F32)
        rowt_sb = _sb("rowt_sb", [1, ROWS], F32)
        rowt2_sb = _sb("rowt2_sb", [1, ROWS], F32)
        rowb_sb = _sb("rowb_sb", [1, ROWS], BF16)
        scxb_sb = _sb("scxb_sb", [128, ROWS], BF16)
        deqg_sb = _sb("deqg_sb", [128, ROWS], BF16)
        deqv_sb = _sb("deqv_sb", [128, ROWS], BF16)
        schb_sb = _sb("schb_sb", [128, 512], BF16)
        deqh_sb = _sb("deqh_sb", [128, 512], BF16)
        mxt_sb = _sb("mxt_sb", [128, ROWS], F32)
        colred_sb = _sb("colred_sb", [128, 8], F32)
        ps_abcd = _ps("ps_abcd", [128, 2048], F32)
        ps_a = ps_abcd[:, 0:512]
        ps_b = ps_abcd[:, 512:1024]
        ps_c = ps_abcd[:, 1024:1536]
        ps_d = ps_abcd[:, 1536:2048]
        ps_sum = _ps("ps_sum", [1, 512], F32)
        ps_st = _ps("ps_st", [1, 512], F32)
        ps_e = _ps("ps_e", [128, 512], F32)
        ps_f = _ps("ps_f", [128, 512], F32)
        _ld = _sem("m_ld")
        _gd = _sem("m_gd")
        _pe = _sem("m_pe")
        _act = _sem("m_act")
        _dve = _sem("m_dve")
        block = _stack.enter_context(nc.Block())
        ld, gd, pe, act, dve = Ctr(_ld), Ctr(_gd), Ctr(_pe), Ctr(_act), Ctr(_dve)

        def emit_ln_stats_and_bcast(get_chunk):
            pe0, act0, dve0 = pe.v, act.v, dve.v
            for c in range(DC):
                @block.vector
                def _(v, c=c):
                    if c > 0:
                        v.wait_ge(_pe, pe0 + 2 * c)
                    dve.inc(v.tensor_copy(xpb_sb[:], get_chunk(c)))

                @block.scalar
                def _(s, c=c):
                    if c == 0:
                        s.wait_ge(_dve, dve0)
                    else:
                        s.wait_ge(_pe, pe0 + 2 * c)
                    act.inc(s.activation(sqb_sb[:], get_chunk(c), AF.Square))

                @block.tensor
                def _(t, c=c):
                    t.wait_ge(_dve, dve0 + c + 1)
                    t.wait_ge(_act, act0 + c + 1)
                    pe.inc(t.matmul(ps_st[:], ones_sb[:], xpb_sb[:], start=(c == 0), stop=(c == DC - 1)))
                    pe.inc(t.matmul(ps_sum[:], ones_sb[:], sqb_sb[:], start=(c == 0), stop=(c == DC - 1)))

            HBl = 512

            @block.vector
            def _(v):
                v.wait_ge(_pe, pe.v)
                dve.inc(v.tensor_copy(stat_sb[0:1, 0:HBl], ps_st[:]))
                dve.inc(v.tensor_copy(stat_sb[0:1, HBl:2 * HBl], ps_sum[:]))
                v.wait_ge(_dve, dve.v)
                c1 = 1.0 / float(D)
                dve.inc(v.tensor_scalar(stat_sb[0:1, 4 * HBl:5 * HBl], stat_sb[0:1, 0:HBl], c1, None, ALU.mult))
                v.wait_ge(_dve, dve.v)
                dve.inc(v.tensor_tensor(stat_sb[0:1, 2 * HBl:3 * HBl], stat_sb[0:1, 4 * HBl:5 * HBl], stat_sb[0:1, 4 * HBl:5 * HBl], ALU.mult))
                v.wait_ge(_dve, dve.v)
                dve.inc(v.scalar_tensor_tensor(stat_sb[0:1, 2 * HBl:3 * HBl], stat_sb[0:1, HBl:2 * HBl], c1, stat_sb[0:1, 2 * HBl:3 * HBl], ALU.mult, ALU.subtract))

            @block.scalar
            def _(s):
                s.wait_ge(_dve, dve.v)
                act.inc(s.activation(stat_sb[0:1, 2 * HBl:3 * HBl], stat_sb[0:1, 2 * HBl:3 * HBl], AF.Sqrt, bias=eps_sb[0:1, 0:1]))

            @block.vector
            def _(v):
                v.wait_ge(_act, act.v)
                dve.inc(v.reciprocal(stat_sb[0:1, 2 * HBl:3 * HBl], stat_sb[0:1, 2 * HBl:3 * HBl]))
                v.wait_ge(_dve, dve.v)
                dve.inc(v.tensor_tensor(stat_sb[0:1, 3 * HBl:4 * HBl], stat_sb[0:1, 4 * HBl:5 * HBl], stat_sb[0:1, 2 * HBl:3 * HBl], ALU.mult))
                v.wait_ge(_dve, dve.v)
                v.tensor_copy(statb_sb[0:1, 0:HBl], stat_sb[0:1, 2 * HBl:3 * HBl])
                dve.inc(v.tensor_copy(statb_sb[0:1, HBl:2 * HBl], stat_sb[0:1, 3 * HBl:4 * HBl]))

            @block.tensor
            def _(t):
                t.wait_ge(_dve, dve.v)
                pe.inc(t.matmul(ps_a[:], ones1r_sb[:], statb_sb[0:1, 0:HBl], start=True, stop=True))
                pe.inc(t.matmul(ps_b[:], ones1r_sb[:], statb_sb[0:1, HBl:2 * HBl], start=True, stop=True))

            @block.vector
            def _(v):
                v.wait_ge(_pe, pe.v)
                dve.inc(v.tensor_copy(lnA_sb[:], ps_a[:]))
                dve.inc(v.tensor_copy(lnB_sb[:], ps_b[:]))

        @block.sync
        def _(sync):
            for dst, src_ in [(iden_sb[:], c_iden), (ones_sb[:], c_ones),
                              (ones1r_sb[:], c_ones1r), (prep_sb[:], prep)]:
                ld.inc(sync.dma_start(out=dst, in_=src_), 16)

        @block.vector
        def _(v):
            v.wait_ge(_ld, ld.v)
            v.memset(eps_sb[:], LN_EPS)
            dve.inc(v.memset(scal_sb[:, 11:12], -3.0 * LN2C))
            pv = prep_sb[:].rearrange("p (c s) -> p s c", c=8, s=8)
            for j in range(3):
                dve.inc(v.tensor_reduce(scal_sb[:, j:j + 1], pv[:, j:j + 1, :], axis=AX.X, op=ALU.max))
            for j in range(3):
                dve.inc(v.tensor_reduce(scal_sb[:, 12 + j:13 + j], pv[:, 3 + j:4 + j, :], axis=AX.X, op=ALU.add))
            v.wait_ge(_dve, dve.v)
            dve.inc(v.tensor_scalar(scal_sb[:, 4:5], scal_sb[:, 2:3], 1.0 / 448.0, None, ALU.mult))
            dve.inc(v.tensor_scalar(scal_sb[:, 3:4], scal_sb[:, 0:1], 1.0 / 448.0, None, ALU.mult))
            v.wait_ge(_dve, dve.v)
            dve.inc(v.scalar_tensor_tensor(scal_sb[:, 3:4], scal_sb[:, 1:2], 1.0 / 448.0, scal_sb[:, 3:4], ALU.mult, ALU.mult))
            v.wait_ge(_dve, dve.v)
            dve.inc(v.tensor_scalar(scal_sb[:, 3:4], scal_sb[:, 3:4], INV_SQRT_HD, None, ALU.mult))
            v.wait_ge(_dve, dve.v)
            for j in range(3):
                dve.inc(v.reciprocal(scal_sb[:, j:j + 1], scal_sb[:, j:j + 1]))
                v.wait_ge(_dve, dve.v)
            v.wait_ge(_dve, dve.v)
            for j in range(3):
                dve.inc(v.tensor_scalar(scal_sb[:, j:j + 1], scal_sb[:, j:j + 1], 448.0, None, ALU.mult))
            v.wait_ge(_dve, dve.v)
            for j in range(3):
                dve.inc(v.tensor_scalar(scal_sb[:, 12 + j:13 + j], scal_sb[:, 12 + j:13 + j], GMEAN, None, ALU.mult))
            v.wait_ge(_dve, dve.v)
            for j in range(3):
                dve.inc(v.tensor_scalar(scal_sb[:, 12 + j:13 + j], scal_sb[:, 12 + j:13 + j], 1e-5, None, ALU.max))
            v.wait_ge(_dve, dve.v)
            for j in range(3):
                dve.inc(v.tensor_scalar(scal_sb[:, 5 + 2 * j:6 + 2 * j], scal_sb[:, 12 + j:13 + j], 0.5, None, ALU.mult))
            v.wait_ge(_dve, dve.v)
            for j in range(3):
                dve.inc(v.tensor_scalar(scal_sb[:, 6 + 2 * j:7 + 2 * j], scal_sb[:, 5 + 2 * j:6 + 2 * j], -1.0, None, ALU.mult))
            v.wait_ge(_dve, dve.v)
            dve.inc(v.reciprocal(scal_sb[:, 15:16], scal_sb[:, 12:13]))
            v.wait_ge(_dve, dve.v)
            dve.inc(v.tensor_tensor(scal_sb[:, 15:16], scal_sb[:, 15:16], scal_sb[:, 13:14], ALU.mult))
        b0_dve = dve.v

        kfm_r = kfm.rearrange("(c p) s -> c p s", p=128)
        qfm_r = qfm.rearrange("(c p) s -> c p s", p=128)
        vtm_r = vtm.rearrange("(c p) h -> p c h", p=128)
        with nc.sbuf_tensor("ctx_all", [128, H * ROWS], BF16) as ctx_all:
            _st1 = contextlib.ExitStack()
            with _st1:
                W1 = 1024
                qin2 = _st1.enter_context(nc.sbuf_tensor("qin2", [128, 4 * W1], F32))
                q8st = _st1.enter_context(nc.sbuf_tensor("q8st", [128, 4 * W1], BF16))
                qa_s = _st1.enter_context(nc.sbuf_tensor("qa_s", [128, 4 * W1], F32))
                qu_s = _st1.enter_context(nc.sbuf_tensor("qu_s", [128, 4 * W1], F32))
                qw_s = _st1.enter_context(nc.sbuf_tensor("qw_s", [128, 4 * W1], F32))
                qr_s = _st1.enter_context(nc.sbuf_tensor("qr_s", [128, 4 * W1], F32))
                qti_s = _st1.enter_context(nc.sbuf_tensor("qti_s", [128, 4 * W1], I32))
                qsg_s = _st1.enter_context(nc.sbuf_tensor("qsg_s", [128, 4 * W1], BF16))
                qjobs = []
                for h in range(H):
                    for u in range(4):
                        qjobs.append((kfm_r[h][:, u * W1:(u + 1) * W1], 1,
                                      k8_dram[h][:, u * W1:(u + 1) * W1], False))
                    for u in range(4):
                        qjobs.append((vtm_r[:, u * 8:(u + 1) * 8, h * 128:(h + 1) * 128], 2,
                                      v8_dram[h][:, u * W1:(u + 1) * W1], True))
                    qjobs.append((qfm_r[h], 0, q8_dram[h], False))
                njob = len(qjobs)
                q_dve0, q_pe0, q_act0 = dve.v, pe.v, act.v
                src_done = [0] * 4
                d4_done = [q_dve0] * 4
                st_done = [0] * 4
                for p in range(njob // 2):
                    prs = []
                    for j in (2 * p, 2 * p + 1):
                        sl = j % 4
                        srcap, invc, dstd, isv = qjobs[j]
                        dst = q8st[:, sl * W1:(sl + 1) * W1]
                        prs.append((sl, srcap, invc, dstd, isv, dst))

                    @block.sync
                    def _(sy, p=p, prs=prs):
                        for sl, srcap, invc, dstd, isv, dst in prs:
                            if p <= 1:
                                sy.wait_ge(_dve, q_dve0)
                                sy.wait_ge(_pe, q_pe0)
                                sy.wait_ge(_act, q_act0)
                            else:
                                sy.wait_ge(_act, src_done[sl])
                            if isv:
                                ld.inc(sy.dma_start(out=qin2[:, sl * W1:(sl + 1) * W1]
                                                    .rearrange("p (c h2) -> p c h2", c=8),
                                                    in_=srcap), 16)
                            else:
                                ld.inc(sy.dma_start(out=qin2[:, sl * W1:(sl + 1) * W1], in_=srcap), 16)
                    pair_ld = ld.v

                    @block.scalar
                    def _(s, prs=prs, pair_ld=pair_ld):
                        s.wait_ge(_ld, pair_ld)
                        for sl, srcap, invc, dstd, isv, dst in prs:
                            s.wait_ge(_dve, d4_done[sl])
                        for sl, srcap, invc, dstd, isv, dst in prs:
                            act.inc(s.activation(qsg_s[:, sl * W1:(sl + 1) * W1],
                                                 qin2[:, sl * W1:(sl + 1) * W1], AF.Sign))
                        for sl, srcap, invc, dstd, isv, dst in prs:
                            act.inc(s.activation(qu_s[:, sl * W1:(sl + 1) * W1],
                                                 qin2[:, sl * W1:(sl + 1) * W1], AF.Abs,
                                                 scale=scal_sb[:, invc:invc + 1]))
                            src_done[sl] = act.v
                    a1 = act.v

                    @block.vector
                    def _(v, prs=prs, a1=a1):
                        v.wait_ge(_act, a1)
                        for sl, srcap, invc, dstd, isv, dst in prs:
                            dve.inc(v.tensor_scalar(qa_s[:, sl * W1:(sl + 1) * W1],
                                                    qu_s[:, sl * W1:(sl + 1) * W1],
                                                    2.0 ** -20, None, ALU.max))
                    d1 = dve.v

                    @block.scalar
                    def _(s, prs=prs, d1=d1):
                        s.wait_ge(_dve, d1)
                        for sl, srcap, invc, dstd, isv, dst in prs:
                            act.inc(s.activation(qu_s[:, sl * W1:(sl + 1) * W1],
                                                 qa_s[:, sl * W1:(sl + 1) * W1], AF.Ln))
                    a2 = act.v

                    @block.vector
                    def _(v, prs=prs, a2=a2):
                        v.wait_ge(_act, a2)
                        for sl, srcap, invc, dstd, isv, dst in prs:
                            dve.inc(v.tensor_scalar(qti_s[:, sl * W1:(sl + 1) * W1],
                                                    qu_s[:, sl * W1:(sl + 1) * W1],
                                                    1.0 / LN2C, -0.49995, ALU.mult, ALU.add))
                        v.wait_ge(_dve, dve.v)
                        for sl, srcap, invc, dstd, isv, dst in prs:
                            dve.inc(v.tensor_scalar(qu_s[:, sl * W1:(sl + 1) * W1],
                                                    qti_s[:, sl * W1:(sl + 1) * W1],
                                                    -6.0, None, ALU.max))
                    d2 = dve.v

                    @block.scalar
                    def _(s, prs=prs, d2=d2):
                        s.wait_ge(_dve, d2)
                        for sl, srcap, invc, dstd, isv, dst in prs:
                            act.inc(s.activation(qw_s[:, sl * W1:(sl + 1) * W1],
                                                 qu_s[:, sl * W1:(sl + 1) * W1], AF.Exp, scale=-LN2C))
                    a3 = act.v

                    @block.vector
                    def _(v, prs=prs, a3=a3):
                        v.wait_ge(_act, a3)
                        for sl, srcap, invc, dstd, isv, dst in prs:
                            dve.inc(v.scalar_tensor_tensor(qr_s[:, sl * W1:(sl + 1) * W1],
                                                           qa_s[:, sl * W1:(sl + 1) * W1], 8.0,
                                                           qw_s[:, sl * W1:(sl + 1) * W1],
                                                           ALU.mult, ALU.mult))
                        v.wait_ge(_dve, dve.v)
                        for sl, srcap, invc, dstd, isv, dst in prs:
                            dve.inc(v.tensor_copy(qti_s[:, sl * W1:(sl + 1) * W1],
                                                  qr_s[:, sl * W1:(sl + 1) * W1]))
                    d3 = dve.v

                    @block.scalar
                    def _(s, prs=prs, d3=d3):
                        s.wait_ge(_dve, d3)
                        for sl, srcap, invc, dstd, isv, dst in prs:
                            act.inc(s.activation(qw_s[:, sl * W1:(sl + 1) * W1],
                                                 qu_s[:, sl * W1:(sl + 1) * W1], AF.Exp,
                                                 scale=LN2C, bias=scal_sb[:, 11:12]))
                    a4 = act.v

                    @block.vector
                    def _(v, prs=prs, a4=a4):
                        v.wait_ge(_act, a4)
                        for sl, srcap, invc, dstd, isv, dst in prs:
                            if dstd is not None:
                                v.wait_ge(_gd, st_done[sl])
                        for sl, srcap, invc, dstd, isv, dst in prs:
                            dve.inc(v.tensor_tensor(qr_s[:, sl * W1:(sl + 1) * W1],
                                                    qti_s[:, sl * W1:(sl + 1) * W1],
                                                    qw_s[:, sl * W1:(sl + 1) * W1], ALU.mult))
                        v.wait_ge(_dve, dve.v)
                        for sl, srcap, invc, dstd, isv, dst in prs:
                            dve.inc(v.tensor_tensor(dst, qr_s[:, sl * W1:(sl + 1) * W1],
                                                    qsg_s[:, sl * W1:(sl + 1) * W1], ALU.mult))
                            d4_done[sl] = dve.v

                    @block.gpsimd
                    def _(g, prs=prs, dvev=dve.v):
                        g.wait_ge(_dve, dvev)
                        for sl, srcap, invc, dstd, isv, dst in prs:
                            if dstd is not None:
                                gd.inc(g.dma_start(out=dstd, in_=q8st[:, sl * W1:(sl + 1) * W1]), 16)
                                st_done[sl] = gd.v
                stq_ld, stq_gd = ld.v, gd.v
                stq_dve, stq_act = dve.v, act.v

            _stF = contextlib.ExitStack()
            with _stF:
                k8sl = _stF.enter_context(nc.sbuf_tensor("k8sl", [128, 4 * S], BF16))
                v8sl = _stF.enter_context(nc.sbuf_tensor("v8sl", [128, 4 * S], BF16))
                ptil = _stF.enter_context(nc.sbuf_tensor("ptil", [128, 2 * 2048], BF16))
                rec_sb = _stF.enter_context(nc.sbuf_tensor("rec_sb", [1, 512], F32))
                recb_sb = _stF.enter_context(nc.sbuf_tensor("recb_sb", [1, 512], BF16))
                bcast_sb = _stF.enter_context(nc.sbuf_tensor("bcast_sb", [128, 512], BF16))
                q8_all = _stF.enter_context(nc.sbuf_tensor("q8_all", [128, H * ROWS], BF16))

                @block.sync
                def _(sy):
                    sy.wait_ge(_ld, stq_ld)
                    sy.wait_ge(_gd, stq_gd)
                    sy.wait_ge(_dve, stq_dve)
                    sy.wait_ge(_act, stq_act)
                    for h in range(H):
                        ld.inc(sy.dma_start(out=q8_all[:, h * ROWS:(h + 1) * ROWS], in_=q8_dram[h]), 16)

                nch = S // 128
                ngr = nch // 4
                fl_pe, fl_ld = [], []
                for h in range(H):
                    hsl = h % 4

                    @block.sync
                    def _(sy, h=h, hsl=hsl):
                        if h >= 4:
                            sy.wait_ge(_pe, fl_pe[h - 4])
                        ld.inc(sy.dma_start(out=k8sl[:, hsl * S:(hsl + 1) * S], in_=k8_dram[h]), 16)
                        ld.inc(sy.dma_start(out=v8sl[:, hsl * S:(hsl + 1) * S], in_=v8_dram[h]), 16)
                    fl_ld.append(ld.v)

                    v3 = v8sl[:, hsl * S:(hsl + 1) * S].rearrange("p (c h2) -> p c h2", c=nch)
                    for qb in range(2):
                        s_done, e_done, p_done = [], [], []
                        for g in range(ngr):
                            @block.tensor
                            def _(tensor, g=g, qb=qb, h=h, hsl=hsl):
                                if g == 0:
                                    tensor.wait_ge(_dve, dve.v)
                                    tensor.wait_ge(_ld, fl_ld[h])
                                else:
                                    tensor.wait_ge(_act, e_done[g - 1])
                                for c in range(4):
                                    idx = 4 * g + c
                                    pe.inc(tensor.matmul(ps_abcd[:, c * 512:(c + 1) * 512],
                                                         k8sl[:, hsl * S + idx * 128:hsl * S + (idx + 1) * 128],
                                                         q8_all[:, h * ROWS + qb * 512:h * ROWS + qb * 512 + 512],
                                                         start=True, stop=True))
                                s_done.append(pe.v)

                            @block.scalar
                            def _(scalar, g=g):
                                scalar.wait_ge(_pe, s_done[g])
                                if g >= 2:
                                    scalar.wait_ge(_pe, p_done[g - 2])
                                act.inc(scalar.activation(ptil[:, (g % 2) * 2048:(g % 2) * 2048 + 2048],
                                                          ps_abcd[:, 0:2048],
                                                          AF.Exp, scale=scal_sb[:, 3:4]))
                                e_done.append(act.v)

                            @block.tensor
                            def _(tensor, g=g):
                                tensor.wait_ge(_act, e_done[g])
                                if g == 0:
                                    tensor.wait_ge(_dve, dve.v)
                                for c in range(4):
                                    idx = 4 * g + c
                                    sl2 = ptil[:, (g % 2) * 2048 + c * 512:(g % 2) * 2048 + (c + 1) * 512]
                                    pe.inc(tensor.matmul(ps_sum[:], ones_sb[:], sl2,
                                                         start=(g == 0 and c == 0), stop=(g == ngr - 1 and c == 3)))
                                    pe.inc(tensor.matmul(ps_e[:], v3[:, idx], sl2,
                                                         start=(g == 0 and c == 0), stop=(g == ngr - 1 and c == 3)))
                                p_done.append(pe.v)

                        @block.vector
                        def _(v):
                            v.wait_ge(_pe, pe.v)
                            dve.inc(v.reciprocal(rec_sb[:], ps_sum[:]))
                            v.wait_ge(_dve, dve.v)
                            dve.inc(v.tensor_scalar(rec_sb[:], rec_sb[:], scal_sb[0:1, 4:5], None, ALU.mult))
                            v.wait_ge(_dve, dve.v)
                            dve.inc(v.tensor_copy(recb_sb[:], rec_sb[:]))

                        @block.tensor
                        def _(t):
                            t.wait_ge(_dve, dve.v)
                            pe.inc(t.matmul(ps_f[:], ones1r_sb[:], recb_sb[:], start=True, stop=True))

                        @block.vector
                        def _(v, h=h, qb=qb):
                            v.wait_ge(_pe, pe.v)
                            dve.inc(v.tensor_copy(bcast_sb[:], ps_f[:]))
                            v.wait_ge(_dve, dve.v)
                            dve.inc(v.tensor_tensor(ctx_all[:, h * ROWS + qb * 512:h * ROWS + qb * 512 + 512],
                                                    ps_e[:], bcast_sb[:], ALU.mult))
                    fl_pe.append(pe.v)
            b2_dve, b2_pe = dve.v, pe.v
            if dbg:
                @block.sync
                def _(sy):
                    sy.wait_ge(_dve, dve.v)
                    sy.wait_ge(_gd, gd.v)
                    ld.inc(sy.dma_start(out=dbg_ctx, in_=ctx_all[:]), 16)
                    ld.inc(sy.dma_start(out=dbg_scal, in_=scal_sb[:]), 16)
                    for h in range(H):
                        ld.inc(sy.dma_start(out=dbg_k8[h], in_=k8_dram[h]), 16)
                        ld.inc(sy.dma_start(out=dbg_q8[h], in_=q8_dram[h]), 16)

            with (
                nc.sbuf_tensor("wo_sb", [128, DC * 128], BF16) as wo_sb,
                nc.sbuf_tensor("bo_sb", [128, DC], F32) as bo_sb,
                nc.sbuf_tensor("xpreH", [128, DC * 512], F32) as xpreH,
            ):
                @block.sync
                def _(sy):
                    sy.wait_ge(_dve, b2_dve)
                    ld.inc(sy.dma_start(out=bo_sb[:], in_=b_out), 16)

                @block.vector
                def _(v):
                    v.wait_ge(_ld, ld.v)
                    dve.inc(v.memset(mxt_sb[:], 0.0))

                Wo_r = W_out.rearrange("(c p) f -> c p f", p=128)
                xBi_r = xBi.rearrange("(c p) s -> c p s", p=128)
                for hb in range(2):
                    evac_dve = []
                    for oc in range(DC):
                        @block.gpsimd
                        def _(g, oc=oc, hb=hb):
                            g.wait_ge(_pe, pe.v)
                            g.wait_ge(_dve, dve.v)
                            for c in range(DC):
                                gd.inc(g.dma_start(out=wo_sb[:, c * 128:(c + 1) * 128], in_=Wo_r[c, :, oc * 128:(oc + 1) * 128]), 16)
                            gd.inc(g.dma_start(out=evtmp2[:], in_=xBi_r[oc][:, hb * 512:hb * 512 + 512]), 16)

                        @block.tensor
                        def _(tensor, oc=oc, hb=hb):
                            tensor.wait_ge(_gd, gd.v)
                            if oc == 0 and hb == 0:
                                tensor.wait_ge(_dve, b2_dve)
                            if oc >= 2:
                                tensor.wait_ge(_dve, evac_dve[oc - 2])
                            bank = ps_a if oc % 2 == 0 else ps_b
                            for ic in range(DC):
                                pe.inc(tensor.matmul(bank[:], wo_sb[:, ic * 128:(ic + 1) * 128],
                                                     ctx_all[:, ic * ROWS + hb * 512:ic * ROWS + hb * 512 + 512],
                                                     start=(ic == 0), stop=(ic == DC - 1)))

                        @block.vector
                        def _(v, oc=oc, hb=hb):
                            v.wait_ge(_pe, pe.v)
                            v.wait_ge(_gd, gd.v)
                            bank = ps_a if oc % 2 == 0 else ps_b
                            dve.inc(v.tensor_scalar(evtmp[:], bank[:], bo_sb[:, oc:oc + 1], None, ALU.add))
                            v.wait_ge(_dve, dve.v)
                            dve.inc(v.tensor_tensor(xpreH[:, oc * 512:oc * 512 + 512], evtmp[:], evtmp2[:], ALU.add))
                            evac_dve.append(dve.v)

                    emit_ln_stats_and_bcast(lambda c: xpreH[:, c * 512:c * 512 + 512])

                    for oc in range(DC):
                        @block.vector
                        def _(v, oc=oc, hb=hb):
                            v.wait_ge(_dve, dve.v)
                            v.wait_ge(_ld, ld.v)
                            dve.inc(v.tensor_tensor(evtmp[:], xpreH[:, oc * 512:oc * 512 + 512], lnA_sb[:], ALU.mult))
                            v.wait_ge(_dve, dve.v)
                            dve.inc(v.tensor_tensor(evtmp[:], evtmp[:], lnB_sb[:], ALU.subtract))
                        @block.scalar
                        def _(s, oc=oc):
                            s.wait_ge(_dve, dve.v)
                            act.inc(s.activation(evtmp2[:], evtmp[:], AF.Abs))

                        @block.vector
                        def _(v, oc=oc, hb=hb):
                            v.wait_ge(_act, act.v)
                            dve.inc(v.tensor_tensor(mxt_sb[:, hb * 512:hb * 512 + 512],
                                                    mxt_sb[:, hb * 512:hb * 512 + 512], evtmp2[:], ALU.max))

                        @block.sync
                        def _(sy, oc=oc, hb=hb):
                            sy.wait_ge(_dve, dve.v)
                            ld.inc(sy.dma_start(out=xC_dram[oc][:, hb * 512:hb * 512 + 512], in_=evtmp[:]), 16)
            b3_ld, b3_dve = ld.v, dve.v

        _st4 = contextlib.ExitStack()
        with _st4:
            xi_sb = _st4.enter_context(nc.sbuf_tensor("xi_sb", [128, DC * 512], BF16))
            h2_sb = _st4.enter_context(nc.sbuf_tensor("h2_sb", [128, IC * 512], BF16))
            wTg_sb = _st4.enter_context(nc.sbuf_tensor("wTg_sb", [128, DC * 512], BF16))
            wTv_sb = _st4.enter_context(nc.sbuf_tensor("wTv_sb", [128, DC * 512], BF16))
            gb_sb = _st4.enter_context(nc.sbuf_tensor("gb_sb", [128, IC], F32))
            vb_sb = _st4.enter_context(nc.sbuf_tensor("vb_sb", [128, IC], F32))
            ob_sb = _st4.enter_context(nc.sbuf_tensor("ob_sb", [128, DC], F32))
            gtmp = _st4.enter_context(nc.sbuf_tensor("gtmp", [128, 512], F32))
            vtmp = _st4.enter_context(nc.sbuf_tensor("vtmp", [128, 512], F32))
            gbf = _st4.enter_context(nc.sbuf_tensor("gbf", [128, 512], BF16))
            mxh_sb = _st4.enter_context(nc.sbuf_tensor("mxh_sb", [128, 512], F32))
            qtii = _st4.enter_context(nc.sbuf_tensor("qtii", [128, 512], I32))
            xcin = _st4.enter_context(nc.sbuf_tensor("xcin", [128, 512], F32))
            xpre2 = _st4.enter_context(nc.sbuf_tensor("xpre2", [128, DC * 512], F32))
            @block.sync
            def _(sy):
                ld.inc(sy.dma_start(out=gb_sb[:], in_=gate_b), 16)
                ld.inc(sy.dma_start(out=vb_sb[:], in_=val_b), 16)
                ld.inc(sy.dma_start(out=ob_sb[:], in_=out_b), 16)

            for tb in range(ROWS // 128):
                @block.tensor
                def _(t, tb=tb):
                    t.wait_ge(_dve, b3_dve if tb == 0 else dve.v)
                    pe.inc(t.transpose(ps_d[:, 0:128], mxt_sb[:, tb * 128:(tb + 1) * 128], iden_sb[:]))

                @block.vector
                def _(v, tb=tb):
                    v.wait_ge(_pe, pe.v)
                    dve.inc(v.tensor_reduce(colred_sb[:, tb:tb + 1], ps_d[:, 0:128], axis=AX.X, op=ALU.max))

            gx_v = gx_dram.rearrange("a (c p) -> a c p", p=128)

            @block.sync
            def _(sy):
                sy.wait_ge(_dve, dve.v)
                for tb in range(ROWS // 128):
                    ld.inc(sy.dma_start(out=gx_v[0, tb], in_=colred_sb[:, tb:tb + 1]), 16)
                sy.wait_ge(_ld, ld.v)
                ld.inc(sy.dma_start(out=rowt_sb[:], in_=gx_dram), 16)

            def emit_rowmath_and_bcast(dst_pair, mulcol, postmul127):
                @block.vector
                def _(v):
                    v.wait_ge(_ld, ld.v)
                    v.wait_ge(_pe, pe.v)
                    if mulcol is None:
                        dve.inc(v.tensor_scalar(rowt2_sb[:], rowt_sb[:], 1e-5, None, ALU.max))
                        v.wait_ge(_dve, dve.v)
                        dve.inc(v.tensor_copy(rowt_sb[:], rowt2_sb[:]))
                        v.wait_ge(_dve, dve.v)
                        dve.inc(v.reciprocal(rowt2_sb[:], rowt2_sb[:]))
                        v.wait_ge(_dve, dve.v)
                        dve.inc(v.tensor_scalar(rowt2_sb[:], rowt2_sb[:], 127.0, None, ALU.mult))
                    else:
                        dve.inc(v.tensor_scalar(rowt2_sb[:], rowt_sb[:], 1.0 / 127.0, None, ALU.mult))
                        v.wait_ge(_dve, dve.v)
                        dve.inc(v.tensor_scalar(rowt2_sb[:], rowt2_sb[:], scal_sb[0:1, mulcol:mulcol + 1], None, ALU.mult))
                    v.wait_ge(_dve, dve.v)
                    dve.inc(v.tensor_copy(rowb_sb[:], rowt2_sb[:]))

                @block.tensor
                def _(t):
                    t.wait_ge(_dve, dve.v)
                    pe.inc(t.matmul(ps_a[:], ones1r_sb[:], rowb_sb[0:1, 0:512], start=True, stop=True))
                    pe.inc(t.matmul(ps_b[:], ones1r_sb[:], rowb_sb[0:1, 512:1024], start=True, stop=True))

                @block.vector
                def _(v):
                    v.wait_ge(_pe, pe.v)
                    v.tensor_copy(dst_pair[:, 0:512], ps_a[:])
                    dve.inc(v.tensor_copy(dst_pair[:, 512:1024], ps_b[:]))

            emit_rowmath_and_bcast(scxb_sb, None, None)
            emit_rowmath_and_bcast(deqg_sb, 12, None)
            emit_rowmath_and_bcast(deqv_sb, 13, None)

            Wg_r = gate_w.rearrange("(m p) d -> m p d", p=128)
            Wv2_r = val_w.rearrange("(m p) d -> m p d", p=128)
            Wo2_r = out_w.rearrange("(m p) d -> m p d", p=128)
            gh_v = gh_dram.rearrange("a (c p) -> a c p", p=128)
            wgt_r = wgt_dram.rearrange("(m p) d -> m p d", p=128)
            wvt_r = wvt_dram.rearrange("(m p) d -> m p d", p=128)
            wot_r = wot_dram.rearrange("(m p) d -> m p d", p=128)

            wnat_al = xpre2[:, 0:D]
            s1_al = h2_sb[:, 0:D]
            s2_al = h2_sb[:, D:2 * D]
            tern_jobs = []
            for m in range(IC):
                tern_jobs.append((Wg_r[m], 6, 5, wgt_r[m]))
            for m in range(IC):
                tern_jobs.append((Wv2_r[m], 8, 7, wvt_r[m]))
            for m in range(DC):
                for s4 in range(4):
                    tern_jobs.append((Wo2_r[m, :, s4 * D:(s4 + 1) * D], 10, 9,
                                      wot_r[m * 4 + s4] if False else wot_dram.rearrange("o (s d) -> o s d", s=4)[m * 128:(m + 1) * 128, s4]))
            tj_s2, tj_ad, tj_st = [], [], []
            for j, (srcap, bneg, bpos, dstap) in enumerate(tern_jobs):
                sl = j % 2
                wslot = xpre2[:, sl * D:(sl + 1) * D]
                stg = wTg_sb[:, 0:D] if sl == 0 else wTv_sb[:, 0:D]

                @block.gpsimd
                def _(g, j=j, srcap=srcap, wslot=wslot, dvev=dve.v):
                    if j <= 1:
                        g.wait_ge(_dve, dvev)
                    else:
                        g.wait_ge(_dve, tj_s2[j - 2])
                    gd.inc(g.dma_start(out=wslot, in_=srcap), 16)

                @block.vector
                def _(v, j=j, bneg=bneg, bpos=bpos, wslot=wslot, stg=stg):
                    v.wait_ge(_gd, gd.v)
                    if j >= 2:
                        v.wait_ge(_ld, tj_st[j - 2])
                    dve.inc(v.tensor_scalar(s1_al, wslot, scal_sb[:, bpos:bpos + 1], None, ALU.is_gt))
                    dve.inc(v.tensor_scalar(s2_al, wslot, scal_sb[:, bneg:bneg + 1], None, ALU.is_lt))
                    tj_s2.append(dve.v)
                    v.wait_ge(_dve, dve.v)
                    dve.inc(v.tensor_tensor(stg, s1_al, s2_al, ALU.subtract))
                    tj_ad.append(dve.v)

                @block.sync
                def _(sy, j=j, dstap=dstap, stg=stg):
                    sy.wait_ge(_dve, tj_ad[j])
                    ld.inc(sy.dma_start(out=dstap, in_=stg), 16)
                    tj_st.append(ld.v)
            tern_ld = ld.v

            for hb in range(2):
                for oc in range(DC):
                    @block.gpsimd
                    def _(g, oc=oc, hb=hb):
                        g.wait_ge(_ld, b3_ld)
                        g.wait_ge(_dve, dve.v)
                        gd.inc(g.dma_start(out=xcin[:], in_=xC_dram[oc][:, hb * 512:hb * 512 + 512]), 16)

                    @block.vector
                    def _(v, oc=oc, hb=hb):
                        v.wait_ge(_gd, gd.v)
                        dve.inc(v.tensor_tensor(gtmp[:], xcin[:], scxb_sb[:, hb * 512:hb * 512 + 512], ALU.mult))
                        v.wait_ge(_dve, dve.v)
                        dve.inc(v.tensor_copy(qtii[:], gtmp[:]))
                        v.wait_ge(_dve, dve.v)
                        dve.inc(v.tensor_scalar(xi_sb[:, oc * 512:oc * 512 + 512], qtii[:], -128.0, 127.0, ALU.max, ALU.min))

                @block.vector
                def _(v):
                    v.wait_ge(_dve, dve.v)
                    dve.inc(v.memset(mxh_sb[:], 0.0))
                xi_dve = dve.v
                ev_done = []

                for hb4 in range(IC // 4):
                    @block.sync
                    def _(sy, hb4=hb4, first=(hb4 == 0)):
                        sy.wait_ge(_pe, pe.v)
                        if first:
                            sy.wait_ge(_ld, tern_ld)
                        for k in range(DC):
                            ld.inc(sy.dma_start(out=wTg_sb[:, k * 512:(k + 1) * 512],
                                                in_=wgt_dram[hb4 * 512:(hb4 + 1) * 512, k * 128:(k + 1) * 128], transpose=True), 16)
                        for k in range(DC):
                            ld.inc(sy.dma_start(out=wTv_sb[:, k * 512:(k + 1) * 512],
                                                in_=wvt_dram[hb4 * 512:(hb4 + 1) * 512, k * 128:(k + 1) * 128], transpose=True), 16)

                    for mi in range(4):
                        m = hb4 * 4 + mi
                        ga = ps_a if m % 2 == 0 else ps_c
                        vbk = ps_b if m % 2 == 0 else ps_d

                        @block.tensor
                        def _(t, m=m, mi=mi, ga=ga, vbk=vbk):
                            t.wait_ge(_ld, ld.v)
                            if m <= 1:
                                t.wait_ge(_dve, xi_dve)
                            else:
                                t.wait_ge(_dve, ev_done[m - 2])
                            for k in range(DC):
                                pe.inc(t.matmul(ga[:], wTg_sb[:, k * 512 + mi * 128:k * 512 + (mi + 1) * 128],
                                                xi_sb[:, k * 512:k * 512 + 512], start=(k == 0), stop=(k == DC - 1)))
                            for k in range(DC):
                                pe.inc(t.matmul(vbk[:], wTv_sb[:, k * 512 + mi * 128:k * 512 + (mi + 1) * 128],
                                                xi_sb[:, k * 512:k * 512 + 512], start=(k == 0), stop=(k == DC - 1)))

                        @block.vector
                        def _(v, m=m, hb=hb, ga=ga):
                            v.wait_ge(_pe, pe.v)
                            dve.inc(v.tensor_tensor(gtmp[:], ga[:], deqg_sb[:, hb * 512:hb * 512 + 512], ALU.mult))
                            v.wait_ge(_dve, dve.v)
                            dve.inc(v.tensor_scalar(gtmp[:], gtmp[:], gb_sb[:, m:m + 1], None, ALU.add))

                        @block.scalar
                        def _(s, m=m):
                            s.wait_ge(_dve, dve.v)
                            act.inc(s.activation(gbf[:], gtmp[:], AF.Silu))

                        @block.vector
                        def _(v, m=m, hb=hb, vbk=vbk):
                            dve.inc(v.tensor_tensor(vtmp[:], vbk[:], deqv_sb[:, hb * 512:hb * 512 + 512], ALU.mult))
                            v.wait_ge(_dve, dve.v)
                            dve.inc(v.tensor_scalar(vtmp[:], vtmp[:], vb_sb[:, m:m + 1], None, ALU.add))
                            v.wait_ge(_act, act.v)
                            v.wait_ge(_dve, dve.v)
                            dve.inc(v.tensor_tensor(h2_sb[:, m * 512:m * 512 + 512], gbf[:], vtmp[:], ALU.mult))
                            v.wait_ge(_dve, dve.v)

                        @block.vector
                        def _(v, m=m):
                            dve.inc(v.scalar_tensor_tensor(vtmp[:], h2_sb[:, m * 512:m * 512 + 512], -1.0,
                                                           h2_sb[:, m * 512:m * 512 + 512], ALU.mult, ALU.max))
                            v.wait_ge(_dve, dve.v)
                            dve.inc(v.tensor_tensor(mxh_sb[:], mxh_sb[:], vtmp[:], ALU.max))
                            v.wait_ge(_dve, dve.v)
                        ev_done.append(dve.v)

                for tb in range(4):
                    @block.tensor
                    def _(t, tb=tb):
                        t.wait_ge(_dve, dve.v)
                        pe.inc(t.transpose(ps_d[:, 0:128], mxh_sb[:, tb * 128:(tb + 1) * 128], iden_sb[:]))

                    @block.vector
                    def _(v, tb=tb):
                        v.wait_ge(_pe, pe.v)
                        dve.inc(v.tensor_reduce(colred_sb[:, tb:tb + 1], ps_d[:, 0:128], axis=AX.X, op=ALU.max))

                @block.sync
                def _(sy, hb=hb):
                    sy.wait_ge(_dve, dve.v)
                    for tb in range(4):
                        ld.inc(sy.dma_start(out=gh_v[0, 4 * hb + tb], in_=colred_sb[:, tb:tb + 1]), 16)
                    sy.wait_ge(_ld, ld.v)
                    ld.inc(sy.dma_start(out=rowt_sb[0:1, 0:512], in_=gh_dram[:, hb * 512:hb * 512 + 512]), 16)

                @block.vector
                def _(v):
                    v.wait_ge(_ld, ld.v)
                    v.wait_ge(_pe, pe.v)
                    dve.inc(v.tensor_scalar(rowt2_sb[0:1, 0:512], rowt_sb[0:1, 0:512], 1e-5, None, ALU.max))
                    v.wait_ge(_dve, dve.v)
                    dve.inc(v.tensor_copy(rowt_sb[0:1, 0:512], rowt2_sb[0:1, 0:512]))
                    v.wait_ge(_dve, dve.v)
                    dve.inc(v.reciprocal(rowt2_sb[0:1, 0:512], rowt2_sb[0:1, 0:512]))
                    v.wait_ge(_dve, dve.v)
                    dve.inc(v.tensor_scalar(rowt2_sb[0:1, 0:512], rowt2_sb[0:1, 0:512], 127.0, None, ALU.mult))
                    v.wait_ge(_dve, dve.v)
                    dve.inc(v.tensor_copy(rowb_sb[0:1, 0:512], rowt2_sb[0:1, 0:512]))

                @block.tensor
                def _(t):
                    t.wait_ge(_dve, dve.v)
                    pe.inc(t.matmul(ps_a[:], ones1r_sb[:], rowb_sb[0:1, 0:512], start=True, stop=True))

                @block.vector
                def _(v):
                    v.wait_ge(_pe, pe.v)
                    dve.inc(v.tensor_copy(schb_sb[:], ps_a[:]))
                    v.wait_ge(_dve, dve.v)
                    dve.inc(v.tensor_scalar(rowt2_sb[0:1, 0:512], rowt_sb[0:1, 0:512], 1.0 / 127.0, None, ALU.mult))
                    v.wait_ge(_dve, dve.v)
                    dve.inc(v.tensor_scalar(rowt2_sb[0:1, 0:512], rowt2_sb[0:1, 0:512], scal_sb[0:1, 14:15], None, ALU.mult))
                    v.wait_ge(_dve, dve.v)
                    dve.inc(v.tensor_copy(rowb_sb[0:1, 0:512], rowt2_sb[0:1, 0:512]))

                @block.tensor
                def _(t):
                    t.wait_ge(_dve, dve.v)
                    pe.inc(t.matmul(ps_a[:], ones1r_sb[:], rowb_sb[0:1, 0:512], start=True, stop=True))

                @block.vector
                def _(v):
                    v.wait_ge(_pe, pe.v)
                    dve.inc(v.tensor_copy(deqh_sb[:], ps_a[:]))

                for m in range(IC):
                    @block.vector
                    def _(v, m=m):
                        v.wait_ge(_dve, dve.v)
                        dve.inc(v.tensor_tensor(gtmp[:], h2_sb[:, m * 512:m * 512 + 512], schb_sb[:], ALU.mult))
                        v.wait_ge(_dve, dve.v)
                        dve.inc(v.tensor_copy(qtii[:], gtmp[:]))
                        v.wait_ge(_dve, dve.v)
                        dve.inc(v.tensor_scalar(h2_sb[:, m * 512:m * 512 + 512], qtii[:], -128.0, 127.0, ALU.max, ALU.min))

                h2q_dve = dve.v
                for ob in range(4):
                    obank = [ps_a, ps_b, ps_c, ps_d]
                    for fg in range(4):
                        @block.sync
                        def _(sy, ob=ob, fg=fg):
                            sy.wait_ge(_pe, pe.v)
                            for k in range(DC):
                                fk = fg * DC + k
                                ld.inc(sy.dma_start(out=wTg_sb[:, k * 512:(k + 1) * 512],
                                                    in_=wot_dram[ob * 512:(ob + 1) * 512, fk * 128:(fk + 1) * 128], transpose=True), 16)

                        for oi in range(4):
                            oc = ob * 4 + oi

                            @block.tensor
                            def _(t, ob=ob, fg=fg, oi=oi, oc=oc):
                                t.wait_ge(_ld, ld.v)
                                if ob == 0 and fg == 0 and oi == 0:
                                    t.wait_ge(_dve, h2q_dve)
                                if fg == 0 and oi == 0 and ob > 0:
                                    t.wait_ge(_dve, dve.v)
                                bank = obank[oi]
                                for k in range(DC):
                                    fk = fg * DC + k
                                    pe.inc(t.matmul(bank[:], wTg_sb[:, k * 512 + oi * 128:k * 512 + (oi + 1) * 128],
                                                    h2_sb[:, fk * 512:fk * 512 + 512], start=(fg == 0 and k == 0), stop=(fg == 3 and k == DC - 1)))

                    for oi in range(4):
                        oc = ob * 4 + oi

                        @block.gpsimd
                        def _(g, oc=oc, hb=hb):
                            g.wait_ge(_dve, dve.v)
                            gd.inc(g.dma_start(out=xcin[:], in_=xC_dram[oc][:, hb * 512:hb * 512 + 512]), 16)

                        @block.vector
                        def _(v, oc=oc, oi=oi):
                            v.wait_ge(_pe, pe.v)
                            v.wait_ge(_gd, gd.v)
                            bank = obank[oi]
                            dve.inc(v.tensor_tensor(gtmp[:], bank[:], deqh_sb[:], ALU.mult))
                            v.wait_ge(_dve, dve.v)
                            dve.inc(v.tensor_scalar(gtmp[:], gtmp[:], ob_sb[:, oc:oc + 1], None, ALU.add))
                            v.wait_ge(_dve, dve.v)
                            dve.inc(v.tensor_tensor(xpre2[:, oc * 512:oc * 512 + 512], gtmp[:], xcin[:], ALU.add))

                emit_ln_stats_and_bcast(lambda c: xpre2[:, c * 512:c * 512 + 512])

                for oc in range(DC):
                    @block.vector
                    def _(v, oc=oc):
                        v.wait_ge(_dve, dve.v)
                        v.wait_ge(_ld, ld.v)
                        dve.inc(v.tensor_tensor(evtmp[:], xpre2[:, oc * 512:oc * 512 + 512], lnA_sb[:], ALU.mult))
                        v.wait_ge(_dve, dve.v)
                        dve.inc(v.tensor_tensor(evtmp[:], evtmp[:], lnB_sb[:], ALU.subtract))

                    @block.sync
                    def _(sy, oc=oc, hb=hb):
                        sy.wait_ge(_dve, dve.v)
                        ld.inc(sy.dma_start(out=y_out[oc][:, hb * 512:hb * 512 + 512], in_=evtmp[:]), 16)

        if dbg:
            @block.sync
            def _(sy):
                sy.wait_ge(_dve, dve.v)
                sy.wait_ge(_pe, pe.v)
                for c in range(DC):
                    ld.inc(sy.dma_start(out=dbg_xc[c], in_=xC_dram[c]), 16)

        @block.gpsimd
        def _(g):
            g.wait_ge(_ld, ld.v)
            g.wait_ge(_gd, gd.v)
            g.wait_ge(_act, act.v)
            g.wait_ge(_pe, pe.v)
            g.wait_ge(_dve, dve.v)

    return nc



def prep_B_inputs(inputs, resA, core):
    grp = core // 4
    gcores = range(grp * 4, grp * 4 + 4)
    qkv = {c: np.asarray(resA[c]["qkv_out"]).reshape(3 * D, ROWS) for c in gcores}
    kfull = np.concatenate([qkv[c][D:2 * D, :] for c in gcores], axis=1)
    vfull = np.concatenate([qkv[c][2 * D:3 * D, :] for c in gcores], axis=1).T
    parts = np.stack([np.asarray(resA[c]["partials"])[0] for c in range(NCORE)])
    prep_rep = np.tile(parts.reshape(-1), (128, 1)).astype(np.float32)
    f32 = lambda a: np.ascontiguousarray(np.asarray(a, dtype=np.float32))
    chunk_vec = lambda v, n: f32(np.asarray(v).reshape(n, 128).T)
    return {
        "qfm": f32(qkv[core][0:D, :]),
        "kfm": f32(kfull),
        "vtm": f32(vfull),
        "xBi": f32(np.asarray(resA[core]["xB_out"]).reshape(D, ROWS)),
        "prep": f32(prep_rep),
        "W_out": f32(inputs["L1_attn_W_out"]),
        "b_out": chunk_vec(inputs["L1_attn_b_out"], DC),
        "gate_w": f32(inputs["L1_ffn_gate_w"]),
        "val_w": f32(inputs["L1_ffn_val_w"]),
        "out_w": f32(inputs["L1_ffn_out_w"]),
        "gate_b": chunk_vec(inputs["L1_ffn_gate_b"], IC),
        "val_b": chunk_vec(inputs["L1_ffn_val_b"], IC),
        "out_b": chunk_vec(inputs["L1_ffn_out_b"], DC),
    }


def kernel(**inputs):
    ncA = build_A()
    in_maps_A = [prep_A_inputs(inputs, c) for c in range(NCORE)]
    resA = run_bass_kernel_spmd(ncA, in_maps_A, core_ids=list(range(NCORE))).results

    ncB = build_B()
    in_maps_B = [prep_B_inputs(inputs, resA, c) for c in range(NCORE)]
    resB = run_bass_kernel_spmd(ncB, in_maps_B, core_ids=list(range(NCORE))).results

    y = np.zeros((B, S, D), dtype=np.float32)
    for core in range(NCORE):
        grp, j = divmod(core, 4)
        yfm = np.asarray(resB[core]["y_out"]).reshape(D, ROWS)
        y[grp, j * ROWS:(j + 1) * ROWS, :] = yfm.T
    return y
